# version 13
# baseline (speedup 1.0000x reference)
import numpy as np
import ml_dtypes
from contextlib import ExitStack

import concourse.bass as bass
import concourse.mybir as mybir
from concourse.bass_utils import run_bass_kernel_spmd

F32 = mybir.dt.float32
BF16 = mybir.dt.bfloat16
AF = mybir.ActivationFunctionType
ALU = mybir.AluOpType

NCORES = 8
FUSED = True
T = 2048
TH = T + 4
D = 2048
DI = 4096
NST = 16
DTR = 128
DCONV = 4
HF = 5632
DH = 1024
TB = 64
NBLK = T // TB
KD = D // 128
MI = DI // 128
bf16 = ml_dtypes.bfloat16


class BufT:
    __slots__ = ("w", "r", "sname", "waw", "wlist")

    def __init__(self, sname=None, waw=True):
        self.w = None
        self.r = []
        self.sname = sname
        self.waw = waw
        self.wlist = None if waw else []


class Prog:
    ENG = ("v", "s", "g", "t", "y")

    def __init__(self, nc, es):
        self.nc = nc
        self.es = es
        self.eng = {"v": nc.vector, "s": nc.scalar, "g": nc.gpsimd,
                    "t": nc.tensor, "y": nc.sync}
        self.sem = {e: es.enter_context(nc.semaphore(f"sem_{e}")) for e in self.ENG}
        self.cnt = {e: 0 for e in self.ENG}
        self.waited = {}

    def dsem(self, name):
        if not hasattr(self, "semmap"):
            self.semmap = {}
            self.nslots = 0
        if name in self.semmap:
            return self.semmap[name]
        slot = "D%d" % len(self.semmap)
        if slot not in self.sem:
            self.sem[slot] = self.es.enter_context(self.nc.semaphore("sem_" + slot))
            self.cnt[slot] = 0
        self.semmap[name] = slot
        return slot

    def _dep(self, consumer, deps):
        need = {}
        for d in deps:
            if d is None:
                continue
            pe, v = d
            if pe == consumer and v <= self.cnt[consumer]:
                pass
            if v > 0 and self.waited.get((consumer, pe), 0) < v:
                need[pe] = max(need.get(pe, 0), v)
        for pe, v in need.items():
            self.eng[consumer].wait_ge(self.sem[pe], v)
            self.waited[(consumer, pe)] = v

    def op(self, engine, emit_fn, reads=(), writes=(), extra=(), damt=1, sem=None):
        deps = []
        for b in reads:
            deps.extend(b.wlist if b.wlist is not None else [b.w])
        for b in writes:
            if b.waw:
                deps.append(b.w)
            deps.extend(b.r)
        deps.extend(extra)
        self._dep(engine, deps)
        inst = emit_fn(self.eng[engine])
        s = sem if sem is not None else engine
        if damt is None:
            inst.then_inc(self.sem[s])
            damt = 1
        else:
            inst.then_inc(self.sem[s], damt)
        self.cnt[s] += damt
        tok = (s, self.cnt[s])
        for b in reads:
            b.r.append(tok)
        for b in writes:
            b.w = tok
            if b.wlist is not None:
                b.wlist.append(tok)
            else:
                b.r = []
        return tok

    def dma(self, queue, out, in_, reads=(), writes=(), extra=()):
        assert writes, "dma needs a written buffer for its semaphore"
        if writes[0].wlist is not None and reads:
            s = self.dsem("st" + reads[0].sname)
        else:
            s = self.dsem(writes[0].sname)
        return self.op(queue, lambda e: e.dma_start(out=out, in_=in_),
                       reads, writes, extra, damt=16, sem=s)

    def barrier(self):
        for e in self.ENG:
            self._dep(e, [(o, self.cnt[o]) for o in self.sem if self.cnt[o] > 0])
        if hasattr(self, "semmap"):
            self.semmap.clear()


def sbuf(nc, es, name, shape, dt, sname=None):
    h = es.enter_context(nc.sbuf_tensor(name, shape, dt))
    return h, BufT(sname=sname or ("L" + name))


def psum(nc, es, name, shape, dt=F32, sname=None):
    h = es.enter_context(nc.psum_tensor(name, shape, dt))
    return h, BufT(sname=sname or ("L" + name))


def emit_l0pre(nc, es, p, Dio, pre="a_"):
    x_f, _ = Dio["x_f"]
    hmaskc, _ = Dio["hmaskc"]
    w_inT, _ = Dio["w_inT"]
    w_xT, _ = Dio["w_xT"]
    w_dtT, _ = Dio["w_dtT"]
    conv_w, _ = Dio["conv_w"]
    conv_b, _ = Dio["conv_b"]
    b_dt, _ = Dio["b_dt"]
    o_xn, b_o_xn = Dio["o_xn"]
    o_delta, b_o_delta = Dio["o_delta"]
    o_du, b_o_du = Dio["o_du"]
    o_u, b_o_u = Dio["o_u"]
    o_sz, b_o_sz = Dio["o_sz"]
    o_bc, b_o_bc = Dio["o_bc"]
    o_dsum, b_o_dsum = Dio["o_dsum"]
    ub_hbm, b_ub_hbm = Dio["ub_hbm"]
    SB = lambda n, sh, dt: sbuf(nc, es, pre + n, sh, dt, sname="L" + n)
    PS = lambda n, sh: psum(nc, es, pre + n, sh, sname="L" + n)

    xn, b_xn = SB("xn", [128, KD * TH], BF16)

    cpool, b_cp = SB("cpool", [128, 8], F32)
    onesb, b_on = SB("onesb", [1, 128], BF16)
    e_a, b_ea = SB("e_a", [128, 33], BF16)
    e_b, b_eb = SB("e_b", [128, 33], BF16)
    hmt, b_hmt = SB("hmt", [128, 1], F32)
    p.op("v", lambda v: v.memset(cpool[:, 0:1], -1.0), writes=[b_cp])
    p.op("v", lambda v: v.memset(cpool[:, 1:2], 1.0), writes=[b_cp])
    p.op("v", lambda v: v.memset(cpool[:, 2:3], 1e-5), writes=[b_cp])
    p.op("v", lambda v: v.memset(cpool[:, 3:4], -0.5), writes=[b_cp])
    p.op("v", lambda v: v.memset(e_a[:, :], 0.0), writes=[b_ea])
    p.op("v", lambda v: v.memset(e_a[:, 0:1], 1.0), writes=[b_ea])
    p.op("v", lambda v: v.memset(e_b[:, :], 0.0), writes=[b_eb])
    p.op("v", lambda v: v.memset(e_b[:, 32:33], 1.0), writes=[b_eb])
    p.op("v", lambda v: v.memset(onesb[:], 1.0), writes=[b_on])
    p.dma("y", hmt[:], hmaskc, writes=[b_hmt])
    c_m1 = cpool[:, 0:1]
    c_eps = cpool[0:1, 2:3]
    c_mhalf = cpool[0:1, 3:4]

    with ExitStack() as ph:
        SBp = lambda n, sh, dt: sbuf(nc, ph, pre + n, sh, dt, sname="L" + n)
        PSp = lambda n, sh: psum(nc, ph, pre + n, sh, sname="L" + n)
        xf32 = [SBp("xf32_%d" % i, [128, TH], F32) for i in range(2)]
        sqb = [SBp("sqb_%d" % i, [128, TH], BF16) for i in range(2)]
        pstat = [PSp("pstat%d" % n, [33, 512 if n < 4 else 4]) for n in range(5)]
        m_r = SBp("m_r", [1, TH], F32)
        e_r = SBp("e_r", [1, TH], F32)
        q_r = SBp("q_r", [1, TH], F32)
        r_r = SBp("r_r", [1, TH], F32)
        mrowb = SBp("mrowb", [1, TH], BF16)
        rrowb = SBp("rrowb", [1, TH], BF16)
        pbc = [PSp("pbc%d" % i, [128, 512]) for i in range(2)]
        mbc = SBp("mbc", [128, TH], BF16)
        rbc = SBp("rbc", [128, TH], BF16)
        tscr = SBp("tscr", [128, TH], BF16)

        for k in range(KD):
            b = k % 2
            xb, bxb = xf32[b]
            sq, bsq = sqb[b]
            p.dma("y", xb[:], x_f[k * 128:(k + 1) * 128, :], writes=[bxb])
            p.op("s", lambda s, k=k, xb=xb: s.copy(
                out=xn[:, k * TH:(k + 1) * TH], in_=xb[:]), reads=[bxb], writes=[b_xn])
            p.op("s", lambda s, xb=xb, sq=sq: s.activation(
                out=sq[:], in_=xb[:], func=AF.Square), reads=[bxb], writes=[bsq])
            for n in range(5):
                ncs = 512 if n < 4 else 4
                noff = n * 512
                ps, bps = pstat[n]
                p.op("t", lambda t, k=k, ps=ps, ncs=ncs, noff=noff: t.matmul(
                    ps[0:33, 0:ncs], e_a[:, 0:33], xn[:, k * TH + noff:k * TH + noff + ncs],
                    start=(k == 0), stop=False),
                    reads=[b_xn, b_ea], writes=[bps])
                p.op("t", lambda t, k=k, ps=ps, sq=sq, ncs=ncs, noff=noff: t.matmul(
                    ps[0:33, 0:ncs], e_b[:, 0:33], sq[:, noff:noff + ncs],
                    start=False, stop=(k == KD - 1)),
                    reads=[bsq, b_eb], writes=[bps])
        mr, bmr = m_r
        er, ber = e_r
        qr, bqr = q_r
        rr, brr = r_r
        for n in range(5):
            ncs = 512 if n < 4 else 4
            ps, bps = pstat[n]
            p.op("s", lambda s, ps=ps, n=n, ncs=ncs: s.copy(
                out=mr[0:1, n * 512:n * 512 + ncs], in_=ps[0:1, 0:ncs]),
                reads=[bps], writes=[bmr])
            p.op("s", lambda s, ps=ps, n=n, ncs=ncs: s.copy(
                out=er[0:1, n * 512:n * 512 + ncs], in_=ps[32:33, 0:ncs]),
                reads=[bps], writes=[ber])
        p.op("v", lambda v: v.tensor_scalar_mul(out=mr[:], in0=mr[:],
                                                scalar1=1.0 / 2048.0),
             reads=[bmr], writes=[bmr])
        p.op("v", lambda v: v.tensor_scalar_mul(out=er[:], in0=er[:],
                                                scalar1=1.0 / 2048.0),
             reads=[ber], writes=[ber])
        p.op("v", lambda v: v.tensor_tensor(out=qr[:], in0=mr[:], in1=mr[:],
                                            op=ALU.mult), reads=[bmr], writes=[bqr])
        p.op("v", lambda v: v.tensor_tensor(out=er[:], in0=er[:], in1=qr[:],
                                            op=ALU.subtract), reads=[bqr, ber], writes=[ber])
        p.op("s", lambda s: s.activation(out=qr[:], in_=er[:], func=AF.Ln,
                                         bias=c_eps), reads=[ber, b_cp], writes=[bqr])
        p.op("s", lambda s: s.activation(out=rr[:], in_=qr[:], func=AF.Exp,
                                         scale=c_mhalf), reads=[bqr, b_cp], writes=[brr])
        mb, bmb = mrowb
        rb2, brb2 = rrowb
        p.op("s", lambda s: s.copy(out=mb[:], in_=mr[:]), reads=[bmr], writes=[bmb])
        p.op("s", lambda s: s.copy(out=rb2[:], in_=rr[:]), reads=[brr], writes=[brb2])
        mc, bmc = mbc
        rc, brc = rbc
        for n in range(5):
            ncs = 512 if n < 4 else 4
            noff = n * 512
            ps, bps = pbc[n % 2]
            p.op("t", lambda t, ps=ps, noff=noff, ncs=ncs: t.matmul(
                ps[:, 0:ncs], onesb[:], mb[0:1, noff:noff + ncs], start=True, stop=True),
                reads=[b_on, bmb], writes=[bps])
            p.op("s", lambda s, ps=ps, noff=noff, ncs=ncs: s.copy(
                out=mc[:, noff:noff + ncs], in_=ps[:, 0:ncs]), reads=[bps], writes=[bmc])
            p.op("t", lambda t, ps=ps, noff=noff, ncs=ncs: t.matmul(
                ps[:, 0:ncs], onesb[:], rb2[0:1, noff:noff + ncs], start=True, stop=True),
                reads=[b_on, brb2], writes=[bps])
            p.op("s", lambda s, ps=ps, noff=noff, ncs=ncs: s.copy(
                out=rc[:, noff:noff + ncs], in_=ps[:, 0:ncs]), reads=[bps], writes=[brc])
        ts_, bts = tscr
        for k in range(KD):
            p.op("v", lambda v, k=k: v.tensor_tensor(
                out=ts_[:], in0=xn[:, k * TH:(k + 1) * TH], in1=mc[:], op=ALU.subtract),
                reads=[b_xn, bmc], writes=[bts])
            p.op("v", lambda v, k=k: v.tensor_tensor(
                out=xn[:, k * TH:(k + 1) * TH], in0=ts_[:], in1=rc[:], op=ALU.mult),
                reads=[bts, brc], writes=[b_xn])
            p.dma("y", o_xn[k * 128:(k + 1) * 128, :], xn[:, k * TH:k * TH + T],
                  reads=[b_xn], writes=[b_o_xn])
        p.barrier()

    with ExitStack() as ph:
        SBp = lambda n, sh, dt: sbuf(nc, ph, pre + n, sh, dt, sname="L" + n)
        PSp = lambda n, sh: psum(nc, ph, pre + n, sh, sname="L" + n)
        wv = [SBp("wv%d" % i, [128, KD * 128], BF16) for i in range(2)]
        pxz = [PSp("pxz%d" % n, [128, 512]) for n in range(4)]
        pxzh = PSp("pxzh", [128, 4])
        up = [SBp("up%d" % i, [128, TH], F32) for i in range(2)]
        csA = SBp("csA", [128, T], F32)
        csB = SBp("csB", [128, T], F32)
        uf = SBp("uf", [128, T], F32)
        ubf = SBp("ubf", [128, T], BF16)
        szf = SBp("szf", [128, T], F32)
        cwt = SBp("cwt", [128, DCONV], F32)
        cbt = SBp("cbt", [128, 1], F32)

        for m in range(2 * MI):
            wb, bwb = wv[m % 2]
            p.dma("g", wb[:].rearrange("p (k c) -> p k c", k=KD),
                  w_inT[:, m * 128:(m + 1) * 128].rearrange("(k p) c -> p k c", p=128),
                  writes=[bwb])
            is_x = m < MI
            if is_x:
                cw, bcw = cwt
                cb, bcb = cbt
                p.dma("y", cw[:], conv_w[m * 128:(m + 1) * 128, :], writes=[bcw])
                p.dma("y", cb[:], conv_b[m * 128:(m + 1) * 128, :], writes=[bcb])
                ub_, bub = up[m % 2]
            for k in range(KD):
                for n in range(5):
                    if n < 4:
                        ps, bps = pxz[n]
                        ncs, roff = 512, n * 512
                    else:
                        ps, bps = pxzh
                        ncs, roff = 4, T
                    p.op("t", lambda t, ps=ps, k=k, ncs=ncs, roff=roff: t.matmul(
                        ps[:, 0:ncs], wb[:, k * 128:(k + 1) * 128],
                        xn[:, k * TH + roff:k * TH + roff + ncs],
                        start=(k == 0), stop=(k == KD - 1)),
                        reads=[bwb, b_xn], writes=[bps])
            for n in range(5):
                if n < 4:
                    ps, bps = pxz[n]
                    ncs, roff = 512, n * 512
                else:
                    ps, bps = pxzh
                    ncs, roff = 4, T
                if is_x:
                    if n < 4:
                        p.op("s", lambda s, ps=ps, n=n: s.copy(
                            out=ub_[:, 4 + n * 512:4 + (n + 1) * 512], in_=ps[:, 0:512]),
                            reads=[bps], writes=[bub])
                    else:
                        p.op("s", lambda s, ps=ps: s.activation(
                            out=ub_[:, 0:4], in_=ps[:, 0:4], func=AF.Identity,
                            scale=hmt[:]), reads=[bps, b_hmt], writes=[bub])
                else:
                    if n < 4:
                        p.op("s", lambda s, ps=ps, n=n: s.activation(
                            out=szf[0][:, n * 512:(n + 1) * 512], in_=ps[:, 0:512],
                            func=AF.Silu), reads=[bps], writes=[szf[1]])
            if is_x:
                cA, bcA = csA
                cB, bcB = csB
                p.op("v", lambda v: v.tensor_scalar(
                    out=cA[:], in0=ub_[:, 1:1 + T], scalar1=cw[:, 0:1], scalar2=None,
                    op0=ALU.mult), reads=[bub, bcw], writes=[bcA])
                p.op("v", lambda v: v.scalar_tensor_tensor(
                    out=cB[:], in0=ub_[:, 2:2 + T], scalar=cw[:, 1:2], in1=cA[:],
                    op0=ALU.mult, op1=ALU.add), reads=[bub, bcw, bcA], writes=[bcB])
                p.op("v", lambda v: v.scalar_tensor_tensor(
                    out=cA[:], in0=ub_[:, 3:3 + T], scalar=cw[:, 2:3], in1=cB[:],
                    op0=ALU.mult, op1=ALU.add), reads=[bub, bcw, bcB], writes=[bcA])
                p.op("v", lambda v: v.scalar_tensor_tensor(
                    out=cB[:], in0=ub_[:, 4:4 + T], scalar=cw[:, 3:4], in1=cA[:],
                    op0=ALU.mult, op1=ALU.add), reads=[bub, bcw, bcA], writes=[bcB])
                u_, bu_ = uf
                p.op("s", lambda s: s.activation(out=u_[:], in_=cB[:], func=AF.Silu,
                                                 bias=cb[:]),
                     reads=[bcB, bcb], writes=[bu_])
                ub2, bub2 = ubf
                p.op("s", lambda s: s.copy(out=ub2[:], in_=u_[:]), reads=[bu_], writes=[bub2])
                p.dma("y", o_u[m * 128:(m + 1) * 128, :], u_[:], reads=[bu_], writes=[b_o_u])
                p.dma("s", ub_hbm[m * 128:(m + 1) * 128, :], ub2[:], reads=[bub2],
                      writes=[b_ub_hbm])
            else:
                mi = m - MI
                p.dma("y", o_sz[mi * 128:(mi + 1) * 128, :], szf[0][:], reads=[szf[1]],
                      writes=[b_o_sz])
        p.barrier()

    dtlb, b_dtlb = SB("dtlb", [128, T], BF16)
    with ExitStack() as ph:
        SBp = lambda n, sh, dt: sbuf(nc, ph, pre + n, sh, dt, sname="L" + n)
        PSp = lambda n, sh: psum(nc, ph, pre + n, sh, sname="L" + n)
        wx = [SBp("wx%d" % i, [128, 160], BF16) for i in range(2)]
        ubk = [SBp("ubk%d" % i, [128, T], BF16) for i in range(2)]
        pxa = [PSp("pxa%d" % n, [128, 512]) for n in range(4)]
        pxb = [PSp("pxb%d" % n, [32, 512]) for n in range(4)]
        bcs = SBp("bcs", [32, T], BF16)

        for k in range(MI):
            wbx, bwx = wx[k % 2]
            ub_, bub = ubk[k % 2]
            p.dma("g", wbx[:], w_xT[k * 128:(k + 1) * 128, :], writes=[bwx])
            p.dma("y", ub_[:], ub_hbm[k * 128:(k + 1) * 128, :], reads=[b_ub_hbm],
                  writes=[bub])
            for n in range(4):
                pa, bpa = pxa[n]
                pb, bpb = pxb[n]
                p.op("t", lambda t, pa=pa, k=k, n=n: t.matmul(
                    pa[:, :], wbx[:, 0:128], ub_[:, n * 512:(n + 1) * 512],
                    start=(k == 0), stop=(k == MI - 1)), reads=[bwx, bub], writes=[bpa])
                p.op("t", lambda t, pb=pb, k=k, n=n: t.matmul(
                    pb[:, :], wbx[:, 128:160], ub_[:, n * 512:(n + 1) * 512],
                    start=(k == 0), stop=(k == MI - 1)), reads=[bwx, bub], writes=[bpb])
        bc_, bbc = bcs
        for n in range(4):
            pa, bpa = pxa[n]
            pb, bpb = pxb[n]
            p.op("s", lambda s, pa=pa, n=n: s.copy(
                out=dtlb[:, n * 512:(n + 1) * 512], in_=pa[:, :]),
                reads=[bpa], writes=[b_dtlb])
            p.op("s", lambda s, pb=pb, n=n: s.copy(
                out=bc_[0:32, n * 512:(n + 1) * 512], in_=pb[:, :]),
                reads=[bpb], writes=[bbc])
        p.dma("y", o_bc[:, :], bc_[0:32, :], reads=[bbc], writes=[b_o_bc])
        p.barrier()

    with ExitStack() as ph:
        SBp = lambda n, sh, dt: sbuf(nc, ph, pre + n, sh, dt, sname="L" + n)
        PSp = lambda n, sh: psum(nc, ph, pre + n, sh, sname="L" + n)
        wdt = SBp("wdt", [128, DI], BF16)
        pdt = [PSp("pdt%d" % n, [128, 512]) for n in range(4)]
        sA = SBp("sA", [128, T], F32)
        sB = SBp("sB", [128, T], F32)
        delta = SBp("delta", [128, T], F32)
        ufr = SBp("ufr", [128, T], F32)
        dub = SBp("dub", [128, T], F32)
        bdtc = SBp("bdtc", [128, 1], F32)
        dsum = SBp("dsum", [128, MI], F32)

        wd, bwd = wdt
        p.dma("g", wd[:], w_dtT[:, :], writes=[bwd])
        dall, bdall = dsum
        for m in range(MI):
            bd, bbd = bdtc
            p.dma("y", bd[:], b_dt[m * 128:(m + 1) * 128, :], writes=[bbd])
            de, bde = delta
            a_, ba_ = sA
            b2, bb2 = sB
            for n in range(4):
                ps, bps = pdt[n]
                p.op("t", lambda t, ps=ps, m=m, n=n: t.matmul(
                    ps[:, :], wd[:, m * 128:(m + 1) * 128],
                    dtlb[:, n * 512:(n + 1) * 512], start=True, stop=True),
                    reads=[bwd, b_dtlb], writes=[bps])
                sl = slice(n * 512, (n + 1) * 512)
                p.op("s", lambda s, ps=ps, sl=sl: s.activation(
                    out=a_[:, sl], in_=ps[:, :], func=AF.Abs, bias=bd[:]),
                    reads=[bps, bbd], writes=[ba_])
                p.op("s", lambda s, ps=ps, sl=sl: s.activation(
                    out=de[:, sl], in_=ps[:, :], func=AF.Relu, bias=bd[:]),
                    reads=[bps, bbd], writes=[bde])
            p.op("s", lambda s: s.activation(out=b2[:], in_=a_[:], func=AF.Exp,
                                             scale=cpool[:, 0:1]),
                 reads=[ba_, b_cp], writes=[bb2])
            p.op("s", lambda s: s.activation(out=a_[:], in_=b2[:], func=AF.Ln,
                                             bias=cpool[:, 1:2]),
                 reads=[bb2, b_cp], writes=[ba_])
            p.op("v", lambda v: v.tensor_tensor(out=de[:], in0=de[:], in1=a_[:],
                                                op=ALU.add),
                 reads=[ba_, bde], writes=[bde])
            u_, bu_ = ufr
            p.dma("s", u_[:], o_u[m * 128:(m + 1) * 128, :], reads=[b_o_u], writes=[bu_])
            du_, bdu = dub
            p.op("v", lambda v: v.tensor_tensor(out=du_[:], in0=de[:], in1=u_[:],
                                                op=ALU.mult),
                 reads=[bde, bu_], writes=[bdu])
            p.op("v", lambda v, m=m: v.tensor_reduce(
                out=dall[:, m:m + 1], in_=de[:], axis=mybir.AxisListType.X, op=ALU.add),
                reads=[bde], writes=[bdall])
            p.dma("y", o_delta[m * 128:(m + 1) * 128, :], de[:], reads=[bde],
                  writes=[b_o_delta])
            p.dma("g", o_du[m * 128:(m + 1) * 128, :], du_[:], reads=[bdu],
                  writes=[b_o_du])
        p.dma("y", o_dsum[:, :], dall[:], reads=[bdall], writes=[b_o_dsum])

    p.barrier()




def build_l0pre():
    nc = bass.Bass("TRN2", target_bir_lowering=False)
    es = ExitStack()
    dram = lambda n, sh, dt, k="ExternalInput": (nc.dram_tensor(n, sh, dt, kind=k).ap(),
                                                 BufT(sname="S" + n, waw=False))

    x_f, _ = dram("x_f", [D, TH], F32)
    hmaskc, _ = dram("hmaskc", [128, 1], F32)
    w_inT, _ = dram("w_inT", [D, 2 * DI], F32)
    w_xT, _ = dram("w_xT", [DI, 160], F32)
    w_dtT, _ = dram("w_dtT", [DTR, DI], F32)
    conv_w, _ = dram("conv_w", [DI, DCONV], F32)
    conv_b, _ = dram("conv_b", [DI, 1], F32)
    b_dt, _ = dram("b_dt", [DI, 1], F32)
    o_xn, b_o_xn = dram("o_xn", [D, T], BF16, "ExternalOutput")
    o_delta, b_o_delta = dram("o_delta", [DI, T], F32, "ExternalOutput")
    o_du, b_o_du = dram("o_du", [DI, T], BF16, "ExternalOutput")
    o_u, b_o_u = dram("o_u", [DI, T], F32, "ExternalOutput")
    o_sz, b_o_sz = dram("o_sz", [DI, T], F32, "ExternalOutput")
    o_bc, b_o_bc = dram("o_bc", [32, T], BF16, "ExternalOutput")
    o_dsum, b_o_dsum = dram("o_dsum", [128, MI], F32, "ExternalOutput")
    ub_hbm, b_ub_hbm = dram("ub_hbm", [DI, T], BF16, "Internal")
    Dio = {
        "x_f": (x_f, _),
        "hmaskc": (hmaskc, _),
        "w_inT": (w_inT, _),
        "w_xT": (w_xT, _),
        "w_dtT": (w_dtT, _),
        "conv_w": (conv_w, _),
        "conv_b": (conv_b, _),
        "b_dt": (b_dt, _),
        "o_xn": (o_xn, b_o_xn),
        "o_delta": (o_delta, b_o_delta),
        "o_du": (o_du, b_o_du),
        "o_u": (o_u, b_o_u),
        "o_sz": (o_sz, b_o_sz),
        "o_bc": (o_bc, b_o_bc),
        "o_dsum": (o_dsum, b_o_dsum),
        "ub_hbm": (ub_hbm, b_ub_hbm),
    }
    p = Prog(nc, es)
    emit_l0pre(nc, es, p, Dio)
    es.close()
    return nc


def emit_l0scan1(nc, es, p, Dio, pre="b1_"):
    d_delta, b_delta = Dio["delta"]
    d_du, b_du = Dio["du"]
    d_bc, b_bc = Dio["bc"]
    d_alog, b_alog = Dio["alog"]
    d_idb, b_idb = Dio["identb"]
    o_y1, b_y1 = Dio["o_y1"]
    o_sc, b_sc = Dio["o_sc"]

    SB = lambda n, sh, dt: sbuf(nc, es, pre + n, sh, dt, sname="L" + n)
    G = 2

    aal, b_aal = SB("aal", [128, MI * NST], F32)
    p.dma("y", aal[:].rearrange("p (m n) -> p m n", n=NST),
          d_alog.rearrange("(m p) n -> p m n", p=128), reads=[b_alog], writes=[b_aal])
    p.op("s", lambda s: s.activation(out=aal[:], in_=aal[:], func=AF.Exp),
         reads=[b_aal], writes=[b_aal])
    p.op("v", lambda v: v.tensor_scalar_mul(out=aal[:], in0=aal[:], scalar1=-1.0),
         reads=[b_aal], writes=[b_aal])
    idb, b_idbs = SB("idb", [128, 128], BF16)
    p.dma("y", idb[:], d_idb, reads=[b_idb], writes=[b_idbs])

    deltat = [SB("deltat%d" % i, [128, T], F32) for i in range(G)]
    dut = [SB("dut%d" % i, [128, T], BF16) for i in range(G)]
    yac = [SB("yac%d" % i, [128, T], F32) for i in range(G)]
    bbn = [SB("bbn%d" % i, [128, T], BF16) for i in range(2)]
    cbn = [SB("cbn%d" % i, [128, T], BF16) for i in range(2)]
    da_s = [SB("da_s%d" % i, [128, T], F32) for i in range(2)]
    h_s = [SB("h_s%d" % i, [128, T], BF16) for i in range(2)]
    xb_s = [SB("xb_s%d" % i, [128, T], BF16) for i in range(2)]
    ym_s = [SB("ym_s%d" % i, [128, T], BF16) for i in range(2)]
    scol, b_scol = SB("scol", [128, MI * NST], F32)
    with ExitStack() as ph:
        py = [[psum(nc, ph, pre + "py%d_%d" % (i, n), [128, 512], sname="Lpy%d_%d" % (i, n))
               for n in range(4)] for i in range(G)]
        for mg in range(MI // G):
            for i in range(G):
                m = mg * G + i
                dt_, bdt_ = deltat[i]
                du_, bdu_ = dut[i]
                p.dma("y", dt_[:], d_delta[m * 128:(m + 1) * 128, :], reads=[b_delta],
                      writes=[bdt_])
                p.dma("s", du_[:], d_du[m * 128:(m + 1) * 128, :], reads=[b_du],
                      writes=[bdu_])
            for n in range(NST):
                bb, bbb = bbn[n % 2]
                cb, bcb = cbn[n % 2]
                p.dma("g", bb[:], d_bc[n:n + 1, :].broadcast_to([128, T]), reads=[b_bc],
                      writes=[bbb])
                p.dma("g", cb[:], d_bc[16 + n:17 + n, :].broadcast_to([128, T]),
                      reads=[b_bc], writes=[bcb])
                for i in range(G):
                    m = mg * G + i
                    dt_, bdt_ = deltat[i]
                    du_, bdu_ = dut[i]
                    da, bda = da_s[i % 2]
                    hh, bhh = h_s[i % 2]
                    xb_, bxb_ = xb_s[i % 2]
                    ym, bym = ym_s[i % 2]
                    p.op("s", lambda s, dt_=dt_, da=da, m=m, n=n: s.activation(
                        out=da[:], in_=dt_[:], func=AF.Exp,
                        scale=aal[:, m * NST + n:m * NST + n + 1]),
                        reads=[bdt_, b_aal], writes=[bda])
                    p.op("v", lambda v, du_=du_, xb_=xb_: v.tensor_tensor(
                        out=xb_[:], in0=du_[:], in1=bb[:], op=ALU.mult),
                        reads=[bdu_, bbb], writes=[bxb_])
                    p.op("v", lambda v, da=da, xb_=xb_, hh=hh: v.tensor_tensor_scan(
                        out=hh[:], data0=da[:], data1=xb_[:], initial=0.0,
                        op0=ALU.mult, op1=ALU.add),
                        reads=[bda, bxb_], writes=[bhh])
                    p.op("v", lambda v, hh=hh, m=m, n=n: v.tensor_copy(
                        out=scol[:, m * NST + n:m * NST + n + 1], in_=hh[:, T - 1:T]),
                        reads=[bhh], writes=[b_scol])
                    p.op("v", lambda v, hh=hh, ym=ym: v.tensor_tensor(
                        out=ym[:], in0=hh[:], in1=cb[:], op=ALU.mult),
                        reads=[bhh, bcb], writes=[bym])
                    for nc4 in range(4):
                        ps, bps = py[i][nc4]
                        p.op("t", lambda t, ps=ps, ym=ym, nc4=nc4, n=n: t.matmul(
                            ps[:, :], idb[:], ym[:, nc4 * 512:(nc4 + 1) * 512],
                            start=(n == 0), stop=(n == NST - 1)),
                            reads=[b_idbs, bym], writes=[bps])
            for i in range(G):
                m = mg * G + i
                ya, bya = yac[i]
                for nc4 in range(4):
                    ps, bps = py[i][nc4]
                    p.op("s", lambda s, ps=ps, ya=ya, nc4=nc4: s.copy(
                        out=ya[:, nc4 * 512:(nc4 + 1) * 512], in_=ps[:, :]),
                        reads=[bps], writes=[bya])
                p.dma("y", o_y1[m * 128:(m + 1) * 128, :], ya[:], reads=[bya],
                      writes=[b_y1])
    p.dma("y", o_sc[:, :], scol[:], reads=[b_scol], writes=[b_sc])
    p.barrier()


def build_l0scan1():
    nc = bass.Bass("TRN2", target_bir_lowering=False)
    es = ExitStack()
    dram = lambda n, sh, dt, k="ExternalInput": (nc.dram_tensor(n, sh, dt, kind=k).ap(),
                                                 BufT(sname="S" + n, waw=False))
    d_delta, b_delta = dram("delta", [DI, T], F32)
    d_du, b_du = dram("du", [DI, T], BF16)
    d_bc, b_bc = dram("bc", [32, T], BF16)
    d_alog, b_alog = dram("alog", [DI, NST], F32)
    d_idb, b_idb = dram("identb", [128, 128], BF16)
    o_y1, b_y1 = dram("o_y1", [DI, T], F32, "ExternalOutput")
    o_sc, b_sc = dram("o_sc", [128, MI * NST], F32, "ExternalOutput")
    Dio = {
        "delta": (d_delta, b_delta),
        "du": (d_du, b_du),
        "bc": (d_bc, b_bc),
        "alog": (d_alog, b_alog),
        "identb": (d_idb, b_idb),
        "o_y1": (o_y1, b_y1),
        "o_sc": (o_sc, b_sc),
    }
    p = Prog(nc, es)
    emit_l0scan1(nc, es, p, Dio)
    es.close()
    return nc


def emit_l0scan2(nc, es, p, Dio, pre="b2_"):
    d_delta, b_delta = Dio["delta"]
    d_bc, b_bc = Dio["bc"]
    d_alog, b_alog = Dio["alog"]
    d_y1, b_y1 = Dio["y1"]
    d_u, b_u = Dio["u"]
    d_sz, b_sz = Dio["sz"]
    d_dskip, b_dskip = Dio["dskip"]
    d_m2t, b_m2t = Dio["m2T"]
    d_woT, b_woT = Dio["w_outT"]
    o_at0, b_at0 = Dio["o_attn0"]
    ygb_h, b_ygbh = Dio["ygb_hbm"]
    fused = "cx1g" in Dio
    if fused:
        d_cx1g, b_cx1g = Dio["cx1g"]
        d_mv, b_mv = Dio["mv"]
    else:
        d_preds, b_preds = Dio["predS"]
        d_predd, b_predd = Dio["predDT"]
    SB = lambda n, sh, dt: sbuf(nc, es, pre + n, sh, dt, sname="L" + n)
    G = 4

    aal, b_aal = SB("aal", [128, MI * NST], F32)
    p.dma("y", aal[:].rearrange("p (m n) -> p m n", n=NST),
          d_alog.rearrange("(m p) n -> p m n", p=128), writes=[b_aal])
    p.op("s", lambda s: s.activation(out=aal[:], in_=aal[:], func=AF.Exp),
         reads=[b_aal], writes=[b_aal])
    p.op("v", lambda v: v.tensor_scalar_mul(out=aal[:], in0=aal[:], scalar1=-1.0),
         reads=[b_aal], writes=[b_aal])
    dsk, b_dsk = SB("dsk", [128, MI], F32)
    p.dma("y", dsk[:], d_dskip, writes=[b_dsk])

    hbuf, b_hbuf = SB("hbuf", [128, MI * NST], F32)
    m2t, b_m2 = SB("m2t", [8, 8], F32)
    p.dma("y", m2t[:], d_m2t, writes=[b_m2])
    with ExitStack() as ph:
        pdt_, b_pdt = psum(nc, ph, pre + "pE", [128, 8])
        dtile, b_dtile = sbuf(nc, ph, pre + "dtile", [8, 128], F32)
        et, b_et = sbuf(nc, ph, pre + "et", [128, 8], F32)
        stile, b_stile = sbuf(nc, ph, pre + "stile", [128, 8 * NST], F32)
        wj, b_wj = sbuf(nc, ph, pre + "wj", [128, NST], F32)
        cj, b_cj = sbuf(nc, ph, pre + "cj", [128, NST], F32)
        if fused:
            allD, b_allD = sbuf(nc, ph, pre + "allD", [8, MI * 128], F32)
            p.dma("y", allD[:].rearrange("j (p mm) -> j p mm", mm=MI),
                  d_cx1g.rearrange("(j p) c -> j p c", p=128)[:, :, 512:512 + MI],
                  reads=[b_cx1g], writes=[b_allD])
            mvb, b_mvb = sbuf(nc, ph, pre + "mvb", [128, 8], F32)
            p.dma("y", mvb[:], d_mv.broadcast_to([128, 8]), reads=[b_mv], writes=[b_mvb])
        for m in range(MI):
            if fused:
                p.op("t", lambda t, m=m: t.matmul(
                    pdt_[:, :],
                    allD[:].rearrange("j (p mm) -> j p mm", mm=MI)[:, :, m],
                    m2t[:], start=True, stop=True),
                    reads=[b_allD, b_m2], writes=[b_pdt])
                p.dma("s", stile[:].rearrange("p (j n) -> p j n", n=NST),
                      d_cx1g.rearrange("(j p) c -> p j c", p=128)[:, :, m * NST:(m + 1) * NST],
                      reads=[b_cx1g], writes=[b_stile])
            else:
                p.dma("y", dtile[:], d_predd[:, m * 128:(m + 1) * 128], reads=[b_predd],
                      writes=[b_dtile])
                p.dma("s", stile[:].rearrange("p (j n) -> p j n", n=NST),
                      d_preds.rearrange("(j p) c -> p j c", p=128)[:, :, m * NST:(m + 1) * NST],
                      reads=[b_preds], writes=[b_stile])
                p.op("t", lambda t: t.matmul(pdt_[:, :], dtile[:], m2t[:],
                                             start=True, stop=True),
                     reads=[b_dtile, b_m2], writes=[b_pdt])
            p.op("s", lambda s: s.copy(out=et[:], in_=pdt_[:, :]), reads=[b_pdt],
                 writes=[b_et])
            for j in range(8):
                p.op("s", lambda s, m=m, j=j: s.activation(
                    out=wj[:], in_=aal[:, m * NST:(m + 1) * NST], func=AF.Exp,
                    scale=et[:, j:j + 1]), reads=[b_aal, b_et], writes=[b_wj])
                if fused:
                    p.op("v", lambda v, j=j: v.tensor_scalar(
                        out=wj[:], in0=wj[:], scalar1=mvb[:, j:j + 1], scalar2=None,
                        op0=ALU.mult), reads=[b_mvb, b_wj], writes=[b_wj])
                p.op("v", lambda v, j=j: v.tensor_tensor(
                    out=cj[:], in0=wj[:], in1=stile[:, j * NST:(j + 1) * NST],
                    op=ALU.mult), reads=[b_wj, b_stile], writes=[b_cj])
                if j == 0:
                    p.op("v", lambda v, m=m: v.tensor_copy(
                        out=hbuf[:, m * NST:(m + 1) * NST], in_=cj[:]),
                        reads=[b_cj], writes=[b_hbuf])
                else:
                    p.op("v", lambda v, m=m: v.tensor_tensor(
                        out=hbuf[:, m * NST:(m + 1) * NST],
                        in0=hbuf[:, m * NST:(m + 1) * NST], in1=cj[:], op=ALU.add),
                        reads=[b_cj], writes=[b_hbuf])
        p.barrier()

    idb, b_idbs = sbuf(nc, es, pre + "idb", [128, 128], BF16, sname="Lidb")
    d_idb, b_idb = Dio["identb"]
    p.dma("y", idb[:], d_idb, reads=[b_idb], writes=[b_idbs])
    G = 2
    with ExitStack() as ph:
        SBp = lambda n, sh, dt: sbuf(nc, ph, pre + n, sh, dt, sname="L" + n)
        onebc, b_oneb = SBp("onebc", [128, T], F32)
        p.op("v", lambda v: v.memset(onebc[:], 1.0), writes=[b_oneb])
        deltat = [SBp("deltat%d" % i, [128, T], F32) for i in range(G)]
        dcum = [SBp("dcum%d" % i, [128, T], F32) for i in range(G)]
        yac = [SBp("yac%d" % i, [128, T], F32) for i in range(G)]
        cbn = [SBp("cbn%d" % i, [128, T], BF16) for i in range(2)]
        ce_s = [SBp("ce_s%d" % i, [128, T], F32) for i in range(2)]
        cr_s = [SBp("cr_s%d" % i, [128, T], BF16) for i in range(2)]
        uf_s = SBp("uf_s", [128, T], F32)
        sz_s = SBp("sz_s", [128, T], F32)
        yg_s = SBp("yg_s", [128, T], BF16)
        py = [[psum(nc, ph, pre + "py%d_%d" % (i, n), [128, 512], sname="Lpy%d_%d" % (i, n))
               for n in range(4)] for i in range(G)]
        for mg in range(MI // G):
            for i in range(G):
                m = mg * G + i
                dt_, bdt_ = deltat[i]
                dc_, bdc_ = dcum[i]
                ya, bya = yac[i]
                p.dma("y", dt_[:], d_delta[m * 128:(m + 1) * 128, :], reads=[b_delta],
                      writes=[bdt_])
                p.dma("s", ya[:], d_y1[m * 128:(m + 1) * 128, :], reads=[b_y1],
                      writes=[bya])
                p.op("v", lambda v, dt_=dt_, dc_=dc_: v.tensor_tensor_scan(
                    out=dc_[:], data0=onebc[:], data1=dt_[:], initial=0.0,
                    op0=ALU.mult, op1=ALU.add), reads=[bdt_, b_oneb], writes=[bdc_])
            for n in range(NST):
                cb, bcb = cbn[n % 2]
                p.dma("g", cb[:], d_bc[16 + n:17 + n, :].broadcast_to([128, T]),
                      reads=[b_bc], writes=[bcb])
                for i in range(G):
                    m = mg * G + i
                    dc_, bdc_ = dcum[i]
                    ce, bce = ce_s[i % 2]
                    cr, bcr = cr_s[i % 2]
                    p.op("s", lambda s, dc_=dc_, ce=ce, m=m, n=n: s.activation(
                        out=ce[:], in_=dc_[:], func=AF.Exp,
                        scale=aal[:, m * NST + n:m * NST + n + 1]),
                        reads=[bdc_, b_aal], writes=[bce])
                    p.op("v", lambda v, ce=ce, cr=cr, m=m, n=n: v.scalar_tensor_tensor(
                        out=cr[:], in0=ce[:], scalar=hbuf[:, m * NST + n:m * NST + n + 1],
                        in1=cb[:], op0=ALU.mult, op1=ALU.mult),
                        reads=[bce, b_hbuf, bcb], writes=[bcr])
                    for nc4 in range(4):
                        ps, bps = py[i][nc4]
                        p.op("t", lambda t, ps=ps, cr=cr, nc4=nc4, n=n: t.matmul(
                            ps[:, :], idb[:], cr[:, nc4 * 512:(nc4 + 1) * 512],
                            start=(n == 0), stop=(n == NST - 1)),
                            reads=[b_idbs, bcr], writes=[bps])
            for i in range(G):
                m = mg * G + i
                ya, bya = yac[i]
                uf, buf_ = uf_s
                sz, bsz = sz_s
                yg, byg = yg_s
                for nc4 in range(4):
                    ps, bps = py[i][nc4]
                    p.op("v", lambda v, ps=ps, ya=ya, nc4=nc4: v.tensor_tensor(
                        out=ya[:, nc4 * 512:(nc4 + 1) * 512],
                        in0=ya[:, nc4 * 512:(nc4 + 1) * 512], in1=ps[:, :],
                        op=ALU.add), reads=[bps, bya], writes=[bya])
                p.dma("y", uf[:], d_u[m * 128:(m + 1) * 128, :], reads=[b_u],
                      writes=[buf_])
                p.dma("s", sz[:], d_sz[m * 128:(m + 1) * 128, :], reads=[b_sz],
                      writes=[bsz])
                p.op("v", lambda v, m=m, ya=ya: v.scalar_tensor_tensor(
                    out=ya[:], in0=uf[:], scalar=dsk[:, m:m + 1], in1=ya[:],
                    op0=ALU.mult, op1=ALU.add),
                    reads=[buf_, b_dsk, bya], writes=[bya])
                p.op("v", lambda v, ya=ya: v.tensor_tensor(
                    out=yg[:], in0=ya[:], in1=sz[:], op=ALU.mult),
                    reads=[bya, bsz], writes=[byg])
                p.dma("y", ygb_h[m * 128:(m + 1) * 128, :], yg[:], reads=[byg],
                      writes=[b_ygbh])
        p.barrier()

    with ExitStack() as ph:
        SBp = lambda n, sh, dt: sbuf(nc, ph, pre + n, sh, dt, sname="L" + n)
        PSp = lambda n, sh: psum(nc, ph, pre + n, sh, sname="L" + n)
        ygr, b_ygr = SBp("ygr", [128, MI * T], BF16)
        p.dma("y", ygr[:].rearrange("p (m t) -> p m t", t=T),
              ygb_h.rearrange("(m p) t -> p m t", p=128), reads=[b_ygbh],
              writes=[b_ygr])
        wv = [SBp("wv%d" % i, [128, MI * 128], BF16) for i in range(2)]
        pso = [PSp("pso%d" % n, [128, 512]) for n in range(4)]
        oat = SBp("oat", [128, T], F32)
        for m in range(D // 128):
            wb, bwb = wv[m % 2]
            p.dma("g", wb[:].rearrange("p (k c) -> p k c", k=MI),
                  d_woT[:, m * 128:(m + 1) * 128].rearrange("(k p) c -> p k c", p=128),
                  reads=[b_woT], writes=[bwb])
            oa, boa = oat
            for k in range(MI):
                for n in range(4):
                    ps, bps = pso[n]
                    p.op("t", lambda t, ps=ps, k=k, n=n: t.matmul(
                        ps[:, :], wb[:, k * 128:(k + 1) * 128],
                        ygr[:, k * T + n * 512:k * T + (n + 1) * 512],
                        start=(k == 0), stop=(k == MI - 1)),
                        reads=[bwb, b_ygr], writes=[bps])
            for n in range(4):
                ps, bps = pso[n]
                p.op("s", lambda s, ps=ps, n=n: s.copy(
                    out=oa[:, n * 512:(n + 1) * 512], in_=ps[:, :]),
                    reads=[bps], writes=[boa])
            p.dma("y", o_at0[m * 128:(m + 1) * 128, :], oa[:], reads=[boa],
                  writes=[b_at0])
        p.barrier()
    p.barrier()




def build_l0scan2():
    nc = bass.Bass("TRN2", target_bir_lowering=False)
    es = ExitStack()
    dram = lambda n, sh, dt, k="ExternalInput": (nc.dram_tensor(n, sh, dt, kind=k).ap(),
                                                 BufT(sname="S" + n, waw=False))
    d_delta, b_delta = dram("delta", [DI, T], F32)
    d_bc, b_bc = dram("bc", [32, T], BF16)
    d_alog, b_alog = dram("alog", [DI, NST], F32)
    d_y1, b_y1 = dram("y1", [DI, T], F32)
    d_u, b_u = dram("u", [DI, T], F32)
    d_sz, b_sz = dram("sz", [DI, T], F32)
    d_dskip, b_dskip = dram("dskip", [128, MI], F32)
    d_preds, b_preds = dram("predS", [8 * 128, MI * NST], F32)
    d_predd, b_predd = dram("predDT", [8, MI * 128], F32)
    d_m2t, b_m2t = dram("m2T", [8, 8], F32)
    d_woT, b_woT = dram("w_outT", [DI, D], F32)
    d_idb, b_idb = dram("identb", [128, 128], BF16)
    o_at0, b_at0 = dram("o_attn0", [D, T], F32, "ExternalOutput")
    ygb_h, b_ygbh = dram("ygb_hbm", [DI, T], BF16, "Internal")
    Dio = {
        "identb": (d_idb, b_idb),
        "delta": (d_delta, b_delta),
        "bc": (d_bc, b_bc),
        "alog": (d_alog, b_alog),
        "y1": (d_y1, b_y1),
        "u": (d_u, b_u),
        "sz": (d_sz, b_sz),
        "dskip": (d_dskip, b_dskip),
        "predS": (d_preds, b_preds),
        "predDT": (d_predd, b_predd),
        "m2T": (d_m2t, b_m2t),
        "w_outT": (d_woT, b_woT),
        "o_attn0": (o_at0, b_at0),
        "ygb_hbm": (ygb_h, b_ygbh),
    }
    p = Prog(nc, es)
    emit_l0scan2(nc, es, p, Dio)
    es.close()
    return nc


def emit_ffn0(nc, es, p, Dio, pre="c_"):
    d_xn, b_xn_h = Dio["xn"]
    d_w1T, b_w1T = Dio["w1T"]
    d_w2T, b_w2T = Dio["w2T"]
    d_w3T, b_w3T = Dio["w3T"]
    o_f0, b_f0 = Dio["o_ffn0"]
    hid_h, b_hid_h = Dio["hid_hbm"]
    SB = lambda n, sh, dt: sbuf(nc, es, pre + n, sh, dt, sname="L" + n)
    MH = HF // 128

    ones_f = SB("ones_f", [128, 1], F32)
    ones_r = SB("ones_r", [1, 128], F32)
    chalf = SB("chalf", [128, 2], F32)
    p.op("v", lambda v: v.memset(ones_f[0][:], 1.0), writes=[ones_f[1]])
    p.op("v", lambda v: v.memset(ones_r[0][:], 1.0), writes=[ones_r[1]])
    p.op("v", lambda v: v.memset(chalf[0][:, 0:1], 0.5), writes=[chalf[1]])
    p.op("v", lambda v: v.memset(chalf[0][:, 1:2], 0.25), writes=[chalf[1]])
    thr = SB("thr", [128, 6], F32)

    with ExitStack() as ph:
        SBp = lambda n, sh, dt: sbuf(nc, ph, pre + n, sh, dt, sname="L" + n)
        wsl = [SBp("wsl%d" % i, [128, HF], F32) for i in range(2)]
        red = SBp("red", [128, 3], F32)
        acc = SBp("acc", [128, 1], F32)
        prg = psum(nc, ph, pre + "prg", [1, 1])
        prb = psum(nc, ph, pre + "prb", [128, 1])
        g1 = SBp("g1", [1, 1], F32)
        for wi, (dw, bdw, kt) in enumerate([(d_w1T, b_w1T, KD), (d_w2T, b_w2T, KD),
                                            (d_w3T, b_w3T, MH)]):
            ac, bac = acc
            for k in range(kt):
                ws, bws = wsl[k % 2]
                ncol = HF if wi < 2 else D
                p.dma("y" if k % 2 == 0 else "s", ws[:, 0:ncol],
                      dw[k * 128:(k + 1) * 128, :], reads=[bdw], writes=[bws])
                rd, brd = red
                p.op("v", lambda v, ws=ws, ncol=ncol: v.tensor_reduce(
                    out=rd[:, 0:1], in_=ws[:, 0:ncol], axis=mybir.AxisListType.X,
                    op=ALU.add, apply_absolute_value=True), reads=[bws], writes=[brd])
                if k == 0:
                    p.op("v", lambda v: v.tensor_copy(out=ac[:], in_=rd[:, 0:1]),
                         reads=[brd], writes=[bac])
                else:
                    p.op("v", lambda v: v.tensor_tensor(
                        out=ac[:], in0=ac[:], in1=rd[:, 0:1], op=ALU.add),
                        reads=[brd], writes=[bac])
            pg, bpg = prg
            p.op("t", lambda t: t.matmul(pg[:, :], ones_f[0][:], ac[:],
                                         start=True, stop=True),
                 reads=[ones_f[1], bac], writes=[bpg])
            gg, bgg = g1
            nel = float(D * HF)
            p.op("v", lambda v, nel=nel: v.tensor_scalar(
                out=gg[:], in0=pg[:, :], scalar1=0.5 / nel, scalar2=0.5e-8,
                op0=ALU.mult, op1=ALU.add), reads=[bpg], writes=[bgg])
            pb, bpb = prb
            p.op("t", lambda t: t.matmul(pb[:, :], ones_r[0][:], gg[:],
                                         start=True, stop=True),
                 reads=[ones_r[1], bgg], writes=[bpb])
            p.op("s", lambda s, wi=wi: s.copy(out=thr[0][:, wi:wi + 1], in_=pb[:, :]),
                 reads=[bpb], writes=[thr[1]])
            p.op("v", lambda v, wi=wi: v.tensor_scalar_mul(
                out=thr[0][:, 3 + wi:4 + wi], in0=thr[0][:, wi:wi + 1], scalar1=-1.0),
                reads=[thr[1]], writes=[thr[1]])
        p.barrier()

    def quantize(pq, wf, bwf, qb, bqb, scr, bscr, ncol, wi):
        pq.op("s", lambda s: s.activation(
            out=qb[:, 0:ncol], in_=wf[:, 0:ncol], func=AF.Sign,
            bias=thr[0][:, 3 + wi:4 + wi]), reads=[bwf, thr[1]], writes=[bqb])
        pq.op("s", lambda s: s.activation(
            out=scr[:, 0:ncol], in_=wf[:, 0:ncol], func=AF.Sign,
            bias=thr[0][:, wi:wi + 1]), reads=[bwf, thr[1]], writes=[bscr])
        pq.op("v", lambda v: v.tensor_tensor(
            out=qb[:, 0:ncol], in0=qb[:, 0:ncol], in1=scr[:, 0:ncol],
            op=ALU.add), reads=[bscr, bqb], writes=[bqb])

    with ExitStack() as ph:
        SBp = lambda n, sh, dt: sbuf(nc, ph, pre + n, sh, dt, sname="L" + n)
        PSp = lambda n, sh: psum(nc, ph, pre + n, sh, sname="L" + n)
        xnr = SBp("xnr", [128, KD * T], BF16)
        b_xnr = xnr[1]
        p.dma("y", xnr[0][:].rearrange("p (k t) -> p k t", t=T),
              d_xn.rearrange("(k p) t -> p k t", p=128), reads=[b_xn_h], writes=[b_xnr])
        w1s = [SBp("w1s%d" % i, [128, KD * 128], F32) for i in range(2)]
        w2s = [SBp("w2s%d" % i, [128, KD * 128], F32) for i in range(2)]
        q1d = [SBp("q1_%d" % i, [128, KD * 128], BF16) for i in range(2)]
        q2d = [SBp("q2_%d" % i, [128, KD * 128], BF16) for i in range(2)]
        qsd = [SBp("qs_%d" % i, [128, KD * 128], BF16) for i in range(2)]
        ps1 = [PSp("ps1%d" % n, [128, 512]) for n in range(4)]
        ps2 = [PSp("ps2%d" % n, [128, 512]) for n in range(4)]
        sgt = SBp("sgt", [128, T], BF16)
        hidt = SBp("hidt", [128, T], BF16)
        for m in range(MH):
            wa, bwa = w1s[m % 2]
            wb, bwb = w2s[m % 2]
            p.dma("y", wa[:].rearrange("p (k c) -> p k c", k=KD),
                  d_w1T[:, m * 128:(m + 1) * 128].rearrange("(k p) c -> p k c", p=128),
                  reads=[b_w1T], writes=[bwa])
            p.dma("s", wb[:].rearrange("p (k c) -> p k c", k=KD),
                  d_w2T[:, m * 128:(m + 1) * 128].rearrange("(k p) c -> p k c", p=128),
                  reads=[b_w2T], writes=[bwb])
            q1 = q1d[m % 2]
            q2 = q2d[m % 2]
            qs = qsd[m % 2]
            quantize(p, wa, bwa, q1[0], q1[1], qs[0], qs[1], KD * 128, 0)
            quantize(p, wb, bwb, q2[0], q2[1], qs[0], qs[1], KD * 128, 1)
            sg, bsg = sgt
            hd, bhd = hidt
            for k in range(KD):
                for n in range(4):
                    pa, bpa = ps1[n]
                    p.op("t", lambda t, pa=pa, k=k, n=n, q1=q1: t.matmul(
                        pa[:, :], q1[0][:, k * 128:(k + 1) * 128],
                        xnr[0][:, k * T + n * 512:k * T + (n + 1) * 512],
                        start=(k == 0), stop=(k == KD - 1)),
                        reads=[q1[1], b_xnr], writes=[bpa])
                for n in range(4):
                    pb, bpb = ps2[n]
                    p.op("t", lambda t, pb=pb, k=k, n=n, q2=q2: t.matmul(
                        pb[:, :], q2[0][:, k * 128:(k + 1) * 128],
                        xnr[0][:, k * T + n * 512:k * T + (n + 1) * 512],
                        start=(k == 0), stop=(k == KD - 1)),
                        reads=[q2[1], b_xnr], writes=[bpb])
            for n in range(4):
                pa, bpa = ps1[n]
                pb, bpb = ps2[n]
                p.op("s", lambda s, pa=pa, n=n: s.activation(
                    out=sg[:, n * 512:(n + 1) * 512], in_=pa[:, :], func=AF.Sigmoid,
                    scale=chalf[0][:, 0:1]), reads=[bpa, chalf[1]], writes=[bsg])
                p.op("v", lambda v, pb=pb, n=n: v.tensor_tensor(
                    out=hd[:, n * 512:(n + 1) * 512], in0=sg[:, n * 512:(n + 1) * 512],
                    in1=pb[:, :], op=ALU.mult), reads=[bsg, bpb], writes=[bhd])
            p.dma("y", hid_h[m * 128:(m + 1) * 128, :], hd[:], reads=[bhd],
                  writes=[b_hid_h])
        p.barrier()

    with ExitStack() as ph:
        SBp = lambda n, sh, dt: sbuf(nc, ph, pre + n, sh, dt, sname="L" + n)
        PSp = lambda n, sh: psum(nc, ph, pre + n, sh, sname="L" + n)
        hfr = SBp("hfr", [128, MH * 1024], BF16)
        w3s = [SBp("w3s%d" % i, [128, MH * 128], F32) for i in range(2)]
        q3d = [SBp("q3_%d" % i, [128, MH * 128], BF16) for i in range(2)]
        qs3d = [SBp("qs3_%d" % i, [128, MH * 128], BF16) for i in range(2)]
        ps3 = [PSp("ps3%d" % n, [128, 512]) for n in range(4)]
        outt = SBp("outt", [128, 1024], F32)
        for th in range(2):
            toff = th * 1024
            bh = hfr[1]
            p.dma("y", hfr[0][:].rearrange("p (k t) -> p k t", t=1024),
                  hid_h[:, toff:toff + 1024].rearrange("(k p) t -> p k t", p=128),
                  reads=[b_hid_h], writes=[bh])
            for m in range(D // 128):
                w3_, bw3_ = w3s[m % 2]
                p.dma("s", w3_[:].rearrange("p (k c) -> p k c", k=MH),
                      d_w3T[:, m * 128:(m + 1) * 128].rearrange("(k p) c -> p k c", p=128),
                      reads=[b_w3T], writes=[bw3_])
                q3 = q3d[m % 2]
                qs3 = qs3d[m % 2]
                quantize(p, w3_, bw3_, q3[0], q3[1], qs3[0], qs3[1], MH * 128, 2)
                ot, bot = outt
                for k in range(MH):
                    for n in range(2):
                        ps, bps = ps3[(th * 2 + n) % 4]
                        p.op("t", lambda t, ps=ps, k=k, n=n, q3=q3: t.matmul(
                            ps[:, :], q3[0][:, k * 128:(k + 1) * 128],
                            hfr[0][:, k * 1024 + n * 512:k * 1024 + (n + 1) * 512],
                            start=(k == 0), stop=(k == MH - 1)),
                            reads=[q3[1], bh], writes=[bps])
                for n in range(2):
                    ps, bps = ps3[(th * 2 + n) % 4]
                    p.op("s", lambda s, ps=ps, n=n: s.activation(
                        out=ot[:, n * 512:(n + 1) * 512], in_=ps[:, :],
                        func=AF.Identity, scale=chalf[0][:, 1:2]),
                        reads=[bps, chalf[1]], writes=[bot])
                p.dma("y", o_f0[m * 128:(m + 1) * 128, toff:toff + 1024],
                      ot[:, 0:1024], reads=[bot], writes=[b_f0])
        p.barrier()
    p.barrier()




def build_ffn0():
    nc = bass.Bass("TRN2", target_bir_lowering=False)
    es = ExitStack()
    dram = lambda n, sh, dt, k="ExternalInput": (nc.dram_tensor(n, sh, dt, kind=k).ap(),
                                                 BufT(sname="S" + n, waw=False))
    d_xn, b_xn_h = dram("xn", [D, T], BF16)
    d_w1T, b_w1T = dram("w1T", [D, HF], F32)
    d_w2T, b_w2T = dram("w2T", [D, HF], F32)
    d_w3T, b_w3T = dram("w3T", [HF, D], F32)
    o_f0, b_f0 = dram("o_ffn0", [D, T], F32, "ExternalOutput")
    hid_h, b_hid_h = dram("hid_hbm", [HF, T], BF16, "Internal")
    Dio = {
        "xn": (d_xn, b_xn_h),
        "w1T": (d_w1T, b_w1T),
        "w2T": (d_w2T, b_w2T),
        "w3T": (d_w3T, b_w3T),
        "o_ffn0": (o_f0, b_f0),
        "hid_hbm": (hid_h, b_hid_h),
    }
    p = Prog(nc, es)
    emit_ffn0(nc, es, p, Dio)
    es.close()
    return nc


def emit_rg1(nc, es, p, Dio, pre="d1_"):
    d_x, b_x = Dio["x_f"]
    d_a0, b_a0 = Dio["attn0"]
    d_f0, b_f0 = Dio["ffn0"]
    d_win1, b_win1 = Dio["w_in1"]
    d_bin, b_bin = Dio["b_in"]
    d_wg, b_wg = Dio["wg"]
    d_bg, b_bg = Dio["bg"]
    d_arp, b_arp = Dio["arp"]
    d_aip, b_aip = Dio["aip"]
    d_mask, b_mask = Dio["maskrow"]
    o_x1, b_x1 = Dio["o_x1"]
    o_xn1, b_oxn1 = Dio["o_xn1"]
    o_ec, b_oec = Dio["o_ec"]
    o_es, b_oes = Dio["o_es"]
    o_ur, b_our = Dio["o_ur"]
    o_ui, b_oui = Dio["o_ui"]
    o_sum, b_osum = Dio["o_sumry"]
    SB = lambda n, sh, dt: sbuf(nc, es, pre + n, sh, dt, sname="L" + n)

    cpool, b_cp = SB("cpool", [128, 12], F32)
    p.op("v", lambda v: v.memset(cpool[:, 0:1], -1.0), writes=[b_cp])
    p.op("v", lambda v: v.memset(cpool[:, 1:2], 1.0), writes=[b_cp])
    p.op("v", lambda v: v.memset(cpool[:, 2:3], 1e-5), writes=[b_cp])
    p.op("v", lambda v: v.memset(cpool[:, 3:4], -0.5), writes=[b_cp])
    p.op("v", lambda v: v.memset(cpool[:, 4:5], 0.5), writes=[b_cp])
    p.op("v", lambda v: v.memset(cpool[:, 5:6], 2.0), writes=[b_cp])
    p.op("v", lambda v: v.memset(cpool[:, 6:7], float(np.pi)), writes=[b_cp])
    p.op("v", lambda v: v.memset(cpool[:, 7:8], float(-np.pi)), writes=[b_cp])
    p.op("v", lambda v: v.memset(cpool[:, 8:9], float(np.pi / 2)), writes=[b_cp])
    p.op("v", lambda v: v.memset(cpool[:, 9:10], float(2 * np.pi)), writes=[b_cp])
    p.op("v", lambda v: v.memset(cpool[:, 10:11], float(-2 * np.pi)), writes=[b_cp])
    c_m1 = cpool[:, 0:1]; c_p1 = cpool[:, 1:2]; c_mh = cpool[:, 3:4]
    c_half = cpool[:, 4:5]; c_two = cpool[:, 5:6]; c_pi = cpool[:, 6:7]
    c_mpi = cpool[:, 7:8]; c_pi2 = cpool[:, 8:9]; c_2pi = cpool[:, 9:10]

    spc, b_spc = SB("spc", [128, 8], F32)
    aic, b_aic = SB("aic", [128, 8], F32)
    with ExitStack() as ph:
        t1, bt1 = sbuf(nc, ph, pre + "spt1", [128, 8], F32)
        t2, bt2 = sbuf(nc, ph, pre + "spt2", [128, 8], F32)
        p.dma("y", aic[:], d_aip, reads=[b_aip], writes=[b_aic])
        p.dma("y", t1[:], d_arp, reads=[b_arp], writes=[bt1])
        p.op("s", lambda s: s.activation(out=t2[:], in_=t1[:], func=AF.Abs),
             reads=[bt1], writes=[bt2])
        p.op("s", lambda s: s.activation(out=t2[:], in_=t2[:], func=AF.Exp,
                                         scale=c_m1), reads=[bt2, b_cp], writes=[bt2])
        p.op("s", lambda s: s.activation(out=t2[:], in_=t2[:], func=AF.Ln,
                                         bias=c_p1), reads=[bt2, b_cp], writes=[bt2])
        p.op("s", lambda s: s.activation(out=t1[:], in_=t1[:], func=AF.Relu),
             reads=[bt1], writes=[bt1])
        p.op("v", lambda v: v.tensor_tensor(out=spc[:], in0=t1[:], in1=t2[:],
                                            op=ALU.add), reads=[bt1, bt2], writes=[b_spc])
        p.op("v", lambda v: v.tensor_scalar_mul(out=spc[:], in0=spc[:], scalar1=-8.0),
             reads=[b_spc], writes=[b_spc])
        p.barrier()

    xh, b_xh = SB("xh", [128, KD * T], BF16)

    ph_xn = ExitStack()
    xn1, b_xn1 = sbuf(nc, ph_xn, pre + "xn1", [128, KD * T], BF16)
    with ExitStack() as ph:
        SBp = lambda n, sh, dt: sbuf(nc, ph, pre + n, sh, dt, sname="L" + n)
        PSp = lambda n, sh: psum(nc, ph, pre + n, sh, sname="L" + n)
        e_a, b_ea = SBp("e_a", [128, 33], BF16)
        e_b, b_eb = SBp("e_b", [128, 33], BF16)
        onesb, b_on = SBp("onesb", [1, 128], BF16)
        p.op("v", lambda v: v.memset(e_a[:, :], 0.0), writes=[b_ea])
        p.op("v", lambda v: v.memset(e_a[:, 0:1], 1.0), writes=[b_ea])
        p.op("v", lambda v: v.memset(e_b[:, :], 0.0), writes=[b_eb])
        p.op("v", lambda v: v.memset(e_b[:, 32:33], 1.0), writes=[b_eb])
        p.op("v", lambda v: v.memset(onesb[:], 1.0), writes=[b_on])
        x1t = [SBp("x1t0", [128, T], F32)] * 2
        att = SBp("att", [128, T], F32)
        sqb = [SBp("sqb0", [128, T], BF16)] * 2
        pstat = [PSp("pstat%d" % n, [33, 512]) for n in range(4)]
        m_r = SBp("m_r", [1, T], F32)
        e_r = SBp("e_r", [1, T], F32)
        q_r = SBp("q_r", [1, T], F32)
        r_r = SBp("r_r", [1, T], F32)
        mrowb = SBp("mrowb", [1, T], BF16)
        rrowb = SBp("rrowb", [1, T], BF16)
        pbc = [PSp("pbc%d" % i, [128, 512]) for i in range(2)]
        mbc = SBp("mbc", [128, T], BF16)
        rbc = SBp("rbc", [128, T], BF16)
        tscr = SBp("tscr", [128, T], BF16)
        for k in range(KD):
            bb = k % 2
            xb, bxb = x1t[bb]
            at, bat = att
            sq, bsq = sqb[bb]
            p.dma("y", xb[:], d_x[k * 128:(k + 1) * 128, 0:T], reads=[b_x], writes=[bxb])
            p.dma("s", at[:], d_a0[k * 128:(k + 1) * 128, :], reads=[b_a0], writes=[bat])
            p.op("v", lambda v, xb=xb: v.tensor_tensor(out=xb[:], in0=xb[:], in1=at[:],
                                                       op=ALU.add),
                 reads=[bat, bxb], writes=[bxb])
            p.dma("s", at[:], d_f0[k * 128:(k + 1) * 128, :], reads=[b_f0], writes=[bat])
            p.op("v", lambda v, xb=xb: v.tensor_tensor(out=xb[:], in0=xb[:], in1=at[:],
                                                       op=ALU.add),
                 reads=[bat, bxb], writes=[bxb])
            p.dma("y", o_x1[k * 128:(k + 1) * 128, :], xb[:], reads=[bxb], writes=[b_x1])
            p.op("s", lambda s, k=k, xb=xb: s.copy(
                out=xn1[:, k * T:(k + 1) * T], in_=xb[:]), reads=[bxb], writes=[b_xn1])
            p.op("s", lambda s, xb=xb, sq=sq: s.activation(
                out=sq[:], in_=xb[:], func=AF.Square), reads=[bxb], writes=[bsq])
            for n in range(4):
                ps, bps = pstat[n]
                p.op("t", lambda t, k=k, ps=ps, n=n: t.matmul(
                    ps[0:33, :], e_a[:, 0:33], xn1[:, k * T + n * 512:k * T + (n + 1) * 512],
                    start=(k == 0), stop=False), reads=[b_xn1, b_ea], writes=[bps])
                p.op("t", lambda t, k=k, ps=ps, sq=sq, n=n: t.matmul(
                    ps[0:33, :], e_b[:, 0:33], sq[:, n * 512:(n + 1) * 512],
                    start=False, stop=(k == KD - 1)), reads=[bsq, b_eb], writes=[bps])
        mr, bmr = m_r
        er, ber = e_r
        qr, bqr = q_r
        rr, brr = r_r
        for n in range(4):
            ps, bps = pstat[n]
            p.op("s", lambda s, ps=ps, n=n: s.copy(
                out=mr[0:1, n * 512:(n + 1) * 512], in_=ps[0:1, :]),
                reads=[bps], writes=[bmr])
            p.op("s", lambda s, ps=ps, n=n: s.copy(
                out=er[0:1, n * 512:(n + 1) * 512], in_=ps[32:33, :]),
                reads=[bps], writes=[ber])
        p.op("v", lambda v: v.tensor_scalar_mul(out=mr[:], in0=mr[:], scalar1=1.0 / 2048.0),
             reads=[bmr], writes=[bmr])
        p.op("v", lambda v: v.tensor_scalar_mul(out=er[:], in0=er[:], scalar1=1.0 / 2048.0),
             reads=[ber], writes=[ber])
        p.op("v", lambda v: v.tensor_tensor(out=qr[:], in0=mr[:], in1=mr[:], op=ALU.mult),
             reads=[bmr], writes=[bqr])
        p.op("v", lambda v: v.tensor_tensor(out=er[:], in0=er[:], in1=qr[:],
                                            op=ALU.subtract), reads=[bqr, ber], writes=[ber])
        p.op("s", lambda s: s.activation(out=qr[:], in_=er[:], func=AF.Ln,
                                         bias=cpool[0:1, 2:3]), reads=[ber, b_cp],
             writes=[bqr])
        p.op("s", lambda s: s.activation(out=rr[:], in_=qr[:], func=AF.Exp,
                                         scale=cpool[0:1, 3:4]), reads=[bqr, b_cp],
             writes=[brr])
        mb, bmb = mrowb
        rb2, brb2 = rrowb
        p.op("s", lambda s: s.copy(out=mb[:], in_=mr[:]), reads=[bmr], writes=[bmb])
        p.op("s", lambda s: s.copy(out=rb2[:], in_=rr[:]), reads=[brr], writes=[brb2])
        mc, bmc = mbc
        rc, brc = rbc
        for n in range(4):
            ps, bps = pbc[n % 2]
            p.op("t", lambda t, ps=ps, n=n: t.matmul(
                ps[:, :], onesb[:], mb[0:1, n * 512:(n + 1) * 512], start=True, stop=True),
                reads=[b_on, bmb], writes=[bps])
            p.op("s", lambda s, ps=ps, n=n: s.copy(
                out=mc[:, n * 512:(n + 1) * 512], in_=ps[:, :]), reads=[bps], writes=[bmc])
            p.op("t", lambda t, ps=ps, n=n: t.matmul(
                ps[:, :], onesb[:], rb2[0:1, n * 512:(n + 1) * 512], start=True, stop=True),
                reads=[b_on, brb2], writes=[bps])
            p.op("s", lambda s, ps=ps, n=n: s.copy(
                out=rc[:, n * 512:(n + 1) * 512], in_=ps[:, :]), reads=[bps], writes=[brc])
        ts_, bts = tscr
        for k in range(KD):
            p.op("v", lambda v, k=k: v.tensor_tensor(
                out=ts_[:], in0=xn1[:, k * T:(k + 1) * T], in1=mc[:], op=ALU.subtract),
                reads=[b_xn1, bmc], writes=[bts])
            p.op("v", lambda v, k=k: v.tensor_tensor(
                out=xn1[:, k * T:(k + 1) * T], in0=ts_[:], in1=rc[:], op=ALU.mult),
                reads=[bts, brc], writes=[b_xn1])
            p.dma("y", o_xn1[k * 128:(k + 1) * 128, :], xn1[:, k * T:(k + 1) * T],
                  reads=[b_xn1], writes=[b_oxn1])
        p.barrier()

    with ExitStack() as ph:
        SBp = lambda n, sh, dt: sbuf(nc, ph, pre + n, sh, dt, sname="L" + n)
        PSp = lambda n, sh: psum(nc, ph, pre + n, sh, sname="L" + n)
        wv = [SBp("wv%d" % i, [128, KD * 128], BF16) for i in range(2)]
        pxz = [PSp("pxz%d" % n, [128, 512]) for n in range(4)]
        binc, b_binc = SBp("binc", [128, KD], F32)
        p.dma("y", binc[:], d_bin, reads=[b_bin], writes=[b_binc])
        for m in range(KD):
            wb, bwb = wv[m % 2]
            p.dma("g", wb[:].rearrange("p (k c) -> p k c", k=KD),
                  d_win1[:, m * 128:(m + 1) * 128].rearrange("(k p) c -> p k c", p=128),
                  reads=[b_win1], writes=[bwb])
            for k in range(KD):
                for n in range(4):
                    ps, bps = pxz[n]
                    p.op("t", lambda t, ps=ps, k=k, n=n: t.matmul(
                        ps[:, :], wb[:, k * 128:(k + 1) * 128],
                        xn1[:, k * T + n * 512:k * T + (n + 1) * 512],
                        start=(k == 0), stop=(k == KD - 1)),
                        reads=[bwb, b_xn1], writes=[bps])
            for n in range(4):
                ps, bps = pxz[n]
                p.op("s", lambda s, ps=ps, m=m, n=n: s.activation(
                    out=xh[:, m * T + n * 512:m * T + (n + 1) * 512], in_=ps[:, :],
                    func=AF.Identity, bias=binc[:, m:m + 1]),
                    reads=[bps, b_binc], writes=[b_xh])
        p.barrier()
    ph_xn.close()

    with ExitStack() as ph:
        SBp = lambda n, sh, dt: sbuf(nc, ph, pre + n, sh, dt, sname="L" + n)
        PSp = lambda n, sh: psum(nc, ph, pre + n, sh, sname="L" + n)
        wgt = SBp("wgt", [128, 4 * 128], BF16)
        pgx = [PSp("pgx%d" % n, [128, 512]) for n in range(4)]
        pga = [PSp("pga%d" % n, [128, 512]) for n in range(4)]
        bgc, b_bgc = SBp("bgc", [128, 16], F32)
        p.dma("y", bgc[:], d_bg, reads=[b_bg], writes=[b_bgc])
        maskb, b_maskb = SBp("maskb", [128, T], F32)
        p.dma("y", maskb[:], d_mask.broadcast_to([128, T]), reads=[b_mask],
              writes=[b_maskb])
        W = [SBp("w%d" % i, [128, T], F32) for i in range(10)]
        summ, b_summ = SBp("summ", [128, 32], F32)
        carr, b_carr = SBp("carr", [128, (NBLK + 1) * 8], F32)
        cari, b_cari = SBp("cari", [128, (NBLK + 1) * 8], F32)
        rrp, b_rrp = SBp("rrp", [128, (NBLK + 1) * 8], F32)
        rip, b_rip = SBp("rip", [128, (NBLK + 1) * 8], F32)
        ecE, b_ecE = SBp("ecE", [128, NBLK * 8], F32)
        esE, b_esE = SBp("esE", [128, NBLK * 8], F32)
        srE, b_srE = SBp("srE", [128, NBLK * 8], F32)
        siE, b_siE = SBp("siE", [128, NBLK * 8], F32)
        ctmp, b_ctmp = SBp("ctmp", [128, 48], F32)

        for ct in range(8):
            wg, bwg = wgt
            p.dma("g", wg[:, 0:256].rearrange("p (k c) -> p k c", k=2),
                  d_wg.rearrange("(k p) c -> p k c", p=128)[:, 2 * ct:2 * ct + 2, :],
                  reads=[b_wg], writes=[bwg])
            p.dma("g", wg[:, 256:512].rearrange("p (k c) -> p k c", k=2),
                  d_wg.rearrange("(k p) c -> p k c", p=128)[:, 16 + 2 * ct:18 + 2 * ct, :],
                  reads=[b_wg], writes=[bwg])
            gx, bgx = W[0]
            ga, bga = W[1]
            for n in range(4):
                px, bpx = pgx[n]
                pa, bpa = pga[n]
                for kk in range(2):
                    ksrc = (2 * ct + kk) * T + n * 512
                    p.op("t", lambda t, px=px, kk=kk, ksrc=ksrc: t.matmul(
                        px[:, :], wg[:, kk * 128:(kk + 1) * 128],
                        xh[:, ksrc:ksrc + 512],
                        start=(kk == 0), stop=(kk == 1)), reads=[bwg, b_xh], writes=[bpx])
                    p.op("t", lambda t, pa=pa, kk=kk, ksrc=ksrc: t.matmul(
                        pa[:, :], wg[:, (2 + kk) * 128:(3 + kk) * 128],
                        xh[:, ksrc:ksrc + 512],
                        start=(kk == 0), stop=(kk == 1)), reads=[bwg, b_xh], writes=[bpa])
                p.op("s", lambda s, px=px, n=n, ct=ct: s.activation(
                    out=gx[:, n * 512:(n + 1) * 512], in_=px[:, :], func=AF.Sigmoid,
                    bias=bgc[:, ct:ct + 1]), reads=[bpx, b_bgc], writes=[bgx])
                p.op("s", lambda s, pa=pa, n=n, ct=ct: s.activation(
                    out=ga[:, n * 512:(n + 1) * 512], in_=pa[:, :], func=AF.Sigmoid,
                    bias=bgc[:, 8 + ct:9 + ct]), reads=[bpa, b_bgc], writes=[bga])
            la, bla = W[2]
            th, bth = W[3]
            p.op("v", lambda v, ct=ct: v.tensor_scalar(
                out=la[:], in0=ga[:], scalar1=spc[:, ct:ct + 1], scalar2=None,
                op0=ALU.mult), reads=[bga, b_spc], writes=[bla])
            p.op("v", lambda v, ct=ct: v.tensor_scalar(
                out=th[:], in0=ga[:], scalar1=aic[:, ct:ct + 1], scalar2=None,
                op0=ALU.mult), reads=[bga, b_aic], writes=[bth])
            nt, bnt = W[4]
            p.op("s", lambda s: s.activation(out=nt[:], in_=la[:], func=AF.Exp,
                                             scale=c_two), reads=[bla, b_cp], writes=[bnt])
            p.op("v", lambda v: v.tensor_scalar(
                out=nt[:], in0=nt[:], scalar1=-1.0, scalar2=1.0, op0=ALU.mult,
                op1=ALU.add), reads=[bnt], writes=[bnt])
            p.op("s", lambda s: s.activation(out=nt[:], in_=nt[:], func=AF.Ln),
                 reads=[bnt], writes=[bnt])
            p.op("s", lambda s: s.activation(out=nt[:], in_=nt[:], func=AF.Exp,
                                             scale=c_half), reads=[bnt, b_cp], writes=[bnt])
            lb, blb = W[5]
            tb, btb = W[6]
            p.op("v", lambda v: v.tensor_tensor_scan(
                out=lb[:], data0=maskb[:], data1=la[:], initial=0.0,
                op0=ALU.mult, op1=ALU.add), reads=[b_maskb, bla], writes=[blb])
            p.op("v", lambda v: v.tensor_tensor_scan(
                out=tb[:], data0=maskb[:], data1=th[:], initial=0.0,
                op0=ALU.mult, op1=ALU.add), reads=[b_maskb, bth], writes=[btb])
            xr, bxr = W[2]
            xi, bxi = W[3]
            p.op("v", lambda v, ct=ct: v.tensor_tensor(
                out=xr[:], in0=gx[:], in1=xh[:, ct * T:(ct + 1) * T], op=ALU.mult),
                reads=[bgx, b_xh], writes=[bxr])
            p.op("v", lambda v: v.tensor_tensor(out=xr[:], in0=xr[:], in1=nt[:],
                                                op=ALU.mult), reads=[bnt, bxr], writes=[bxr])
            p.op("v", lambda v, ct=ct: v.tensor_tensor(
                out=xi[:], in0=gx[:], in1=xh[:, (8 + ct) * T:(9 + ct) * T], op=ALU.mult),
                reads=[bgx, b_xh], writes=[bxi])
            p.op("v", lambda v: v.tensor_tensor(out=xi[:], in0=xi[:], in1=nt[:],
                                                op=ALU.mult), reads=[bnt, bxi], writes=[bxi])
            mm_, bmm = W[4]
            sc_, bsc_ = W[7]
            p.op("v", lambda v: v.tensor_scalar(
                out=mm_[:], in0=tb[:], scalar1=float(np.pi), scalar2=None,
                op0=ALU.add), reads=[btb, b_cp], writes=[bmm])
            for _ in range(5):
                p.op("v", lambda v: v.tensor_scalar(
                    out=sc_[:], in0=mm_[:], scalar1=float(2 * np.pi), scalar2=None,
                    op0=ALU.is_ge), reads=[bmm], writes=[bsc_])
                p.op("v", lambda v: v.scalar_tensor_tensor(
                    out=mm_[:], in0=sc_[:], scalar=cpool[:, 10:11], in1=mm_[:],
                    op0=ALU.mult, op1=ALU.add), reads=[bsc_, b_cp, bmm], writes=[bmm])
            sinb, bsin = W[6]
            p.op("s", lambda s: s.activation(out=sinb[:], in_=mm_[:], func=AF.Sin,
                                             bias=c_mpi), reads=[bmm, b_cp], writes=[bsin])
            cosb, bcos = W[7]
            p.op("s", lambda s: s.activation(out=cosb[:], in_=mm_[:], func=AF.Abs,
                                             bias=c_mpi), reads=[bmm, b_cp], writes=[bcos])
            p.op("s", lambda s: s.activation(out=cosb[:], in_=cosb[:], func=AF.Sin,
                                             scale=c_m1, bias=c_pi2),
                 reads=[bcos, b_cp], writes=[bcos])
            eb, beb = W[4]
            p.op("s", lambda s: s.activation(out=eb[:], in_=lb[:], func=AF.Exp),
                 reads=[blb], writes=[beb])
            ei, bei = W[8]
            p.op("s", lambda s: s.activation(out=ei[:], in_=lb[:], func=AF.Exp,
                                             scale=c_m1), reads=[blb, b_cp], writes=[bei])
            ur, bur = W[5]
            t9, bt9 = W[9]
            p.op("v", lambda v: v.tensor_tensor(out=ur[:], in0=xr[:], in1=cosb[:],
                                                op=ALU.mult), reads=[bxr, bcos], writes=[bur])
            p.op("v", lambda v: v.tensor_tensor(out=t9[:], in0=xi[:], in1=sinb[:],
                                                op=ALU.mult), reads=[bxi, bsin], writes=[bt9])
            p.op("v", lambda v: v.tensor_tensor(out=ur[:], in0=ur[:], in1=t9[:],
                                                op=ALU.add), reads=[bt9, bur], writes=[bur])
            p.op("v", lambda v: v.tensor_tensor(out=ur[:], in0=ur[:], in1=ei[:],
                                                op=ALU.mult), reads=[bei, bur], writes=[bur])
            p.op("v", lambda v: v.tensor_tensor(out=t9[:], in0=xi[:], in1=cosb[:],
                                                op=ALU.mult), reads=[bxi, bcos], writes=[bt9])
            ui2, bui2 = W[1]
            p.op("v", lambda v: v.tensor_tensor(out=ui2[:], in0=xr[:], in1=sinb[:],
                                                op=ALU.mult), reads=[bxr, bsin], writes=[bui2])
            p.op("v", lambda v: v.tensor_tensor(out=t9[:], in0=t9[:], in1=ui2[:],
                                                op=ALU.subtract), reads=[bui2, bt9], writes=[bt9])
            p.op("v", lambda v: v.tensor_tensor(out=t9[:], in0=t9[:], in1=ei[:],
                                                op=ALU.mult), reads=[bei, bt9], writes=[bt9])
            ui, bui = t9, bt9
            ec, bec = W[2]
            p.op("v", lambda v: v.tensor_tensor(out=ec[:], in0=eb[:], in1=cosb[:],
                                                op=ALU.mult), reads=[beb, bcos], writes=[bec])
            es_, bes = W[3]
            p.op("v", lambda v: v.tensor_tensor(out=es_[:], in0=eb[:], in1=sinb[:],
                                                op=ALU.mult), reads=[beb, bsin], writes=[bes])
            p.dma("y", o_ec[ct * 128:(ct + 1) * 128, :], ec[:], reads=[bec], writes=[b_oec])
            p.dma("s", o_es[ct * 128:(ct + 1) * 128, :], es_[:], reads=[bes], writes=[b_oes])
            p.dma("y", o_ur[ct * 128:(ct + 1) * 128, :], ur[:], reads=[bur], writes=[b_our])
            p.dma("s", o_ui[ct * 128:(ct + 1) * 128, :], ui[:], reads=[bui], writes=[b_oui])
            sr0, bsr0 = W[4]
            si0, bsi0 = W[6]
            p.op("v", lambda v: v.tensor_tensor_scan(
                out=sr0[:], data0=maskb[:], data1=ur[:], initial=0.0,
                op0=ALU.mult, op1=ALU.add), reads=[b_maskb, bur], writes=[bsr0])
            p.op("v", lambda v: v.tensor_tensor_scan(
                out=si0[:], data0=maskb[:], data1=ui[:], initial=0.0,
                op0=ALU.mult, op1=ALU.add), reads=[b_maskb, bui], writes=[bsi0])
            for srcv, bsrc, dst, bdst in [
                    (ec, bec, ecE, b_ecE), (es_, bes, esE, b_esE),
                    (sr0, bsr0, srE, b_srE), (si0, bsi0, siE, b_siE)]:
                p.op("v", lambda v, srcv=srcv, dst=dst, ct=ct: v.tensor_copy(
                    out=dst[:].rearrange("p (b c) -> p b c", c=8)[:, :, ct],
                    in_=srcv[:].rearrange("p (b s) -> p b s", s=TB)[:, :, TB - 1]),
                    reads=[bsrc], writes=[bdst])
        p.op("v", lambda v: v.memset(carr[:, 0:8], 0.0), writes=[b_carr])
        p.op("v", lambda v: v.memset(cari[:, 0:8], 0.0), writes=[b_cari])
        p.op("v", lambda v: v.memset(rrp[:, 0:8], 1.0), writes=[b_rrp])
        p.op("v", lambda v: v.memset(rip[:, 0:8], 0.0), writes=[b_rip])
        for bk in range(NBLK):
            s0, s1 = slice(bk * 8, bk * 8 + 8), slice((bk + 1) * 8, (bk + 1) * 8 + 8)
            p.op("v", lambda v, s0=s0: v.tensor_tensor(
                out=ctmp[:, 0:8], in0=carr[:, s0], in1=srE[:, s0],
                op=ALU.add), reads=[b_carr, b_srE], writes=[b_ctmp])
            p.op("v", lambda v, s0=s0: v.tensor_tensor(
                out=ctmp[:, 8:16], in0=cari[:, s0], in1=siE[:, s0],
                op=ALU.add), reads=[b_cari, b_siE], writes=[b_ctmp])
            p.op("v", lambda v, s0=s0: v.tensor_tensor(
                out=ctmp[:, 16:24], in0=ecE[:, s0], in1=ctmp[:, 0:8],
                op=ALU.mult), reads=[b_ecE, b_ctmp], writes=[b_ctmp])
            p.op("v", lambda v, s0=s0: v.tensor_tensor(
                out=ctmp[:, 24:32], in0=esE[:, s0], in1=ctmp[:, 8:16],
                op=ALU.mult), reads=[b_esE, b_ctmp], writes=[b_ctmp])
            p.op("v", lambda v, s1=s1: v.tensor_tensor(
                out=carr[:, s1], in0=ctmp[:, 16:24], in1=ctmp[:, 24:32],
                op=ALU.subtract), reads=[b_ctmp], writes=[b_carr])
            p.op("v", lambda v, s0=s0: v.tensor_tensor(
                out=ctmp[:, 16:24], in0=esE[:, s0], in1=ctmp[:, 0:8],
                op=ALU.mult), reads=[b_esE, b_ctmp], writes=[b_ctmp])
            p.op("v", lambda v, s0=s0: v.tensor_tensor(
                out=ctmp[:, 24:32], in0=ecE[:, s0], in1=ctmp[:, 8:16],
                op=ALU.mult), reads=[b_ecE, b_ctmp], writes=[b_ctmp])
            p.op("v", lambda v, s1=s1: v.tensor_tensor(
                out=cari[:, s1], in0=ctmp[:, 16:24], in1=ctmp[:, 24:32],
                op=ALU.add), reads=[b_ctmp], writes=[b_cari])
            p.op("g", lambda g, s0=s0: g.tensor_tensor(
                out=ctmp[:, 32:40], in0=ecE[:, s0], in1=rrp[:, s0],
                op=ALU.mult), reads=[b_ecE, b_rrp], writes=[b_ctmp])
            p.op("g", lambda g, s0=s0: g.tensor_tensor(
                out=ctmp[:, 40:48], in0=esE[:, s0], in1=rip[:, s0],
                op=ALU.mult), reads=[b_esE, b_rip], writes=[b_ctmp])
            p.op("g", lambda g, s1=s1: g.tensor_tensor(
                out=rrp[:, s1], in0=ctmp[:, 32:40], in1=ctmp[:, 40:48],
                op=ALU.subtract), reads=[b_ctmp], writes=[b_rrp])
            p.op("g", lambda g, s0=s0: g.tensor_tensor(
                out=ctmp[:, 32:40], in0=esE[:, s0], in1=rrp[:, s0],
                op=ALU.mult), reads=[b_esE, b_rrp], writes=[b_ctmp])
            p.op("g", lambda g, s0=s0: g.tensor_tensor(
                out=ctmp[:, 40:48], in0=ecE[:, s0], in1=rip[:, s0],
                op=ALU.mult), reads=[b_ecE, b_rip], writes=[b_ctmp])
            p.op("g", lambda g, s1=s1: g.tensor_tensor(
                out=rip[:, s1], in0=ctmp[:, 32:40], in1=ctmp[:, 40:48],
                op=ALU.add), reads=[b_ctmp], writes=[b_rip])
        NB8 = NBLK * 8
        p.op("v", lambda v: v.tensor_copy(out=summ[:, 0:8], in_=rrp[:, NB8:NB8 + 8]),
             reads=[b_rrp], writes=[b_summ])
        p.op("v", lambda v: v.tensor_copy(out=summ[:, 8:16], in_=rip[:, NB8:NB8 + 8]),
             reads=[b_rip], writes=[b_summ])
        p.op("v", lambda v: v.tensor_copy(out=summ[:, 16:24], in_=carr[:, NB8:NB8 + 8]),
             reads=[b_carr], writes=[b_summ])
        p.op("v", lambda v: v.tensor_copy(out=summ[:, 24:32], in_=cari[:, NB8:NB8 + 8]),
             reads=[b_cari], writes=[b_summ])
        p.dma("y", o_sum[:, :], summ[:], reads=[b_summ], writes=[b_osum])
        p.barrier()
    p.barrier()




def build_rg1():
    nc = bass.Bass("TRN2", target_bir_lowering=False)
    es = ExitStack()
    dram = lambda n, sh, dt, k="ExternalInput": (nc.dram_tensor(n, sh, dt, kind=k).ap(),
                                                 BufT(sname="S" + n, waw=False))
    d_x, b_x = dram("x_f", [D, TH], F32)
    d_a0, b_a0 = dram("attn0", [D, T], F32)
    d_f0, b_f0 = dram("ffn0", [D, T], F32)
    d_win1, b_win1 = dram("w_in1", [D, D], F32)
    d_bin, b_bin = dram("b_in", [128, KD], F32)
    d_wg, b_wg = dram("wg", [2 * 8 * 256, 128], F32)
    d_bg, b_bg = dram("bg", [128, 16], F32)
    d_arp, b_arp = dram("arp", [128, 8], F32)
    d_aip, b_aip = dram("aip", [128, 8], F32)
    d_mask, b_mask = dram("maskrow", [1, T], F32)
    o_x1, b_x1 = dram("o_x1", [D, T], F32, "ExternalOutput")
    o_xn1, b_oxn1 = dram("o_xn1", [D, T], BF16, "ExternalOutput")
    o_ec, b_oec = dram("o_ec", [DH, T], F32, "ExternalOutput")
    o_es, b_oes = dram("o_es", [DH, T], F32, "ExternalOutput")
    o_ur, b_our = dram("o_ur", [DH, T], F32, "ExternalOutput")
    o_ui, b_oui = dram("o_ui", [DH, T], F32, "ExternalOutput")
    o_sum, b_osum = dram("o_sumry", [128, 32], F32, "ExternalOutput")
    Dio = {
        "x_f": (d_x, b_x),
        "attn0": (d_a0, b_a0),
        "ffn0": (d_f0, b_f0),
        "w_in1": (d_win1, b_win1),
        "b_in": (d_bin, b_bin),
        "wg": (d_wg, b_wg),
        "bg": (d_bg, b_bg),
        "arp": (d_arp, b_arp),
        "aip": (d_aip, b_aip),
        "maskrow": (d_mask, b_mask),
        "o_x1": (o_x1, b_x1),
        "o_xn1": (o_xn1, b_oxn1),
        "o_ec": (o_ec, b_oec),
        "o_es": (o_es, b_oes),
        "o_ur": (o_ur, b_our),
        "o_ui": (o_ui, b_oui),
        "o_sumry": (o_sum, b_osum),
    }
    p = Prog(nc, es)
    emit_rg1(nc, es, p, Dio)
    es.close()
    return nc


def emit_rg2(nc, es, p, Dio, pre="d2_"):
    d_ec, b_ec = Dio["ec"]
    d_es, b_es = Dio["es"]
    d_ur, b_ur = Dio["ur"]
    d_ui, b_ui = Dio["ui"]
    d_mask, b_mask = Dio["maskrow"]
    d_wo1, b_wo1 = Dio["w_out1"]
    d_bout, b_bout = Dio["b_out"]
    o_at1, b_at1 = Dio["o_attn1"]
    fused = "cx2g" in Dio
    if not fused:
        d_pf1r, b_pf1r = Dio["pf1r"]
        d_pf1i, b_pf1i = Dio["pf1i"]
        d_pf2r, b_pf2r = Dio["pf2r"]
        d_pf2i, b_pf2i = Dio["pf2i"]
        d_phr, b_phr = Dio["predHr"]
        d_phi, b_phi = Dio["predHi"]
    else:
        d_cx2g, b_cx2g = Dio["cx2g"]
        d_sel1, b_sel1 = Dio["sel1f"]
        d_sel2, b_sel2 = Dio["sel2f"]
        d_selh, b_selh = Dio["selhf"]
        d_idf1, b_idf1 = Dio["idf1"]
        d_idf2, b_idf2 = Dio["idf2"]
    SB = lambda n, sh, dt: sbuf(nc, es, pre + n, sh, dt, sname="L" + n)

    htot, b_htot = SB("htot", [128, KD * T], BF16)
    with ExitStack() as ph:
        SBp = lambda n, sh, dt: sbuf(nc, ph, pre + n, sh, dt, sname="L" + n)
        maskb, b_maskb = SBp("maskb", [128, T], F32)
        p.dma("y", maskb[:], d_mask.broadcast_to([128, T]), reads=[b_mask],
              writes=[b_maskb])
        pfs = {}
        if not fused:
            for nm, dd, bb in [("f1r", d_pf1r, b_pf1r), ("f1i", d_pf1i, b_pf1i),
                               ("f2r", d_pf2r, b_pf2r), ("f2i", d_pf2i, b_pf2i),
                               ("hr", d_phr, b_phr), ("hi", d_phi, b_phi)]:
                tile, bt = SBp("pf_" + nm, [128, 64], F32)
                p.dma("y", tile[:], dd, reads=[bb], writes=[bt])
                pfs[nm] = (tile, bt)
        else:
            smA, b_smA = SBp("smA", [128, 8 * 32], F32)
            p.dma("y", smA[:].rearrange("p (j c) -> p j c", c=32),
                  d_cx2g.rearrange("(j p) c -> p j c", p=128),
                  reads=[b_cx2g], writes=[b_smA])
            selbc = {}
            for nm, dd, bb in [("s1", d_sel1, b_sel1), ("s2", d_sel2, b_sel2),
                               ("sh", d_selh, b_selh), ("i1", d_idf1, b_idf1),
                               ("i2", d_idf2, b_idf2)]:
                tile, bt = SBp("selb_" + nm, [128, 64], F32)
                p.dma("y", tile[:], dd.broadcast_to([128, 64]), reads=[bb], writes=[bt])
                selbc[nm] = (tile, bt)
            scr3, b_scr3 = SBp("scr3", [128, 64], F32)
            for nm, sel, coff, fill in [("f1r", "s1", 0, "i1"), ("f1i", "s1", 8, None),
                                        ("f2r", "s2", 0, "i2"), ("f2i", "s2", 8, None),
                                        ("hr", "sh", 16, None), ("hi", "sh", 24, None)]:
                tile, bt = SBp("pf_" + nm, [128, 64], F32)
                if fill is None:
                    p.op("v", lambda v, tile=tile: v.memset(tile[:], 0.0), writes=[bt])
                else:
                    ft, fbt = selbc[fill]
                    p.op("v", lambda v, tile=tile, ft=ft: v.tensor_copy(
                        out=tile[:], in_=ft[:]), reads=[fbt], writes=[bt])
                st_, sbt = selbc[sel]
                for k in range(8):
                    src_v = smA[:, k * 32 + coff:k * 32 + coff + 8].rearrange(
                        "p (c one) -> p c one", one=1).broadcast_to([128, 8, 8])
                    sel_v = st_[:, k * 8:(k + 1) * 8].rearrange(
                        "p (one j) -> p one j", one=1).broadcast_to([128, 8, 8])
                    p.op("v", lambda v, src_v=src_v, sel_v=sel_v: v.tensor_tensor(
                        out=scr3[:].rearrange("p (c j) -> p c j", j=8),
                        in0=src_v, in1=sel_v, op=ALU.mult),
                        reads=[b_smA, sbt], writes=[b_scr3])
                    p.op("v", lambda v, tile=tile: v.tensor_tensor(
                        out=tile[:], in0=tile[:], in1=scr3[:], op=ALU.add),
                        reads=[b_scr3, bt], writes=[bt])
                pfs[nm] = (tile, bt)
        W = [SBp("wd%d" % i, [128, T], F32) for i in range(7)]
        sm = [SBp("sm%d" % i, [128, 8], F32) for i in range(4)]
        carr, b_carr = SBp("carr", [128, NBLK * 8], F32)
        cari, b_cari = SBp("cari", [128, NBLK * 8], F32)
        ecE, b_ecE = SBp("ecE", [128, NBLK * 8], F32)
        esE, b_esE = SBp("esE", [128, NBLK * 8], F32)
        srE, b_srE = SBp("srE", [128, NBLK * 8], F32)
        siE, b_siE = SBp("siE", [128, NBLK * 8], F32)
        ctmp, b_ctmp = SBp("ctmp", [128, 32], F32)
        for ct in range(8):
            ec, bec = W[0]
            es_, bes = W[1]
            ur, bur = W[2]
            ui, bui = W[3]
            p.dma("y", ec[:], d_ec[ct * 128:(ct + 1) * 128, :], reads=[b_ec], writes=[bec])
            p.dma("s", es_[:], d_es[ct * 128:(ct + 1) * 128, :], reads=[b_es], writes=[bes])
            p.dma("y", ur[:], d_ur[ct * 128:(ct + 1) * 128, :], reads=[b_ur], writes=[bur])
            p.dma("s", ui[:], d_ui[ct * 128:(ct + 1) * 128, :], reads=[b_ui], writes=[bui])
            cs = slice(ct * 8, (ct + 1) * 8)
            w0, b_w0 = sm[0]
            w1, b_w1 = sm[1]
            t0, b_t0 = sm[2]
            t1, b_t1 = sm[3]
            f1r, bf1r = pfs["f1r"]; f1i, bf1i = pfs["f1i"]
            f2r, bf2r = pfs["f2r"]; f2i, bf2i = pfs["f2i"]
            phr, bphr = pfs["hr"]; phi, bphi = pfs["hi"]
            p.op("v", lambda v, cs=cs: v.tensor_tensor(out=t0[:], in0=f1r[:, cs], in1=f2r[:, cs],
                                                op=ALU.mult), reads=[bf1r, bf2r], writes=[b_t0])
            p.op("v", lambda v, cs=cs: v.tensor_tensor(out=t1[:], in0=f1i[:, cs], in1=f2i[:, cs],
                                                op=ALU.mult), reads=[bf1i, bf2i], writes=[b_t1])
            p.op("v", lambda v: v.tensor_tensor(out=w0[:], in0=t0[:], in1=t1[:],
                                                op=ALU.subtract), reads=[b_t0, b_t1], writes=[b_w0])
            p.op("v", lambda v, cs=cs: v.tensor_tensor(out=t0[:], in0=f1r[:, cs], in1=f2i[:, cs],
                                                op=ALU.mult), reads=[bf1r, bf2i], writes=[b_t0])
            p.op("v", lambda v, cs=cs: v.tensor_tensor(out=t1[:], in0=f1i[:, cs], in1=f2r[:, cs],
                                                op=ALU.mult), reads=[bf1i, bf2r], writes=[b_t1])
            p.op("v", lambda v: v.tensor_tensor(out=w1[:], in0=t0[:], in1=t1[:],
                                                op=ALU.add), reads=[b_t0, b_t1], writes=[b_w1])
            p.op("v", lambda v, cs=cs: v.tensor_tensor(out=t0[:], in0=w0[:], in1=phr[:, cs],
                                                op=ALU.mult), reads=[b_w0, bphr], writes=[b_t0])
            p.op("v", lambda v, cs=cs: v.tensor_tensor(out=t1[:], in0=w1[:], in1=phi[:, cs],
                                                op=ALU.mult), reads=[b_w1, bphi], writes=[b_t1])
            p.op("v", lambda v: v.tensor_tensor(out=t0[:], in0=t0[:], in1=t1[:],
                                                op=ALU.subtract), reads=[b_t1, b_t0], writes=[b_t0])
            p.op("v", lambda v, ct=ct: v.tensor_reduce(
                out=carr[:, ct:ct + 1], in_=t0[:], axis=mybir.AxisListType.X, op=ALU.add),
                reads=[b_t0], writes=[b_carr])
            p.op("v", lambda v, cs=cs: v.tensor_tensor(out=t0[:], in0=w0[:], in1=phi[:, cs],
                                                op=ALU.mult), reads=[b_w0, bphi], writes=[b_t0])
            p.op("v", lambda v, cs=cs: v.tensor_tensor(out=t1[:], in0=w1[:], in1=phr[:, cs],
                                                op=ALU.mult), reads=[b_w1, bphr], writes=[b_t1])
            p.op("v", lambda v: v.tensor_tensor(out=t0[:], in0=t0[:], in1=t1[:],
                                                op=ALU.add), reads=[b_t1, b_t0], writes=[b_t0])
            p.op("v", lambda v, ct=ct: v.tensor_reduce(
                out=cari[:, ct:ct + 1], in_=t0[:], axis=mybir.AxisListType.X, op=ALU.add),
                reads=[b_t0], writes=[b_cari])
            sr0, bsr0 = W[4]
            si0, bsi0 = W[5]
            p.op("v", lambda v, ur=ur, sr0=sr0: v.tensor_tensor_scan(
                out=sr0[:], data0=maskb[:], data1=ur[:], initial=0.0,
                op0=ALU.mult, op1=ALU.add), reads=[b_maskb, bur], writes=[bsr0])
            p.op("v", lambda v, ui=ui, si0=si0: v.tensor_tensor_scan(
                out=si0[:], data0=maskb[:], data1=ui[:], initial=0.0,
                op0=ALU.mult, op1=ALU.add), reads=[b_maskb, bui], writes=[bsi0])
            for srcv, bsrc, dst, bdst in [
                    (ec, bec, ecE, b_ecE), (es_, bes, esE, b_esE),
                    (sr0, bsr0, srE, b_srE), (si0, bsi0, siE, b_siE)]:
                p.op("v", lambda v, srcv=srcv, dst=dst, ct=ct: v.tensor_copy(
                    out=dst[:].rearrange("p (b c) -> p b c", c=8)[:, :, ct],
                    in_=srcv[:].rearrange("p (b s) -> p b s", s=TB)[:, :, TB - 1]),
                    reads=[bsrc], writes=[bdst])
        for bk in range(NBLK - 1):
            s0, s1 = slice(bk * 8, bk * 8 + 8), slice((bk + 1) * 8, (bk + 1) * 8 + 8)
            p.op("v", lambda v, s0=s0: v.tensor_tensor(
                out=ctmp[:, 0:8], in0=carr[:, s0], in1=srE[:, s0],
                op=ALU.add), reads=[b_carr, b_srE], writes=[b_ctmp])
            p.op("v", lambda v, s0=s0: v.tensor_tensor(
                out=ctmp[:, 8:16], in0=cari[:, s0], in1=siE[:, s0],
                op=ALU.add), reads=[b_cari, b_siE], writes=[b_ctmp])
            p.op("v", lambda v, s0=s0: v.tensor_tensor(
                out=ctmp[:, 16:24], in0=ecE[:, s0], in1=ctmp[:, 0:8],
                op=ALU.mult), reads=[b_ecE, b_ctmp], writes=[b_ctmp])
            p.op("v", lambda v, s0=s0: v.tensor_tensor(
                out=ctmp[:, 24:32], in0=esE[:, s0], in1=ctmp[:, 8:16],
                op=ALU.mult), reads=[b_esE, b_ctmp], writes=[b_ctmp])
            p.op("v", lambda v, s1=s1: v.tensor_tensor(
                out=carr[:, s1], in0=ctmp[:, 16:24], in1=ctmp[:, 24:32],
                op=ALU.subtract), reads=[b_ctmp], writes=[b_carr])
            p.op("v", lambda v, s0=s0: v.tensor_tensor(
                out=ctmp[:, 16:24], in0=esE[:, s0], in1=ctmp[:, 0:8],
                op=ALU.mult), reads=[b_esE, b_ctmp], writes=[b_ctmp])
            p.op("v", lambda v, s0=s0: v.tensor_tensor(
                out=ctmp[:, 24:32], in0=ecE[:, s0], in1=ctmp[:, 8:16],
                op=ALU.mult), reads=[b_ecE, b_ctmp], writes=[b_ctmp])
            p.op("v", lambda v, s1=s1: v.tensor_tensor(
                out=cari[:, s1], in0=ctmp[:, 16:24], in1=ctmp[:, 24:32],
                op=ALU.add), reads=[b_ctmp], writes=[b_cari])
        for ct in range(8):
            ec, bec = W[0]
            es_, bes = W[1]
            ur, bur = W[2]
            ui, bui = W[3]
            p.dma("y", ec[:], d_ec[ct * 128:(ct + 1) * 128, :], reads=[b_ec], writes=[bec])
            p.dma("s", es_[:], d_es[ct * 128:(ct + 1) * 128, :], reads=[b_es], writes=[bes])
            p.dma("y", ur[:], d_ur[ct * 128:(ct + 1) * 128, :], reads=[b_ur], writes=[bur])
            p.dma("s", ui[:], d_ui[ct * 128:(ct + 1) * 128, :], reads=[b_ui], writes=[bui])
            urv = ur[:].rearrange("p (b s) -> p b s", s=TB)
            uiv = ui[:].rearrange("p (b s) -> p b s", s=TB)
            cview = carr[:].rearrange("p (b c) -> p b c", c=8)[:, :, ct]
            ciview = cari[:].rearrange("p (b c) -> p b c", c=8)[:, :, ct]
            p.op("v", lambda v, urv=urv, cview=cview: v.tensor_tensor(
                out=urv[:, :, 0:1].rearrange("p b s -> p (b s)"),
                in0=urv[:, :, 0:1].rearrange("p b s -> p (b s)"), in1=cview,
                op=ALU.add), reads=[b_carr, bur], writes=[bur])
            p.op("v", lambda v, uiv=uiv, ciview=ciview: v.tensor_tensor(
                out=uiv[:, :, 0:1].rearrange("p b s -> p (b s)"),
                in0=uiv[:, :, 0:1].rearrange("p b s -> p (b s)"), in1=ciview,
                op=ALU.add), reads=[b_cari, bui], writes=[bui])
            sr, bsr = W[4]
            si, bsi = W[5]
            p.op("v", lambda v, ur=ur, sr=sr: v.tensor_tensor_scan(
                out=sr[:], data0=maskb[:], data1=ur[:], initial=0.0,
                op0=ALU.mult, op1=ALU.add), reads=[b_maskb, bur], writes=[bsr])
            p.op("v", lambda v, ui=ui, si=si: v.tensor_tensor_scan(
                out=si[:], data0=maskb[:], data1=ui[:], initial=0.0,
                op0=ALU.mult, op1=ALU.add), reads=[b_maskb, bui], writes=[bsi])
            t5, bt5 = W[6]
            t9, bt9 = W[2]
            p.op("v", lambda v: v.tensor_tensor(out=t5[:], in0=ec[:], in1=sr[:],
                                                op=ALU.mult), reads=[bec, bsr], writes=[bt5])
            p.op("v", lambda v: v.tensor_tensor(out=t9[:], in0=es_[:], in1=si[:],
                                                op=ALU.mult), reads=[bes, bsi], writes=[bt9])
            p.op("v", lambda v, ct=ct: v.tensor_tensor(
                out=htot[:, ct * T:(ct + 1) * T], in0=t5[:], in1=t9[:],
                op=ALU.subtract), reads=[bt5, bt9], writes=[b_htot])
            p.op("v", lambda v: v.tensor_tensor(out=t5[:], in0=es_[:], in1=sr[:],
                                                op=ALU.mult), reads=[bes, bsr], writes=[bt5])
            p.op("v", lambda v: v.tensor_tensor(out=t9[:], in0=ec[:], in1=si[:],
                                                op=ALU.mult), reads=[bec, bsi], writes=[bt9])
            p.op("v", lambda v, ct=ct: v.tensor_tensor(
                out=htot[:, (8 + ct) * T:(9 + ct) * T], in0=t5[:], in1=t9[:],
                op=ALU.add), reads=[bt5, bt9], writes=[b_htot])
        p.barrier()

    with ExitStack() as ph:
        SBp = lambda n, sh, dt: sbuf(nc, ph, pre + n, sh, dt, sname="L" + n)
        PSp = lambda n, sh: psum(nc, ph, pre + n, sh, sname="L" + n)
        wv = [SBp("wv%d" % i, [128, KD * 128], BF16) for i in range(2)]
        pso = [PSp("pso%d" % n, [128, 512]) for n in range(4)]
        oat = SBp("oat", [128, T], F32)
        boc, b_boc = SBp("boc", [128, KD], F32)
        p.dma("y", boc[:], d_bout, reads=[b_bout], writes=[b_boc])
        for m in range(KD):
            wb, bwb = wv[m % 2]
            p.dma("g", wb[:].rearrange("p (k c) -> p k c", k=KD),
                  d_wo1[:, m * 128:(m + 1) * 128].rearrange("(k p) c -> p k c", p=128),
                  reads=[b_wo1], writes=[bwb])
            oa, boa = oat
            for k in range(KD):
                for n in range(4):
                    ps, bps = pso[n]
                    p.op("t", lambda t, ps=ps, k=k, n=n: t.matmul(
                        ps[:, :], wb[:, k * 128:(k + 1) * 128],
                        htot[:, k * T + n * 512:k * T + (n + 1) * 512],
                        start=(k == 0), stop=(k == KD - 1)),
                        reads=[bwb, b_htot], writes=[bps])
            for n in range(4):
                ps, bps = pso[n]
                p.op("s", lambda s, ps=ps, n=n, m=m: s.activation(
                    out=oa[:, n * 512:(n + 1) * 512], in_=ps[:, :], func=AF.Identity,
                    bias=boc[:, m:m + 1]), reads=[bps, b_boc], writes=[boa])
            p.dma("y", o_at1[m * 128:(m + 1) * 128, :], oa[:], reads=[boa],
                  writes=[b_at1])
        p.barrier()
    p.barrier()




def build_rg2():
    nc = bass.Bass("TRN2", target_bir_lowering=False)
    es = ExitStack()
    dram = lambda n, sh, dt, k="ExternalInput": (nc.dram_tensor(n, sh, dt, kind=k).ap(),
                                                 BufT(sname="S" + n, waw=False))
    d_ec, b_ec = dram("ec", [DH, T], F32)
    d_es, b_es = dram("es", [DH, T], F32)
    d_ur, b_ur = dram("ur", [DH, T], F32)
    d_ui, b_ui = dram("ui", [DH, T], F32)
    d_pf1r, b_pf1r = dram("pf1r", [128, 64], F32)
    d_pf1i, b_pf1i = dram("pf1i", [128, 64], F32)
    d_pf2r, b_pf2r = dram("pf2r", [128, 64], F32)
    d_pf2i, b_pf2i = dram("pf2i", [128, 64], F32)
    d_phr, b_phr = dram("predHr", [128, 64], F32)
    d_phi, b_phi = dram("predHi", [128, 64], F32)
    d_mask, b_mask = dram("maskrow", [1, T], F32)
    d_wo1, b_wo1 = dram("w_out1", [D, D], F32)
    d_bout, b_bout = dram("b_out", [128, KD], F32)
    o_at1, b_at1 = dram("o_attn1", [D, T], F32, "ExternalOutput")
    Dio = {
        "ec": (d_ec, b_ec),
        "es": (d_es, b_es),
        "ur": (d_ur, b_ur),
        "ui": (d_ui, b_ui),
        "pf1r": (d_pf1r, b_pf1r),
        "pf1i": (d_pf1i, b_pf1i),
        "pf2r": (d_pf2r, b_pf2r),
        "pf2i": (d_pf2i, b_pf2i),
        "predHr": (d_phr, b_phr),
        "predHi": (d_phi, b_phi),
        "maskrow": (d_mask, b_mask),
        "w_out1": (d_wo1, b_wo1),
        "b_out": (d_bout, b_bout),
        "o_attn1": (o_at1, b_at1),
    }
    p = Prog(nc, es)
    emit_rg2(nc, es, p, Dio)
    es.close()
    return nc


def emit_ffn1(nc, es, p, Dio, pre="e_"):
    d_xn1, b_xn1h = Dio["xn1"]
    d_gwT, b_gwT = Dio["gwT"]
    d_uwT, b_uwT = Dio["uwT"]
    d_dwT, b_dwT = Dio["dwT"]
    d_gs, b_gs = Dio["gsc"]
    d_us, b_us = Dio["usc"]
    d_ds, b_ds = Dio["dsc"]
    d_x1, b_x1 = Dio["x1"]
    d_at1, b_at1 = Dio["attn1"]
    o_out, b_out = Dio["o_out"]
    hid_h, b_hid_h = Dio["hid1_hbm"]
    SB = lambda n, sh, dt: sbuf(nc, es, pre + n, sh, dt, sname="L" + n)
    MH = HF // 128

    with ExitStack() as ph:
        SBp = lambda n, sh, dt: sbuf(nc, ph, pre + n, sh, dt, sname="L" + n)
        PSp = lambda n, sh: psum(nc, ph, pre + n, sh, sname="L" + n)
        xnr = SBp("xnr", [128, KD * T], BF16)
        b_xnr = xnr[1]
        p.dma("y", xnr[0][:].rearrange("p (k t) -> p k t", t=T),
              d_xn1.rearrange("(k p) t -> p k t", p=128), reads=[b_xn1h], writes=[b_xnr])
        gq = [SBp("gq%d" % i, [128, KD * 128], BF16) for i in range(2)]
        uq = [SBp("uq%d" % i, [128, KD * 128], BF16) for i in range(2)]
        gsc, b_gsc = SBp("gsc_t", [128, MH], F32)
        usc, b_usc = SBp("usc_t", [128, MH], F32)
        p.dma("y", gsc[:], d_gs, reads=[b_gs], writes=[b_gsc])
        p.dma("y", usc[:], d_us, reads=[b_us], writes=[b_usc])
        ps1 = [PSp("ps1%d" % n, [128, 512]) for n in range(4)]
        ps2 = [PSp("ps2%d" % n, [128, 512]) for n in range(4)]
        sgt = SBp("sgt", [128, T], BF16)
        upt = SBp("upt", [128, T], BF16)
        hidt = SBp("hidt", [128, T], BF16)
        for m in range(MH):
            ga, bga = gq[m % 2]
            ua, bua = uq[m % 2]
            p.dma("g", ga[:].rearrange("p (k c) -> p k c", k=KD),
                  d_gwT[:, m * 128:(m + 1) * 128].rearrange("(k p) c -> p k c", p=128),
                  reads=[b_gwT], writes=[bga])
            p.dma("g", ua[:].rearrange("p (k c) -> p k c", k=KD),
                  d_uwT[:, m * 128:(m + 1) * 128].rearrange("(k p) c -> p k c", p=128),
                  reads=[b_uwT], writes=[bua])
            sg, bsg = sgt
            up, bup = upt
            hd, bhd = hidt
            for k in range(KD):
                for n in range(4):
                    pa, bpa = ps1[n]
                    p.op("t", lambda t, pa=pa, ga=ga, k=k, n=n: t.matmul(
                        pa[:, :], ga[:, k * 128:(k + 1) * 128],
                        xnr[0][:, k * T + n * 512:k * T + (n + 1) * 512],
                        start=(k == 0), stop=(k == KD - 1)),
                        reads=[bga, b_xnr], writes=[bpa])
                for n in range(4):
                    pb, bpb = ps2[n]
                    p.op("t", lambda t, pb=pb, ua=ua, k=k, n=n: t.matmul(
                        pb[:, :], ua[:, k * 128:(k + 1) * 128],
                        xnr[0][:, k * T + n * 512:k * T + (n + 1) * 512],
                        start=(k == 0), stop=(k == KD - 1)),
                        reads=[bua, b_xnr], writes=[bpb])
            for n in range(4):
                pa, bpa = ps1[n]
                pb, bpb = ps2[n]
                p.op("s", lambda s, pa=pa, n=n, m=m: s.activation(
                    out=sg[:, n * 512:(n + 1) * 512], in_=pa[:, :], func=AF.Silu,
                    scale=gsc[:, m:m + 1]), reads=[bpa, b_gsc], writes=[bsg])
                p.op("s", lambda s, pb=pb, n=n, m=m: s.activation(
                    out=up[:, n * 512:(n + 1) * 512], in_=pb[:, :], func=AF.Identity,
                    scale=usc[:, m:m + 1]), reads=[bpb, b_usc], writes=[bup])
            p.op("v", lambda v: v.tensor_tensor(out=hd[:], in0=sg[:], in1=up[:],
                                                op=ALU.mult), reads=[bsg, bup],
                 writes=[bhd])
            p.dma("y", hid_h[m * 128:(m + 1) * 128, :], hd[:], reads=[bhd],
                  writes=[b_hid_h])
        p.barrier()

    with ExitStack() as ph:
        SBp = lambda n, sh, dt: sbuf(nc, ph, pre + n, sh, dt, sname="L" + n)
        PSp = lambda n, sh: psum(nc, ph, pre + n, sh, sname="L" + n)
        hfr = SBp("hfr", [128, MH * 1024], BF16)
        dq = [SBp("dq%d" % i, [128, MH * 128], BF16) for i in range(2)]
        dsc, b_dsc = SBp("dsc_t", [128, KD], F32)
        p.dma("y", dsc[:], d_ds, reads=[b_ds], writes=[b_dsc])
        ps3 = [PSp("ps3%d" % n, [128, 512]) for n in range(4)]
        outt = SBp("outt", [128, 1024], F32)
        rest = SBp("rest", [128, 1024], F32)
        for th in range(2):
            toff = th * 1024
            bh = hfr[1]
            p.dma("y", hfr[0][:].rearrange("p (k t) -> p k t", t=1024),
                  hid_h[:, toff:toff + 1024].rearrange("(k p) t -> p k t", p=128),
                  reads=[b_hid_h], writes=[bh])
            for m in range(KD):
                da, bda = dq[m % 2]
                p.dma("g", da[:].rearrange("p (k c) -> p k c", k=MH),
                      d_dwT[:, m * 128:(m + 1) * 128].rearrange("(k p) c -> p k c", p=128),
                      reads=[b_dwT], writes=[bda])
                ot, bot = outt
                rs, brs = rest
                for k in range(MH):
                    for n in range(2):
                        ps, bps = ps3[(th * 2 + n) % 4]
                        p.op("t", lambda t, ps=ps, da=da, k=k, n=n: t.matmul(
                            ps[:, :], da[:, k * 128:(k + 1) * 128],
                            hfr[0][:, k * 1024 + n * 512:k * 1024 + (n + 1) * 512],
                            start=(k == 0), stop=(k == MH - 1)),
                            reads=[bda, bh], writes=[bps])
                for n in range(2):
                    ps, bps = ps3[(th * 2 + n) % 4]
                    p.op("s", lambda s, ps=ps, n=n, m=m: s.activation(
                        out=ot[:, n * 512:(n + 1) * 512], in_=ps[:, :],
                        func=AF.Identity, scale=dsc[:, m:m + 1]),
                        reads=[bps, b_dsc], writes=[bot])
                p.dma("y", rs[:], d_x1[m * 128:(m + 1) * 128, toff:toff + 1024],
                      reads=[b_x1], writes=[brs])
                p.op("v", lambda v: v.tensor_tensor(out=ot[:, 0:1024], in0=ot[:, 0:1024],
                                                    in1=rs[:], op=ALU.add),
                     reads=[brs, bot], writes=[bot])
                p.dma("y", rs[:], d_at1[m * 128:(m + 1) * 128, toff:toff + 1024],
                      reads=[b_at1], writes=[brs])
                p.op("v", lambda v: v.tensor_tensor(out=ot[:, 0:1024], in0=ot[:, 0:1024],
                                                    in1=rs[:], op=ALU.add),
                     reads=[brs, bot], writes=[bot])
                p.dma("y", o_out[m * 128:(m + 1) * 128, toff:toff + 1024],
                      ot[:, 0:1024], reads=[bot], writes=[b_out])
        p.barrier()
    p.barrier()


def build_fused():
    nc = bass.Bass("TRN2", target_bir_lowering=False)
    es = ExitStack()
    dram = lambda n, sh, dt, k="ExternalInput": (nc.dram_tensor(n, sh, dt, kind=k).ap(),
                                                 BufT(sname="S" + n, waw=False))
    I = {}
    for n, sh, dt in [
        ("x_f", [D, TH], F32), ("hmaskc", [128, 1], F32), ("w_inT", [D, 2 * DI], F32),
        ("w_xT", [DI, 160], F32), ("w_dtT", [DTR, DI], F32), ("conv_w", [DI, DCONV], F32),
        ("conv_b", [DI, 1], F32), ("b_dt", [DI, 1], F32), ("alog", [DI, NST], F32),
        ("dskip", [128, MI], F32), ("w_outT", [DI, D], F32),
        ("w1T", [D, HF], F32), ("w2T", [D, HF], F32), ("w3T", [HF, D], F32),
        ("w_in1", [D, D], F32), ("b_in", [128, KD], F32),
        ("wg", [2 * 8 * 256, 128], F32), ("bg", [128, 16], F32),
        ("arp", [128, 8], F32), ("aip", [128, 8], F32), ("maskrow", [1, T], F32),
        ("m2T", [8, 8], F32), ("mv", [1, 8], F32),
        ("w_out1", [D, D], F32), ("b_out", [128, KD], F32),
        ("sel1f", [1, 64], F32), ("sel2f", [1, 64], F32), ("selhf", [1, 64], F32),
        ("idf1", [1, 64], F32), ("idf2", [1, 64], F32),
        ("gwT", [D, HF], F32), ("uwT", [D, HF], F32), ("dwT", [HF, D], F32),
        ("gsc", [128, HF // 128], F32), ("usc", [128, HF // 128], F32),
        ("dsc", [128, KD], F32), ("identb", [128, 128], BF16),
    ]:
        I[n] = dram(n, sh, dt)
    for n, sh, dt in [
        ("xn_h", [D, T], BF16), ("delta", [DI, T], F32), ("du", [DI, T], BF16),
        ("u", [DI, T], F32), ("sz", [DI, T], F32), ("bc", [32, T], BF16),
        ("ub_hbm", [DI, T], BF16), ("y1", [DI, T], F32),
        ("cx1_in", [128, 544], F32), ("cx1_out", [8 * 128, 544], F32),
        ("attn0", [D, T], F32), ("ffn0", [D, T], F32), ("hid_hbm", [HF, T], BF16),
        ("ygb_hbm", [DI, T], BF16), ("x1", [D, T], F32), ("xn1", [D, T], BF16),
        ("ec_h", [DH, T], F32), ("es_h", [DH, T], F32), ("ur_h", [DH, T], F32),
        ("ui_h", [DH, T], F32), ("cx2_in", [128, 32], F32),
        ("cx2_out", [8 * 128, 32], F32), ("attn1", [D, T], F32),
        ("hid1_hbm", [HF, T], BF16),
    ]:
        I[n] = dram(n, sh, dt, "Internal")
    I["o_out"] = dram("o_out", [D, T], F32, "ExternalOutput")

    p = Prog(nc, es)
    sl = lambda name, s0, s1: (I[name][0][:, s0:s1], I[name][1])

    with ExitStack() as est, nc.named_scope("s_a"):
        emit_l0pre(nc, est, p, dict(
            x_f=I["x_f"], hmaskc=I["hmaskc"], w_inT=I["w_inT"], w_xT=I["w_xT"],
            w_dtT=I["w_dtT"], conv_w=I["conv_w"], conv_b=I["conv_b"], b_dt=I["b_dt"],
            o_xn=I["xn_h"], o_delta=I["delta"], o_du=I["du"], o_u=I["u"],
            o_sz=I["sz"], o_bc=I["bc"], o_dsum=sl("cx1_in", 512, 544),
            ub_hbm=I["ub_hbm"]))
    with ExitStack() as est, nc.named_scope("s_b1"):
        emit_l0scan1(nc, est, p, dict(
            delta=I["delta"], du=I["du"], bc=I["bc"], alog=I["alog"],
            identb=I["identb"], o_y1=I["y1"], o_sc=sl("cx1_in", 0, 512)))
    p.op("g", lambda g: g.collective_compute(
        "AllGather", ALU.bypass, replica_groups=[list(range(NCORES))],
        ins=[I["cx1_in"][0].opt()], outs=[I["cx1_out"][0].opt()]),
        reads=[I["cx1_in"][1]], writes=[I["cx1_out"][1]], damt=None, sem=p.dsem("cc1x"))
    p.barrier()
    with ExitStack() as est, nc.named_scope("s_b2"):
        emit_l0scan2(nc, est, p, dict(
            delta=I["delta"], bc=I["bc"], alog=I["alog"], y1=I["y1"], u=I["u"],
            sz=I["sz"], dskip=I["dskip"], m2T=I["m2T"], w_outT=I["w_outT"],
            identb=I["identb"],
            o_attn0=I["attn0"], ygb_hbm=I["ygb_hbm"], cx1g=I["cx1_out"],
            mv=I["mv"]))
    with ExitStack() as est, nc.named_scope("s_c"):
        emit_ffn0(nc, est, p, dict(
            xn=I["xn_h"], w1T=I["w1T"], w2T=I["w2T"], w3T=I["w3T"],
            o_ffn0=I["ffn0"], hid_hbm=I["hid_hbm"]))
    with ExitStack() as est, nc.named_scope("s_d1"):
        emit_rg1(nc, est, p, dict(
            x_f=I["x_f"], attn0=I["attn0"], ffn0=I["ffn0"], w_in1=I["w_in1"],
            b_in=I["b_in"], wg=I["wg"], bg=I["bg"], arp=I["arp"], aip=I["aip"],
            maskrow=I["maskrow"], o_x1=I["x1"], o_xn1=I["xn1"], o_ec=I["ec_h"],
            o_es=I["es_h"], o_ur=I["ur_h"], o_ui=I["ui_h"], o_sumry=I["cx2_in"]))
    p.op("g", lambda g: g.collective_compute(
        "AllGather", ALU.bypass, replica_groups=[list(range(NCORES))],
        ins=[I["cx2_in"][0].opt()], outs=[I["cx2_out"][0].opt()]),
        reads=[I["cx2_in"][1]], writes=[I["cx2_out"][1]], damt=None, sem=p.dsem("cc2x"))
    p.barrier()
    with ExitStack() as est, nc.named_scope("s_d2"):
        emit_rg2(nc, est, p, dict(
            ec=I["ec_h"], es=I["es_h"], ur=I["ur_h"], ui=I["ui_h"],
            maskrow=I["maskrow"], w_out1=I["w_out1"], b_out=I["b_out"],
            o_attn1=I["attn1"], cx2g=I["cx2_out"], sel1f=I["sel1f"],
            sel2f=I["sel2f"], selhf=I["selhf"], idf1=I["idf1"], idf2=I["idf2"]))
    with ExitStack() as est, nc.named_scope("s_e"):
        emit_ffn1(nc, est, p, dict(
            xn1=I["xn1"], gwT=I["gwT"], uwT=I["uwT"], dwT=I["dwT"], gsc=I["gsc"],
            usc=I["usc"], dsc=I["dsc"], x1=I["x1"], attn1=I["attn1"],
            o_out=I["o_out"], hid1_hbm=I["hid1_hbm"]))
    p.barrier()
    es.close()
    return nc




def build_ffn1():
    nc = bass.Bass("TRN2", target_bir_lowering=False)
    es = ExitStack()
    dram = lambda n, sh, dt, k="ExternalInput": (nc.dram_tensor(n, sh, dt, kind=k).ap(),
                                                 BufT(sname="S" + n, waw=False))
    d_xn1, b_xn1h = dram("xn1", [D, T], BF16)
    d_gwT, b_gwT = dram("gwT", [D, HF], F32)
    d_uwT, b_uwT = dram("uwT", [D, HF], F32)
    d_dwT, b_dwT = dram("dwT", [HF, D], F32)
    d_gs, b_gs = dram("gsc", [128, HF // 128], F32)
    d_us, b_us = dram("usc", [128, HF // 128], F32)
    d_ds, b_ds = dram("dsc", [128, KD], F32)
    d_x1, b_x1 = dram("x1", [D, T], F32)
    d_at1, b_at1 = dram("attn1", [D, T], F32)
    o_out, b_out = dram("o_out", [D, T], F32, "ExternalOutput")
    hid_h, b_hid_h = dram("hid1_hbm", [HF, T], BF16, "Internal")
    Dio = {
        "xn1": (d_xn1, b_xn1h),
        "gwT": (d_gwT, b_gwT),
        "uwT": (d_uwT, b_uwT),
        "dwT": (d_dwT, b_dwT),
        "gsc": (d_gs, b_gs),
        "usc": (d_us, b_us),
        "dsc": (d_ds, b_ds),
        "x1": (d_x1, b_x1),
        "attn1": (d_at1, b_at1),
        "o_out": (o_out, b_out),
        "hid1_hbm": (hid_h, b_hid_h),
    }
    p = Prog(nc, es)
    emit_ffn1(nc, es, p, Dio)
    es.close()
    return nc


def _shard_tokens(arr_bt_f, core):
    b, q = core // 4, core % 4
    sl = arr_bt_f[b, q * T:(q + 1) * T, :]
    main = np.ascontiguousarray(sl.T)
    if q == 0:
        halo = np.zeros((arr_bt_f.shape[2], 4), np.float32)
    else:
        halo = np.ascontiguousarray(arr_bt_f[b, q * T - 4:q * T, :].T)
    return np.ascontiguousarray(np.concatenate([main, halo], axis=1))


def _launch(nc, in_maps):
    res = run_bass_kernel_spmd(nc, in_maps, core_ids=list(range(NCORES)))
    return res.results


def kernel(**inp):
    f32 = np.float32
    x = np.asarray(inp["L0_attn_hidden_states"], f32)
    cT = lambda a: np.ascontiguousarray(np.asarray(a, f32).T)
    cC = lambda a: np.ascontiguousarray(np.asarray(a, f32))
    pcol = lambda a, m: np.ascontiguousarray(np.asarray(a, f32).reshape(m, 128).T)

    w_inT = cT(inp["L0_attn_W_in"])
    w_xT = cT(inp["L0_attn_W_x"])
    w_dtT = cT(inp["L0_attn_W_dt"])
    conv_w = cC(inp["L0_attn_conv_w"])
    conv_b = cC(np.asarray(inp["L0_attn_conv_b"], f32).reshape(DI, 1))
    b_dt = cC(np.asarray(inp["L0_attn_b_dt"], f32).reshape(DI, 1))
    alog = cC(inp["L0_attn_A_log"])
    dskip = pcol(inp["L0_attn_D_skip"], MI)
    w_outT = cT(inp["L0_attn_W_out"])
    w1T = cT(inp["L0_ffn_w1"])
    w2T = cT(inp["L0_ffn_w2"])
    w3T = cT(inp["L0_ffn_w3"])
    w_in1 = cC(inp["L1_attn_W_in"])
    b_in = pcol(inp["L1_attn_b_in"], KD)
    wg = np.ascontiguousarray(np.concatenate(
        [np.asarray(inp["L1_attn_Wg_x"], f32).reshape(8 * 256, 128),
         np.asarray(inp["L1_attn_Wg_a"], f32).reshape(8 * 256, 128)], axis=0))
    bg = np.ascontiguousarray(np.concatenate(
        [pcol(inp["L1_attn_bg_x"], 8), pcol(inp["L1_attn_bg_a"], 8)], axis=1))
    arp = pcol(inp["L1_attn_a_real_param"], 8)
    aip = pcol(inp["L1_attn_a_imag_param"], 8)
    w_out1 = cC(inp["L1_attn_W_out"])
    b_out = pcol(inp["L1_attn_b_out"], KD)
    gwT = cT(inp["L1_ffn_gate_w"])
    uwT = cT(inp["L1_ffn_up_w"])
    dwT = cT(inp["L1_ffn_down_w"])
    gsc = pcol(inp["L1_ffn_gate_s"], HF // 128)
    usc = pcol(inp["L1_ffn_up_s"], HF // 128)
    dsc = pcol(inp["L1_ffn_down_s"], KD)
    maskrow = np.ones((1, T), f32)
    maskrow[0, ::TB] = 0.0

    if FUSED:
        nc = build_fused()
        maps = []
        for c in range(NCORES):
            b, q = c // 4, c % 4
            preds = [b * 4 + j for j in range(q)]
            m2 = np.zeros((8, 8), f32)
            for j in preds:
                for k in range(8):
                    if k // 4 == b and j % 4 < k % 4 < q:
                        m2[j, k] = 1.0
            mv = np.zeros((1, 8), f32)
            mv[0, preds] = 1.0
            sel1 = np.zeros((8, 8), f32)
            sel2 = np.zeros((8, 8), f32)
            selh = np.zeros((8, 8), f32)
            idf1 = np.ones((1, 64), f32)
            idf2 = np.ones((1, 64), f32)
            for j in preds:
                jq = j % 4
                between = [b * 4 + k for k in range(jq + 1, q)]
                selh[j, j] = 1.0
                if len(between) >= 1:
                    sel1[between[0], j] = 1.0
                    idf1[0, j::8] = 0.0
                if len(between) >= 2:
                    sel2[between[1], j] = 1.0
                    idf2[0, j::8] = 0.0
            maps.append(dict(
                x_f=_shard_tokens(x, c),
                hmaskc=np.full((128, 1), 0.0 if q == 0 else 1.0, f32),
                w_inT=w_inT, w_xT=w_xT, w_dtT=w_dtT, conv_w=conv_w, conv_b=conv_b,
                b_dt=b_dt, alog=alog, dskip=dskip, w_outT=w_outT,
                w1T=w1T, w2T=w2T, w3T=w3T, w_in1=w_in1, b_in=b_in, wg=wg, bg=bg,
                arp=arp, aip=aip, maskrow=maskrow,
                m2T=np.ascontiguousarray(m2.T), mv=mv,
                w_out1=w_out1, b_out=b_out,
                sel1f=np.ascontiguousarray(sel1.reshape(1, 64)),
                sel2f=np.ascontiguousarray(sel2.reshape(1, 64)),
                selhf=np.ascontiguousarray(selh.reshape(1, 64)),
                idf1=idf1, idf2=idf2,
                gwT=gwT, uwT=uwT, dwT=dwT, gsc=gsc, usc=usc, dsc=dsc,
                identb=np.eye(128, dtype=bf16)))
        res = _launch(nc, maps)
        out = np.empty((2, 8192, D), f32)
        for c in range(NCORES):
            b, q = c // 4, c % 4
            out[b, q * T:(q + 1) * T, :] = np.asarray(res[c]["o_out"], f32).T
        return out

    nc = build_l0pre()
    maps = []
    for c in range(NCORES):
        q = c % 4
        maps.append(dict(
            x_f=_shard_tokens(x, c),
            hmaskc=np.full((128, 1), 0.0 if q == 0 else 1.0, f32),
            w_inT=w_inT, w_xT=w_xT, w_dtT=w_dtT, conv_w=conv_w, conv_b=conv_b,
            b_dt=b_dt))
    ra = _launch(nc, maps)

    nc = build_l0scan1()
    maps = [dict(delta=ra[c]["o_delta"], du=ra[c]["o_du"], bc=ra[c]["o_bc"],
                 alog=alog, identb=np.eye(128, dtype=bf16)) for c in range(NCORES)]
    rb1 = _launch(nc, maps)

    def chain_masks(c):
        b, q = c // 4, c % 4
        m2 = np.zeros((8, 8), f32)
        preds = [b * 4 + j for j in range(q)]
        for j in preds:
            for k in range(8):
                if k // 4 == b and j % 4 < k % 4 < q:
                    m2[j, k] = 1.0
        return preds, np.ascontiguousarray(m2.T)

    nc = build_l0scan2()
    maps = []
    for c in range(NCORES):
        preds, m2T = chain_masks(c)
        predS = np.zeros((8 * 128, MI * NST), f32)
        predDT = np.zeros((8, MI * 128), f32)
        for j in range(NCORES):
            predDT[j] = np.asarray(ra[j]["o_dsum"], f32).T.reshape(-1)
            if j in preds:
                predS[j * 128:(j + 1) * 128] = np.asarray(rb1[j]["o_sc"], f32)
        maps.append(dict(delta=ra[c]["o_delta"], bc=ra[c]["o_bc"], alog=alog,
                         y1=rb1[c]["o_y1"], u=ra[c]["o_u"], sz=ra[c]["o_sz"],
                         dskip=dskip, predS=predS, predDT=predDT, m2T=m2T,
                         w_outT=w_outT, identb=np.eye(128, dtype=bf16)))
    rb2 = _launch(nc, maps)

    nc = build_ffn0()
    maps = [dict(xn=ra[c]["o_xn"], w1T=w1T, w2T=w2T, w3T=w3T) for c in range(NCORES)]
    rc = _launch(nc, maps)

    nc = build_rg1()
    maps = [dict(x_f=_shard_tokens(x, c), attn0=rb2[c]["o_attn0"],
                 ffn0=rc[c]["o_ffn0"], w_in1=w_in1, b_in=b_in, wg=wg, bg=bg,
                 arp=arp, aip=aip, maskrow=maskrow) for c in range(NCORES)]
    rd1 = _launch(nc, maps)

    nc = build_rg2()
    maps = []
    for c in range(NCORES):
        b, q = c // 4, c % 4
        preds = [b * 4 + j for j in range(q)]
        pf1r = np.ones((128, 64), f32); pf1i = np.zeros((128, 64), f32)
        pf2r = np.ones((128, 64), f32); pf2i = np.zeros((128, 64), f32)
        predHr = np.zeros((128, 64), f32)
        predHi = np.zeros((128, 64), f32)
        sms = [np.asarray(rd1[j]["o_sumry"], f32) for j in range(NCORES)]
        for j in preds:
            jq = j % 4
            between = [b * 4 + k for k in range(jq + 1, q)]
            for ct in range(8):
                predHr[:, ct * 8 + j] = sms[j][:, 16 + ct]
                predHi[:, ct * 8 + j] = sms[j][:, 24 + ct]
                if len(between) >= 1:
                    pf1r[:, ct * 8 + j] = sms[between[0]][:, ct]
                    pf1i[:, ct * 8 + j] = sms[between[0]][:, 8 + ct]
                if len(between) >= 2:
                    pf2r[:, ct * 8 + j] = sms[between[1]][:, ct]
                    pf2i[:, ct * 8 + j] = sms[between[1]][:, 8 + ct]
        maps.append(dict(ec=rd1[c]["o_ec"], es=rd1[c]["o_es"], ur=rd1[c]["o_ur"],
                         ui=rd1[c]["o_ui"], pf1r=pf1r, pf1i=pf1i, pf2r=pf2r,
                         pf2i=pf2i, predHr=predHr, predHi=predHi, maskrow=maskrow,
                         w_out1=w_out1, b_out=b_out))
    rd2 = _launch(nc, maps)

    nc = build_ffn1()
    maps = [dict(xn1=rd1[c]["o_xn1"], gwT=gwT, uwT=uwT, dwT=dwT, gsc=gsc, usc=usc,
                 dsc=dsc, x1=rd1[c]["o_x1"], attn1=rd2[c]["o_attn1"])
            for c in range(NCORES)]
    re_ = _launch(nc, maps)

    out = np.empty((2, 8192, D), f32)
    for c in range(NCORES):
        b, q = c // 4, c % 4
        out[b, q * T:(q + 1) * T, :] = np.asarray(re_[c]["o_out"], f32).T
    return out


# version 15 (final)
# speedup vs baseline: 1.0877x; 1.0877x over previous
import numpy as np
import ml_dtypes
from contextlib import ExitStack

import concourse.bass as bass
import concourse.mybir as mybir
from concourse.bass_utils import run_bass_kernel_spmd

F32 = mybir.dt.float32
BF16 = mybir.dt.bfloat16
AF = mybir.ActivationFunctionType
ALU = mybir.AluOpType

NCORES = 8
FUSED = True
T = 2048
TH = T + 4
D = 2048
DI = 4096
NST = 16
DTR = 128
DCONV = 4
HF = 5632
DH = 1024
TB = 64
NBLK = T // TB
KD = D // 128
MI = DI // 128
bf16 = ml_dtypes.bfloat16


class BufT:
    __slots__ = ("w", "r", "sname", "waw", "wlist")

    def __init__(self, sname=None, waw=True):
        self.w = None
        self.r = []
        self.sname = sname
        self.waw = waw
        self.wlist = None if waw else []


class Prog:
    ENG = ("v", "s", "g", "t", "y")

    def __init__(self, nc, es):
        self.nc = nc
        self.es = es
        self.eng = {"v": nc.vector, "s": nc.scalar, "g": nc.gpsimd,
                    "t": nc.tensor, "y": nc.sync}
        self.sem = {e: es.enter_context(nc.semaphore(f"sem_{e}")) for e in self.ENG}
        self.cnt = {e: 0 for e in self.ENG}
        self.waited = {}

    def dsem(self, name):
        if not hasattr(self, "semmap"):
            self.semmap = {}
            self.nslots = 0
        if name in self.semmap:
            return self.semmap[name]
        slot = "D%d" % len(self.semmap)
        if slot not in self.sem:
            self.sem[slot] = self.es.enter_context(self.nc.semaphore("sem_" + slot))
            self.cnt[slot] = 0
        self.semmap[name] = slot
        return slot

    def _dep(self, consumer, deps):
        need = {}
        for d in deps:
            if d is None:
                continue
            pe, v = d
            if pe == consumer and v <= self.cnt[consumer]:
                pass
            if v > 0 and self.waited.get((consumer, pe), 0) < v:
                need[pe] = max(need.get(pe, 0), v)
        for pe, v in need.items():
            self.eng[consumer].wait_ge(self.sem[pe], v)
            self.waited[(consumer, pe)] = v

    def op(self, engine, emit_fn, reads=(), writes=(), extra=(), damt=1, sem=None):
        deps = []
        for b in reads:
            deps.extend(b.wlist if b.wlist is not None else [b.w])
        for b in writes:
            if b.waw:
                deps.append(b.w)
            deps.extend(b.r)
        deps.extend(extra)
        self._dep(engine, deps)
        inst = emit_fn(self.eng[engine])
        s = sem if sem is not None else engine
        if damt is None:
            inst.then_inc(self.sem[s])
            damt = 1
        else:
            inst.then_inc(self.sem[s], damt)
        self.cnt[s] += damt
        tok = (s, self.cnt[s])
        for b in reads:
            b.r.append(tok)
        for b in writes:
            b.w = tok
            if b.wlist is not None:
                b.wlist.append(tok)
            else:
                b.r = []
        return tok

    def dma(self, queue, out, in_, reads=(), writes=(), extra=()):
        assert writes, "dma needs a written buffer for its semaphore"
        if writes[0].wlist is not None and reads:
            s = self.dsem("st" + reads[0].sname)
        else:
            s = self.dsem(writes[0].sname)
        return self.op(queue, lambda e: e.dma_start(out=out, in_=in_),
                       reads, writes, extra, damt=16, sem=s)

    def barrier(self):
        for e in self.ENG:
            self._dep(e, [(o, self.cnt[o]) for o in self.sem if self.cnt[o] > 0])
        if hasattr(self, "semmap"):
            self.semmap.clear()


def sbuf(nc, es, name, shape, dt, sname=None):
    h = es.enter_context(nc.sbuf_tensor(name, shape, dt))
    return h, BufT(sname=sname or ("L" + name))


def psum(nc, es, name, shape, dt=F32, sname=None):
    h = es.enter_context(nc.psum_tensor(name, shape, dt))
    return h, BufT(sname=sname or ("L" + name))


def emit_l0pre(nc, es, p, Dio, pre="a_"):
    x_f, _ = Dio["x_f"]
    hmaskc, _ = Dio["hmaskc"]
    w_inT, _ = Dio["w_inT"]
    w_xT, _ = Dio["w_xT"]
    w_dtT, _ = Dio["w_dtT"]
    conv_w, _ = Dio["conv_w"]
    conv_b, _ = Dio["conv_b"]
    b_dt, _ = Dio["b_dt"]
    o_xn, b_o_xn = Dio["o_xn"]
    o_delta, b_o_delta = Dio["o_delta"]
    o_du, b_o_du = Dio["o_du"]
    o_u, b_o_u = Dio["o_u"]
    o_sz, b_o_sz = Dio["o_sz"]
    o_bc, b_o_bc = Dio["o_bc"]
    o_dsum, b_o_dsum = Dio["o_dsum"]
    ub_hbm, b_ub_hbm = Dio["ub_hbm"]
    SB = lambda n, sh, dt: sbuf(nc, es, pre + n, sh, dt, sname="L" + n)
    PS = lambda n, sh: psum(nc, es, pre + n, sh, sname="L" + n)

    xn, b_xn = SB("xn", [128, KD * TH], BF16)

    cpool, b_cp = SB("cpool", [128, 8], F32)
    onesb, b_on = SB("onesb", [1, 128], BF16)
    e_a, b_ea = SB("e_a", [128, 33], BF16)
    e_b, b_eb = SB("e_b", [128, 33], BF16)
    hmt, b_hmt = SB("hmt", [128, 1], F32)
    p.op("v", lambda v: v.memset(cpool[:, 0:1], -1.0), writes=[b_cp])
    p.op("v", lambda v: v.memset(cpool[:, 1:2], 1.0), writes=[b_cp])
    p.op("v", lambda v: v.memset(cpool[:, 2:3], 1e-5), writes=[b_cp])
    p.op("v", lambda v: v.memset(cpool[:, 3:4], -0.5), writes=[b_cp])
    p.op("v", lambda v: v.memset(e_a[:, :], 0.0), writes=[b_ea])
    p.op("v", lambda v: v.memset(e_a[:, 0:1], 1.0), writes=[b_ea])
    p.op("v", lambda v: v.memset(e_b[:, :], 0.0), writes=[b_eb])
    p.op("v", lambda v: v.memset(e_b[:, 32:33], 1.0), writes=[b_eb])
    p.op("v", lambda v: v.memset(onesb[:], 1.0), writes=[b_on])
    p.dma("y", hmt[:], hmaskc, writes=[b_hmt])
    c_m1 = cpool[:, 0:1]
    c_eps = cpool[0:1, 2:3]
    c_mhalf = cpool[0:1, 3:4]

    with ExitStack() as ph:
        SBp = lambda n, sh, dt: sbuf(nc, ph, pre + n, sh, dt, sname="L" + n)
        PSp = lambda n, sh: psum(nc, ph, pre + n, sh, sname="L" + n)
        xf32 = [SBp("xf32_%d" % i, [128, TH], F32) for i in range(2)]
        sqb = [SBp("sqb_%d" % i, [128, TH], BF16) for i in range(2)]
        pstat = [PSp("pstat%d" % n, [33, 512 if n < 4 else 4]) for n in range(5)]
        m_r = SBp("m_r", [1, TH], F32)
        e_r = SBp("e_r", [1, TH], F32)
        q_r = SBp("q_r", [1, TH], F32)
        r_r = SBp("r_r", [1, TH], F32)
        mrowb = SBp("mrowb", [1, TH], BF16)
        rrowb = SBp("rrowb", [1, TH], BF16)
        pbc = [PSp("pbc%d" % i, [128, 512]) for i in range(2)]
        mbc = SBp("mbc", [128, TH], BF16)
        rbc = SBp("rbc", [128, TH], BF16)
        tscr = SBp("tscr", [128, TH], BF16)

        for k in range(KD):
            b = k % 2
            xb, bxb = xf32[b]
            sq, bsq = sqb[b]
            p.dma("y", xb[:], x_f[k * 128:(k + 1) * 128, :], writes=[bxb])
            p.op("s", lambda s, k=k, xb=xb: s.copy(
                out=xn[:, k * TH:(k + 1) * TH], in_=xb[:]), reads=[bxb], writes=[b_xn])
            p.op("s", lambda s, xb=xb, sq=sq: s.activation(
                out=sq[:], in_=xb[:], func=AF.Square), reads=[bxb], writes=[bsq])
            for n in range(5):
                ncs = 512 if n < 4 else 4
                noff = n * 512
                ps, bps = pstat[n]
                p.op("t", lambda t, k=k, ps=ps, ncs=ncs, noff=noff: t.matmul(
                    ps[0:33, 0:ncs], e_a[:, 0:33], xn[:, k * TH + noff:k * TH + noff + ncs],
                    start=(k == 0), stop=False),
                    reads=[b_xn, b_ea], writes=[bps])
                p.op("t", lambda t, k=k, ps=ps, sq=sq, ncs=ncs, noff=noff: t.matmul(
                    ps[0:33, 0:ncs], e_b[:, 0:33], sq[:, noff:noff + ncs],
                    start=False, stop=(k == KD - 1)),
                    reads=[bsq, b_eb], writes=[bps])
        mr, bmr = m_r
        er, ber = e_r
        qr, bqr = q_r
        rr, brr = r_r
        for n in range(5):
            ncs = 512 if n < 4 else 4
            ps, bps = pstat[n]
            p.op("s", lambda s, ps=ps, n=n, ncs=ncs: s.copy(
                out=mr[0:1, n * 512:n * 512 + ncs], in_=ps[0:1, 0:ncs]),
                reads=[bps], writes=[bmr])
            p.op("s", lambda s, ps=ps, n=n, ncs=ncs: s.copy(
                out=er[0:1, n * 512:n * 512 + ncs], in_=ps[32:33, 0:ncs]),
                reads=[bps], writes=[ber])
        p.op("v", lambda v: v.tensor_scalar_mul(out=mr[:], in0=mr[:],
                                                scalar1=1.0 / 2048.0),
             reads=[bmr], writes=[bmr])
        p.op("v", lambda v: v.tensor_scalar_mul(out=er[:], in0=er[:],
                                                scalar1=1.0 / 2048.0),
             reads=[ber], writes=[ber])
        p.op("v", lambda v: v.tensor_tensor(out=qr[:], in0=mr[:], in1=mr[:],
                                            op=ALU.mult), reads=[bmr], writes=[bqr])
        p.op("v", lambda v: v.tensor_tensor(out=er[:], in0=er[:], in1=qr[:],
                                            op=ALU.subtract), reads=[bqr, ber], writes=[ber])
        p.op("s", lambda s: s.activation(out=qr[:], in_=er[:], func=AF.Ln,
                                         bias=c_eps), reads=[ber, b_cp], writes=[bqr])
        p.op("s", lambda s: s.activation(out=rr[:], in_=qr[:], func=AF.Exp,
                                         scale=c_mhalf), reads=[bqr, b_cp], writes=[brr])
        mb, bmb = mrowb
        rb2, brb2 = rrowb
        p.op("s", lambda s: s.copy(out=mb[:], in_=mr[:]), reads=[bmr], writes=[bmb])
        p.op("s", lambda s: s.copy(out=rb2[:], in_=rr[:]), reads=[brr], writes=[brb2])
        mc, bmc = mbc
        rc, brc = rbc
        for n in range(5):
            ncs = 512 if n < 4 else 4
            noff = n * 512
            ps, bps = pbc[n % 2]
            p.op("t", lambda t, ps=ps, noff=noff, ncs=ncs: t.matmul(
                ps[:, 0:ncs], onesb[:], mb[0:1, noff:noff + ncs], start=True, stop=True),
                reads=[b_on, bmb], writes=[bps])
            p.op("s", lambda s, ps=ps, noff=noff, ncs=ncs: s.copy(
                out=mc[:, noff:noff + ncs], in_=ps[:, 0:ncs]), reads=[bps], writes=[bmc])
            p.op("t", lambda t, ps=ps, noff=noff, ncs=ncs: t.matmul(
                ps[:, 0:ncs], onesb[:], rb2[0:1, noff:noff + ncs], start=True, stop=True),
                reads=[b_on, brb2], writes=[bps])
            p.op("s", lambda s, ps=ps, noff=noff, ncs=ncs: s.copy(
                out=rc[:, noff:noff + ncs], in_=ps[:, 0:ncs]), reads=[bps], writes=[brc])
        ts_, bts = tscr
        for k in range(KD):
            p.op("v", lambda v, k=k: v.tensor_tensor(
                out=ts_[:], in0=xn[:, k * TH:(k + 1) * TH], in1=mc[:], op=ALU.subtract),
                reads=[b_xn, bmc], writes=[bts])
            p.op("v", lambda v, k=k: v.tensor_tensor(
                out=xn[:, k * TH:(k + 1) * TH], in0=ts_[:], in1=rc[:], op=ALU.mult),
                reads=[bts, brc], writes=[b_xn])
            p.dma("y", o_xn[k * 128:(k + 1) * 128, :], xn[:, k * TH:k * TH + T],
                  reads=[b_xn], writes=[b_o_xn])
        p.barrier()

    with ExitStack() as ph:
        SBp = lambda n, sh, dt: sbuf(nc, ph, pre + n, sh, dt, sname="L" + n)
        PSp = lambda n, sh: psum(nc, ph, pre + n, sh, sname="L" + n)
        wv = [SBp("wv%d" % i, [128, KD * 128], BF16) for i in range(2)]
        pxz = [PSp("pxz%d" % n, [128, 512]) for n in range(4)]
        pxzh = PSp("pxzh", [128, 4])
        up = [SBp("up%d" % i, [128, TH], F32) for i in range(2)]
        csA = SBp("csA", [128, T], F32)
        csB = SBp("csB", [128, T], F32)
        uf = SBp("uf", [128, T], F32)
        ubf = SBp("ubf", [128, T], BF16)
        szf = SBp("szf", [128, T], F32)
        cwt = SBp("cwt", [128, DCONV], F32)
        cbt = SBp("cbt", [128, 1], F32)

        for m in range(2 * MI):
            wb, bwb = wv[m % 2]
            p.dma("g", wb[:].rearrange("p (k c) -> p k c", k=KD),
                  w_inT[:, m * 128:(m + 1) * 128].rearrange("(k p) c -> p k c", p=128),
                  writes=[bwb])
            is_x = m < MI
            if is_x:
                cw, bcw = cwt
                cb, bcb = cbt
                p.dma("y", cw[:], conv_w[m * 128:(m + 1) * 128, :], writes=[bcw])
                p.dma("y", cb[:], conv_b[m * 128:(m + 1) * 128, :], writes=[bcb])
                ub_, bub = up[m % 2]
            for k in range(KD):
                for n in range(5):
                    if n < 4:
                        ps, bps = pxz[n]
                        ncs, roff = 512, n * 512
                    else:
                        ps, bps = pxzh
                        ncs, roff = 4, T
                    p.op("t", lambda t, ps=ps, k=k, ncs=ncs, roff=roff: t.matmul(
                        ps[:, 0:ncs], wb[:, k * 128:(k + 1) * 128],
                        xn[:, k * TH + roff:k * TH + roff + ncs],
                        start=(k == 0), stop=(k == KD - 1)),
                        reads=[bwb, b_xn], writes=[bps])
            for n in range(5):
                if n < 4:
                    ps, bps = pxz[n]
                    ncs, roff = 512, n * 512
                else:
                    ps, bps = pxzh
                    ncs, roff = 4, T
                if is_x:
                    if n < 4:
                        p.op("s", lambda s, ps=ps, n=n: s.copy(
                            out=ub_[:, 4 + n * 512:4 + (n + 1) * 512], in_=ps[:, 0:512]),
                            reads=[bps], writes=[bub])
                    else:
                        p.op("s", lambda s, ps=ps: s.activation(
                            out=ub_[:, 0:4], in_=ps[:, 0:4], func=AF.Identity,
                            scale=hmt[:]), reads=[bps, b_hmt], writes=[bub])
                else:
                    if n < 4:
                        p.op("s", lambda s, ps=ps, n=n: s.activation(
                            out=szf[0][:, n * 512:(n + 1) * 512], in_=ps[:, 0:512],
                            func=AF.Silu), reads=[bps], writes=[szf[1]])
            if is_x:
                cA, bcA = csA
                cB, bcB = csB
                p.op("v", lambda v: v.tensor_scalar(
                    out=cA[:], in0=ub_[:, 1:1 + T], scalar1=cw[:, 0:1], scalar2=None,
                    op0=ALU.mult), reads=[bub, bcw], writes=[bcA])
                p.op("v", lambda v: v.scalar_tensor_tensor(
                    out=cB[:], in0=ub_[:, 2:2 + T], scalar=cw[:, 1:2], in1=cA[:],
                    op0=ALU.mult, op1=ALU.add), reads=[bub, bcw, bcA], writes=[bcB])
                p.op("v", lambda v: v.scalar_tensor_tensor(
                    out=cA[:], in0=ub_[:, 3:3 + T], scalar=cw[:, 2:3], in1=cB[:],
                    op0=ALU.mult, op1=ALU.add), reads=[bub, bcw, bcB], writes=[bcA])
                p.op("v", lambda v: v.scalar_tensor_tensor(
                    out=cB[:], in0=ub_[:, 4:4 + T], scalar=cw[:, 3:4], in1=cA[:],
                    op0=ALU.mult, op1=ALU.add), reads=[bub, bcw, bcA], writes=[bcB])
                u_, bu_ = uf
                p.op("s", lambda s: s.activation(out=u_[:], in_=cB[:], func=AF.Silu,
                                                 bias=cb[:]),
                     reads=[bcB, bcb], writes=[bu_])
                ub2, bub2 = ubf
                p.op("s", lambda s: s.copy(out=ub2[:], in_=u_[:]), reads=[bu_], writes=[bub2])
                p.dma("y", o_u[m * 128:(m + 1) * 128, :], u_[:], reads=[bu_], writes=[b_o_u])
                p.dma("s", ub_hbm[m * 128:(m + 1) * 128, :], ub2[:], reads=[bub2],
                      writes=[b_ub_hbm])
            else:
                mi = m - MI
                p.dma("y", o_sz[mi * 128:(mi + 1) * 128, :], szf[0][:], reads=[szf[1]],
                      writes=[b_o_sz])
        p.barrier()

    dtlb, b_dtlb = SB("dtlb", [128, T], BF16)
    with ExitStack() as ph:
        SBp = lambda n, sh, dt: sbuf(nc, ph, pre + n, sh, dt, sname="L" + n)
        PSp = lambda n, sh: psum(nc, ph, pre + n, sh, sname="L" + n)
        wx = [SBp("wx%d" % i, [128, 160], BF16) for i in range(2)]
        ubk = [SBp("ubk%d" % i, [128, T], BF16) for i in range(2)]
        pxa = [PSp("pxa%d" % n, [128, 512]) for n in range(4)]
        pxb = [PSp("pxb%d" % n, [32, 512]) for n in range(4)]
        bcs = SBp("bcs", [32, T], BF16)

        for k in range(MI):
            wbx, bwx = wx[k % 2]
            ub_, bub = ubk[k % 2]
            p.dma("g", wbx[:], w_xT[k * 128:(k + 1) * 128, :], writes=[bwx])
            p.dma("y", ub_[:], ub_hbm[k * 128:(k + 1) * 128, :], reads=[b_ub_hbm],
                  writes=[bub])
            for n in range(4):
                pa, bpa = pxa[n]
                pb, bpb = pxb[n]
                p.op("t", lambda t, pa=pa, k=k, n=n: t.matmul(
                    pa[:, :], wbx[:, 0:128], ub_[:, n * 512:(n + 1) * 512],
                    start=(k == 0), stop=(k == MI - 1)), reads=[bwx, bub], writes=[bpa])
                p.op("t", lambda t, pb=pb, k=k, n=n: t.matmul(
                    pb[:, :], wbx[:, 128:160], ub_[:, n * 512:(n + 1) * 512],
                    start=(k == 0), stop=(k == MI - 1)), reads=[bwx, bub], writes=[bpb])
        bc_, bbc = bcs
        for n in range(4):
            pa, bpa = pxa[n]
            pb, bpb = pxb[n]
            p.op("s", lambda s, pa=pa, n=n: s.copy(
                out=dtlb[:, n * 512:(n + 1) * 512], in_=pa[:, :]),
                reads=[bpa], writes=[b_dtlb])
            p.op("s", lambda s, pb=pb, n=n: s.copy(
                out=bc_[0:32, n * 512:(n + 1) * 512], in_=pb[:, :]),
                reads=[bpb], writes=[bbc])
        p.dma("y", o_bc[:, :], bc_[0:32, :], reads=[bbc], writes=[b_o_bc])
        p.barrier()

    with ExitStack() as ph:
        SBp = lambda n, sh, dt: sbuf(nc, ph, pre + n, sh, dt, sname="L" + n)
        PSp = lambda n, sh: psum(nc, ph, pre + n, sh, sname="L" + n)
        wdt = SBp("wdt", [128, DI], BF16)
        pdt = [PSp("pdt%d" % n, [128, 512]) for n in range(4)]
        sA = SBp("sA", [128, T], F32)
        sB = SBp("sB", [128, T], F32)
        delta = SBp("delta", [128, T], F32)
        ufr = SBp("ufr", [128, T], F32)
        dub = SBp("dub", [128, T], F32)
        bdtc = SBp("bdtc", [128, 1], F32)
        dsum = SBp("dsum", [128, MI], F32)

        wd, bwd = wdt
        p.dma("g", wd[:], w_dtT[:, :], writes=[bwd])
        dall, bdall = dsum
        for m in range(MI):
            bd, bbd = bdtc
            p.dma("y", bd[:], b_dt[m * 128:(m + 1) * 128, :], writes=[bbd])
            de, bde = delta
            a_, ba_ = sA
            b2, bb2 = sB
            for n in range(4):
                ps, bps = pdt[n]
                p.op("t", lambda t, ps=ps, m=m, n=n: t.matmul(
                    ps[:, :], wd[:, m * 128:(m + 1) * 128],
                    dtlb[:, n * 512:(n + 1) * 512], start=True, stop=True),
                    reads=[bwd, b_dtlb], writes=[bps])
                sl = slice(n * 512, (n + 1) * 512)
                p.op("s", lambda s, ps=ps, sl=sl: s.activation(
                    out=a_[:, sl], in_=ps[:, :], func=AF.Abs, bias=bd[:]),
                    reads=[bps, bbd], writes=[ba_])
                p.op("s", lambda s, ps=ps, sl=sl: s.activation(
                    out=de[:, sl], in_=ps[:, :], func=AF.Relu, bias=bd[:]),
                    reads=[bps, bbd], writes=[bde])
            p.op("s", lambda s: s.activation(out=b2[:], in_=a_[:], func=AF.Exp,
                                             scale=cpool[:, 0:1]),
                 reads=[ba_, b_cp], writes=[bb2])
            p.op("s", lambda s: s.activation(out=a_[:], in_=b2[:], func=AF.Ln,
                                             bias=cpool[:, 1:2]),
                 reads=[bb2, b_cp], writes=[ba_])
            p.op("v", lambda v: v.tensor_tensor(out=de[:], in0=de[:], in1=a_[:],
                                                op=ALU.add),
                 reads=[ba_, bde], writes=[bde])
            u_, bu_ = ufr
            p.dma("s", u_[:], o_u[m * 128:(m + 1) * 128, :], reads=[b_o_u], writes=[bu_])
            du_, bdu = dub
            p.op("v", lambda v: v.tensor_tensor(out=du_[:], in0=de[:], in1=u_[:],
                                                op=ALU.mult),
                 reads=[bde, bu_], writes=[bdu])
            p.op("v", lambda v, m=m: v.tensor_reduce(
                out=dall[:, m:m + 1], in_=de[:], axis=mybir.AxisListType.X, op=ALU.add),
                reads=[bde], writes=[bdall])
            p.dma("y", o_delta[m * 128:(m + 1) * 128, :], de[:], reads=[bde],
                  writes=[b_o_delta])
            p.dma("g", o_du[m * 128:(m + 1) * 128, :], du_[:], reads=[bdu],
                  writes=[b_o_du])
        p.dma("y", o_dsum[:, :], dall[:], reads=[bdall], writes=[b_o_dsum])

    p.barrier()




def build_l0pre():
    nc = bass.Bass("TRN2", target_bir_lowering=False)
    es = ExitStack()
    dram = lambda n, sh, dt, k="ExternalInput": (nc.dram_tensor(n, sh, dt, kind=k).ap(),
                                                 BufT(sname="S" + n, waw=False))

    x_f, _ = dram("x_f", [D, TH], F32)
    hmaskc, _ = dram("hmaskc", [128, 1], F32)
    w_inT, _ = dram("w_inT", [D, 2 * DI], F32)
    w_xT, _ = dram("w_xT", [DI, 160], F32)
    w_dtT, _ = dram("w_dtT", [DTR, DI], F32)
    conv_w, _ = dram("conv_w", [DI, DCONV], F32)
    conv_b, _ = dram("conv_b", [DI, 1], F32)
    b_dt, _ = dram("b_dt", [DI, 1], F32)
    o_xn, b_o_xn = dram("o_xn", [D, T], BF16, "ExternalOutput")
    o_delta, b_o_delta = dram("o_delta", [DI, T], F32, "ExternalOutput")
    o_du, b_o_du = dram("o_du", [DI, T], BF16, "ExternalOutput")
    o_u, b_o_u = dram("o_u", [DI, T], F32, "ExternalOutput")
    o_sz, b_o_sz = dram("o_sz", [DI, T], F32, "ExternalOutput")
    o_bc, b_o_bc = dram("o_bc", [32, T], BF16, "ExternalOutput")
    o_dsum, b_o_dsum = dram("o_dsum", [128, MI], F32, "ExternalOutput")
    ub_hbm, b_ub_hbm = dram("ub_hbm", [DI, T], BF16, "Internal")
    Dio = {
        "x_f": (x_f, _),
        "hmaskc": (hmaskc, _),
        "w_inT": (w_inT, _),
        "w_xT": (w_xT, _),
        "w_dtT": (w_dtT, _),
        "conv_w": (conv_w, _),
        "conv_b": (conv_b, _),
        "b_dt": (b_dt, _),
        "o_xn": (o_xn, b_o_xn),
        "o_delta": (o_delta, b_o_delta),
        "o_du": (o_du, b_o_du),
        "o_u": (o_u, b_o_u),
        "o_sz": (o_sz, b_o_sz),
        "o_bc": (o_bc, b_o_bc),
        "o_dsum": (o_dsum, b_o_dsum),
        "ub_hbm": (ub_hbm, b_ub_hbm),
    }
    p = Prog(nc, es)
    emit_l0pre(nc, es, p, Dio)
    es.close()
    return nc


def emit_l0scan1(nc, es, p, Dio, pre="b1_"):
    d_delta, b_delta = Dio["delta"]
    d_du, b_du = Dio["du"]
    d_bc, b_bc = Dio["bc"]
    d_alog, b_alog = Dio["alog"]
    d_idb, b_idb = Dio["identb"]
    o_y1, b_y1 = Dio["o_y1"]
    o_sc, b_sc = Dio["o_sc"]

    SB = lambda n, sh, dt: sbuf(nc, es, pre + n, sh, dt, sname="L" + n)
    G = 2

    aal, b_aal = SB("aal", [128, MI * NST], F32)
    p.dma("y", aal[:].rearrange("p (m n) -> p m n", n=NST),
          d_alog.rearrange("(m p) n -> p m n", p=128), reads=[b_alog], writes=[b_aal])
    p.op("s", lambda s: s.activation(out=aal[:], in_=aal[:], func=AF.Exp),
         reads=[b_aal], writes=[b_aal])
    p.op("v", lambda v: v.tensor_scalar_mul(out=aal[:], in0=aal[:], scalar1=-1.0),
         reads=[b_aal], writes=[b_aal])
    idb, b_idbs = SB("idb", [128, 128], BF16)
    p.dma("y", idb[:], d_idb, reads=[b_idb], writes=[b_idbs])

    deltat = [SB("deltat%d" % i, [128, T], F32) for i in range(G)]
    dut = [SB("dut%d" % i, [128, T], BF16) for i in range(G)]
    yac = [SB("yac%d" % i, [128, T], F32) for i in range(G)]
    bbn = [SB("bbn%d" % i, [128, T], BF16) for i in range(2)]
    cbn = [SB("cbn%d" % i, [128, T], BF16) for i in range(2)]
    da_s = [SB("da_s%d" % i, [128, T], F32) for i in range(2)]
    h_s = [SB("h_s%d" % i, [128, T], BF16) for i in range(2)]
    xb_s = [SB("xb_s%d" % i, [128, T], BF16) for i in range(2)]
    ym_s = [SB("ym_s%d" % i, [128, T], BF16) for i in range(2)]
    scol, b_scol = SB("scol", [128, MI * NST], F32)
    with ExitStack() as ph:
        py = [[psum(nc, ph, pre + "py%d_%d" % (i, n), [128, 512], sname="Lpy%d_%d" % (i, n))
               for n in range(4)] for i in range(G)]
        for mg in range(MI // G):
            for i in range(G):
                m = mg * G + i
                dt_, bdt_ = deltat[i]
                du_, bdu_ = dut[i]
                p.dma("y", dt_[:], d_delta[m * 128:(m + 1) * 128, :], reads=[b_delta],
                      writes=[bdt_])
                p.dma("s", du_[:], d_du[m * 128:(m + 1) * 128, :], reads=[b_du],
                      writes=[bdu_])
            for n in range(NST):
                bb, bbb = bbn[n % 2]
                cb, bcb = cbn[n % 2]
                p.dma("g", bb[:], d_bc[n:n + 1, :].broadcast_to([128, T]), reads=[b_bc],
                      writes=[bbb])
                p.dma("g", cb[:], d_bc[16 + n:17 + n, :].broadcast_to([128, T]),
                      reads=[b_bc], writes=[bcb])
                for i in range(G):
                    m = mg * G + i
                    dt_, bdt_ = deltat[i]
                    du_, bdu_ = dut[i]
                    da, bda = da_s[i % 2]
                    hh, bhh = h_s[i % 2]
                    xb_, bxb_ = xb_s[i % 2]
                    ym, bym = ym_s[i % 2]
                    p.op("s", lambda s, dt_=dt_, da=da, m=m, n=n: s.activation(
                        out=da[:], in_=dt_[:], func=AF.Exp,
                        scale=aal[:, m * NST + n:m * NST + n + 1]),
                        reads=[bdt_, b_aal], writes=[bda])
                    p.op("v", lambda v, du_=du_, xb_=xb_: v.tensor_tensor(
                        out=xb_[:], in0=du_[:], in1=bb[:], op=ALU.mult),
                        reads=[bdu_, bbb], writes=[bxb_])
                    p.op("v", lambda v, da=da, xb_=xb_, hh=hh: v.tensor_tensor_scan(
                        out=hh[:], data0=da[:], data1=xb_[:], initial=0.0,
                        op0=ALU.mult, op1=ALU.add),
                        reads=[bda, bxb_], writes=[bhh])
                    p.op("v", lambda v, hh=hh, m=m, n=n: v.tensor_copy(
                        out=scol[:, m * NST + n:m * NST + n + 1], in_=hh[:, T - 1:T]),
                        reads=[bhh], writes=[b_scol])
                    p.op("v", lambda v, hh=hh, ym=ym: v.tensor_tensor(
                        out=ym[:], in0=hh[:], in1=cb[:], op=ALU.mult),
                        reads=[bhh, bcb], writes=[bym])
                    for nc4 in range(4):
                        ps, bps = py[i][nc4]
                        p.op("t", lambda t, ps=ps, ym=ym, nc4=nc4, n=n: t.matmul(
                            ps[:, :], idb[:], ym[:, nc4 * 512:(nc4 + 1) * 512],
                            start=(n == 0), stop=(n == NST - 1)),
                            reads=[b_idbs, bym], writes=[bps])
            for i in range(G):
                m = mg * G + i
                ya, bya = yac[i]
                for nc4 in range(4):
                    ps, bps = py[i][nc4]
                    p.op("s", lambda s, ps=ps, ya=ya, nc4=nc4: s.copy(
                        out=ya[:, nc4 * 512:(nc4 + 1) * 512], in_=ps[:, :]),
                        reads=[bps], writes=[bya])
                p.dma("y", o_y1[m * 128:(m + 1) * 128, :], ya[:], reads=[bya],
                      writes=[b_y1])
    p.dma("y", o_sc[:, :], scol[:], reads=[b_scol], writes=[b_sc])
    p.barrier()


def build_l0scan1():
    nc = bass.Bass("TRN2", target_bir_lowering=False)
    es = ExitStack()
    dram = lambda n, sh, dt, k="ExternalInput": (nc.dram_tensor(n, sh, dt, kind=k).ap(),
                                                 BufT(sname="S" + n, waw=False))
    d_delta, b_delta = dram("delta", [DI, T], F32)
    d_du, b_du = dram("du", [DI, T], BF16)
    d_bc, b_bc = dram("bc", [32, T], BF16)
    d_alog, b_alog = dram("alog", [DI, NST], F32)
    d_idb, b_idb = dram("identb", [128, 128], BF16)
    o_y1, b_y1 = dram("o_y1", [DI, T], F32, "ExternalOutput")
    o_sc, b_sc = dram("o_sc", [128, MI * NST], F32, "ExternalOutput")
    Dio = {
        "delta": (d_delta, b_delta),
        "du": (d_du, b_du),
        "bc": (d_bc, b_bc),
        "alog": (d_alog, b_alog),
        "identb": (d_idb, b_idb),
        "o_y1": (o_y1, b_y1),
        "o_sc": (o_sc, b_sc),
    }
    p = Prog(nc, es)
    emit_l0scan1(nc, es, p, Dio)
    es.close()
    return nc


def emit_l0scan2(nc, es, p, Dio, pre="b2_"):
    d_delta, b_delta = Dio["delta"]
    d_bc, b_bc = Dio["bc"]
    d_alog, b_alog = Dio["alog"]
    d_y1, b_y1 = Dio["y1"]
    d_u, b_u = Dio["u"]
    d_sz, b_sz = Dio["sz"]
    d_dskip, b_dskip = Dio["dskip"]
    d_m2t, b_m2t = Dio["m2T"]
    d_woT, b_woT = Dio["w_outT"]
    o_at0, b_at0 = Dio["o_attn0"]
    ygb_h, b_ygbh = Dio["ygb_hbm"]
    fused = "cx1g" in Dio
    if fused:
        d_cx1g, b_cx1g = Dio["cx1g"]
        d_mv, b_mv = Dio["mv"]
    else:
        d_preds, b_preds = Dio["predS"]
        d_predd, b_predd = Dio["predDT"]
    SB = lambda n, sh, dt: sbuf(nc, es, pre + n, sh, dt, sname="L" + n)
    G = 4

    aal, b_aal = SB("aal", [128, MI * NST], F32)
    p.dma("y", aal[:].rearrange("p (m n) -> p m n", n=NST),
          d_alog.rearrange("(m p) n -> p m n", p=128), writes=[b_aal])
    p.op("s", lambda s: s.activation(out=aal[:], in_=aal[:], func=AF.Exp),
         reads=[b_aal], writes=[b_aal])
    p.op("v", lambda v: v.tensor_scalar_mul(out=aal[:], in0=aal[:], scalar1=-1.0),
         reads=[b_aal], writes=[b_aal])
    dsk, b_dsk = SB("dsk", [128, MI], F32)
    p.dma("y", dsk[:], d_dskip, writes=[b_dsk])

    hbuf, b_hbuf = SB("hbuf", [128, MI * NST], F32)
    m2t, b_m2 = SB("m2t", [8, 8], F32)
    p.dma("y", m2t[:], d_m2t, writes=[b_m2])
    with ExitStack() as ph:
        pdt_, b_pdt = psum(nc, ph, pre + "pE", [128, 8])
        dtile, b_dtile = sbuf(nc, ph, pre + "dtile", [8, 128], F32)
        et, b_et = sbuf(nc, ph, pre + "et", [128, 8], F32)
        stile, b_stile = sbuf(nc, ph, pre + "stile", [128, 8 * NST], F32)
        wj, b_wj = sbuf(nc, ph, pre + "wj", [128, NST], F32)
        cj, b_cj = sbuf(nc, ph, pre + "cj", [128, NST], F32)
        if fused:
            allD, b_allD = sbuf(nc, ph, pre + "allD", [8, MI * 128], F32)
            p.dma("y", allD[:].rearrange("j (p mm) -> j p mm", mm=MI),
                  d_cx1g.rearrange("(j p) c -> j p c", p=128)[:, :, 512:512 + MI],
                  reads=[b_cx1g], writes=[b_allD])
            mvb, b_mvb = sbuf(nc, ph, pre + "mvb", [128, 8], F32)
            p.dma("y", mvb[:], d_mv.broadcast_to([128, 8]), reads=[b_mv], writes=[b_mvb])
        for m in range(MI):
            if fused:
                p.op("t", lambda t, m=m: t.matmul(
                    pdt_[:, :],
                    allD[:].rearrange("j (p mm) -> j p mm", mm=MI)[:, :, m],
                    m2t[:], start=True, stop=True),
                    reads=[b_allD, b_m2], writes=[b_pdt])
                p.dma("s", stile[:].rearrange("p (j n) -> p j n", n=NST),
                      d_cx1g.rearrange("(j p) c -> p j c", p=128)[:, :, m * NST:(m + 1) * NST],
                      reads=[b_cx1g], writes=[b_stile])
            else:
                p.dma("y", dtile[:], d_predd[:, m * 128:(m + 1) * 128], reads=[b_predd],
                      writes=[b_dtile])
                p.dma("s", stile[:].rearrange("p (j n) -> p j n", n=NST),
                      d_preds.rearrange("(j p) c -> p j c", p=128)[:, :, m * NST:(m + 1) * NST],
                      reads=[b_preds], writes=[b_stile])
                p.op("t", lambda t: t.matmul(pdt_[:, :], dtile[:], m2t[:],
                                             start=True, stop=True),
                     reads=[b_dtile, b_m2], writes=[b_pdt])
            p.op("s", lambda s: s.copy(out=et[:], in_=pdt_[:, :]), reads=[b_pdt],
                 writes=[b_et])
            for j in range(8):
                p.op("s", lambda s, m=m, j=j: s.activation(
                    out=wj[:], in_=aal[:, m * NST:(m + 1) * NST], func=AF.Exp,
                    scale=et[:, j:j + 1]), reads=[b_aal, b_et], writes=[b_wj])
                if fused:
                    p.op("v", lambda v, j=j: v.tensor_scalar(
                        out=wj[:], in0=wj[:], scalar1=mvb[:, j:j + 1], scalar2=None,
                        op0=ALU.mult), reads=[b_mvb, b_wj], writes=[b_wj])
                p.op("v", lambda v, j=j: v.tensor_tensor(
                    out=cj[:], in0=wj[:], in1=stile[:, j * NST:(j + 1) * NST],
                    op=ALU.mult), reads=[b_wj, b_stile], writes=[b_cj])
                if j == 0:
                    p.op("v", lambda v, m=m: v.tensor_copy(
                        out=hbuf[:, m * NST:(m + 1) * NST], in_=cj[:]),
                        reads=[b_cj], writes=[b_hbuf])
                else:
                    p.op("v", lambda v, m=m: v.tensor_tensor(
                        out=hbuf[:, m * NST:(m + 1) * NST],
                        in0=hbuf[:, m * NST:(m + 1) * NST], in1=cj[:], op=ALU.add),
                        reads=[b_cj], writes=[b_hbuf])
        p.barrier()

    idb, b_idbs = sbuf(nc, es, pre + "idb", [128, 128], BF16, sname="Lidb")
    d_idb, b_idb = Dio["identb"]
    p.dma("y", idb[:], d_idb, reads=[b_idb], writes=[b_idbs])
    G = 2
    with ExitStack() as ph:
        SBp = lambda n, sh, dt: sbuf(nc, ph, pre + n, sh, dt, sname="L" + n)
        onebc, b_oneb = SBp("onebc", [128, T], F32)
        p.op("v", lambda v: v.memset(onebc[:], 1.0), writes=[b_oneb])
        deltat = [SBp("deltat%d" % i, [128, T], F32) for i in range(G)]
        dcum = [SBp("dcum%d" % i, [128, T], F32) for i in range(G)]
        yac = [SBp("yac%d" % i, [128, T], F32) for i in range(G)]
        cbn = [SBp("cbn%d" % i, [128, T], BF16) for i in range(2)]
        ce_s = [SBp("ce_s%d" % i, [128, T], F32) for i in range(2)]
        cr_s = [SBp("cr_s%d" % i, [128, T], BF16) for i in range(2)]
        uf_s = SBp("uf_s", [128, T], F32)
        sz_s = SBp("sz_s", [128, T], F32)
        yg_s = SBp("yg_s", [128, T], BF16)
        py = [[psum(nc, ph, pre + "py%d_%d" % (i, n), [128, 512], sname="Lpy%d_%d" % (i, n))
               for n in range(4)] for i in range(G)]
        for mg in range(MI // G):
            for i in range(G):
                m = mg * G + i
                dt_, bdt_ = deltat[i]
                dc_, bdc_ = dcum[i]
                ya, bya = yac[i]
                p.dma("y", dt_[:], d_delta[m * 128:(m + 1) * 128, :], reads=[b_delta],
                      writes=[bdt_])
                p.dma("s", ya[:], d_y1[m * 128:(m + 1) * 128, :], reads=[b_y1],
                      writes=[bya])
                p.op("v", lambda v, dt_=dt_, dc_=dc_: v.tensor_tensor_scan(
                    out=dc_[:], data0=onebc[:], data1=dt_[:], initial=0.0,
                    op0=ALU.mult, op1=ALU.add), reads=[bdt_, b_oneb], writes=[bdc_])
            for n in range(NST):
                cb, bcb = cbn[n % 2]
                p.dma("g", cb[:], d_bc[16 + n:17 + n, :].broadcast_to([128, T]),
                      reads=[b_bc], writes=[bcb])
                for i in range(G):
                    m = mg * G + i
                    dc_, bdc_ = dcum[i]
                    ce, bce = ce_s[i % 2]
                    cr, bcr = cr_s[i % 2]
                    p.op("s", lambda s, dc_=dc_, ce=ce, m=m, n=n: s.activation(
                        out=ce[:], in_=dc_[:], func=AF.Exp,
                        scale=aal[:, m * NST + n:m * NST + n + 1]),
                        reads=[bdc_, b_aal], writes=[bce])
                    p.op("v", lambda v, ce=ce, cr=cr, m=m, n=n: v.scalar_tensor_tensor(
                        out=cr[:], in0=ce[:], scalar=hbuf[:, m * NST + n:m * NST + n + 1],
                        in1=cb[:], op0=ALU.mult, op1=ALU.mult),
                        reads=[bce, b_hbuf, bcb], writes=[bcr])
                    for nc4 in range(4):
                        ps, bps = py[i][nc4]
                        p.op("t", lambda t, ps=ps, cr=cr, nc4=nc4, n=n: t.matmul(
                            ps[:, :], idb[:], cr[:, nc4 * 512:(nc4 + 1) * 512],
                            start=(n == 0), stop=(n == NST - 1)),
                            reads=[b_idbs, bcr], writes=[bps])
            for i in range(G):
                m = mg * G + i
                ya, bya = yac[i]
                uf, buf_ = uf_s
                sz, bsz = sz_s
                yg, byg = yg_s
                for nc4 in range(4):
                    ps, bps = py[i][nc4]
                    p.op("v", lambda v, ps=ps, ya=ya, nc4=nc4: v.tensor_tensor(
                        out=ya[:, nc4 * 512:(nc4 + 1) * 512],
                        in0=ya[:, nc4 * 512:(nc4 + 1) * 512], in1=ps[:, :],
                        op=ALU.add), reads=[bps, bya], writes=[bya])
                p.dma("y", uf[:], d_u[m * 128:(m + 1) * 128, :], reads=[b_u],
                      writes=[buf_])
                p.dma("s", sz[:], d_sz[m * 128:(m + 1) * 128, :], reads=[b_sz],
                      writes=[bsz])
                p.op("v", lambda v, m=m, ya=ya: v.scalar_tensor_tensor(
                    out=ya[:], in0=uf[:], scalar=dsk[:, m:m + 1], in1=ya[:],
                    op0=ALU.mult, op1=ALU.add),
                    reads=[buf_, b_dsk, bya], writes=[bya])
                p.op("v", lambda v, ya=ya: v.tensor_tensor(
                    out=yg[:], in0=ya[:], in1=sz[:], op=ALU.mult),
                    reads=[bya, bsz], writes=[byg])
                p.dma("y", ygb_h[m * 128:(m + 1) * 128, :], yg[:], reads=[byg],
                      writes=[b_ygbh])
        p.barrier()

    with ExitStack() as ph:
        SBp = lambda n, sh, dt: sbuf(nc, ph, pre + n, sh, dt, sname="L" + n)
        PSp = lambda n, sh: psum(nc, ph, pre + n, sh, sname="L" + n)
        ygr, b_ygr = SBp("ygr", [128, MI * T], BF16)
        p.dma("y", ygr[:].rearrange("p (m t) -> p m t", t=T),
              ygb_h.rearrange("(m p) t -> p m t", p=128), reads=[b_ygbh],
              writes=[b_ygr])
        wv = [SBp("wv%d" % i, [128, MI * 128], BF16) for i in range(2)]
        pso = [PSp("pso%d" % n, [128, 512]) for n in range(4)]
        oat = SBp("oat", [128, T], F32)
        for m in range(D // 128):
            wb, bwb = wv[m % 2]
            p.dma("g", wb[:].rearrange("p (k c) -> p k c", k=MI),
                  d_woT[:, m * 128:(m + 1) * 128].rearrange("(k p) c -> p k c", p=128),
                  reads=[b_woT], writes=[bwb])
            oa, boa = oat
            for k in range(MI):
                for n in range(4):
                    ps, bps = pso[n]
                    p.op("t", lambda t, ps=ps, k=k, n=n: t.matmul(
                        ps[:, :], wb[:, k * 128:(k + 1) * 128],
                        ygr[:, k * T + n * 512:k * T + (n + 1) * 512],
                        start=(k == 0), stop=(k == MI - 1)),
                        reads=[bwb, b_ygr], writes=[bps])
            for n in range(4):
                ps, bps = pso[n]
                p.op("s", lambda s, ps=ps, n=n: s.copy(
                    out=oa[:, n * 512:(n + 1) * 512], in_=ps[:, :]),
                    reads=[bps], writes=[boa])
            p.dma("y", o_at0[m * 128:(m + 1) * 128, :], oa[:], reads=[boa],
                  writes=[b_at0])
        p.barrier()
    p.barrier()




def build_l0scan2():
    nc = bass.Bass("TRN2", target_bir_lowering=False)
    es = ExitStack()
    dram = lambda n, sh, dt, k="ExternalInput": (nc.dram_tensor(n, sh, dt, kind=k).ap(),
                                                 BufT(sname="S" + n, waw=False))
    d_delta, b_delta = dram("delta", [DI, T], F32)
    d_bc, b_bc = dram("bc", [32, T], BF16)
    d_alog, b_alog = dram("alog", [DI, NST], F32)
    d_y1, b_y1 = dram("y1", [DI, T], F32)
    d_u, b_u = dram("u", [DI, T], F32)
    d_sz, b_sz = dram("sz", [DI, T], F32)
    d_dskip, b_dskip = dram("dskip", [128, MI], F32)
    d_preds, b_preds = dram("predS", [8 * 128, MI * NST], F32)
    d_predd, b_predd = dram("predDT", [8, MI * 128], F32)
    d_m2t, b_m2t = dram("m2T", [8, 8], F32)
    d_woT, b_woT = dram("w_outT", [DI, D], F32)
    d_idb, b_idb = dram("identb", [128, 128], BF16)
    o_at0, b_at0 = dram("o_attn0", [D, T], F32, "ExternalOutput")
    ygb_h, b_ygbh = dram("ygb_hbm", [DI, T], BF16, "Internal")
    Dio = {
        "identb": (d_idb, b_idb),
        "delta": (d_delta, b_delta),
        "bc": (d_bc, b_bc),
        "alog": (d_alog, b_alog),
        "y1": (d_y1, b_y1),
        "u": (d_u, b_u),
        "sz": (d_sz, b_sz),
        "dskip": (d_dskip, b_dskip),
        "predS": (d_preds, b_preds),
        "predDT": (d_predd, b_predd),
        "m2T": (d_m2t, b_m2t),
        "w_outT": (d_woT, b_woT),
        "o_attn0": (o_at0, b_at0),
        "ygb_hbm": (ygb_h, b_ygbh),
    }
    p = Prog(nc, es)
    emit_l0scan2(nc, es, p, Dio)
    es.close()
    return nc


def emit_ffn0(nc, es, p, Dio, pre="c_"):
    d_xn, b_xn_h = Dio["xn"]
    d_w1T, b_w1T = Dio["w1T"]
    d_w2T, b_w2T = Dio["w2T"]
    d_w3T, b_w3T = Dio["w3T"]
    o_f0, b_f0 = Dio["o_ffn0"]
    hid_h, b_hid_h = Dio["hid_hbm"]
    SB = lambda n, sh, dt: sbuf(nc, es, pre + n, sh, dt, sname="L" + n)
    MH = HF // 128

    ones_f = SB("ones_f", [128, 1], F32)
    ones_r = SB("ones_r", [1, 128], F32)
    chalf = SB("chalf", [128, 2], F32)
    p.op("v", lambda v: v.memset(ones_f[0][:], 1.0), writes=[ones_f[1]])
    p.op("v", lambda v: v.memset(ones_r[0][:], 1.0), writes=[ones_r[1]])
    p.op("v", lambda v: v.memset(chalf[0][:, 0:1], 0.5), writes=[chalf[1]])
    p.op("v", lambda v: v.memset(chalf[0][:, 1:2], 0.25), writes=[chalf[1]])
    thr = SB("thr", [128, 6], F32)

    with ExitStack() as ph:
        SBp = lambda n, sh, dt: sbuf(nc, ph, pre + n, sh, dt, sname="L" + n)
        wsl = [SBp("wsl%d" % i, [128, HF], F32) for i in range(2)]
        red = SBp("red", [128, 3], F32)
        acc = SBp("acc", [128, 1], F32)
        prg = psum(nc, ph, pre + "prg", [1, 1])
        prb = psum(nc, ph, pre + "prb", [128, 1])
        g1 = SBp("g1", [1, 1], F32)
        for wi, (dw, bdw, kt) in enumerate([(d_w1T, b_w1T, KD), (d_w2T, b_w2T, KD),
                                            (d_w3T, b_w3T, MH)]):
            ac, bac = acc
            for k in range(kt):
                ws, bws = wsl[k % 2]
                ncol = HF if wi < 2 else D
                p.dma("y" if k % 2 == 0 else "s", ws[:, 0:ncol],
                      dw[k * 128:(k + 1) * 128, :], reads=[bdw], writes=[bws])
                rd, brd = red
                p.op("v", lambda v, ws=ws, ncol=ncol: v.tensor_reduce(
                    out=rd[:, 0:1], in_=ws[:, 0:ncol], axis=mybir.AxisListType.X,
                    op=ALU.add, apply_absolute_value=True), reads=[bws], writes=[brd])
                if k == 0:
                    p.op("v", lambda v: v.tensor_copy(out=ac[:], in_=rd[:, 0:1]),
                         reads=[brd], writes=[bac])
                else:
                    p.op("v", lambda v: v.tensor_tensor(
                        out=ac[:], in0=ac[:], in1=rd[:, 0:1], op=ALU.add),
                        reads=[brd], writes=[bac])
            pg, bpg = prg
            p.op("t", lambda t: t.matmul(pg[:, :], ones_f[0][:], ac[:],
                                         start=True, stop=True),
                 reads=[ones_f[1], bac], writes=[bpg])
            gg, bgg = g1
            nel = float(D * HF)
            p.op("v", lambda v, nel=nel: v.tensor_scalar(
                out=gg[:], in0=pg[:, :], scalar1=0.5 / nel, scalar2=0.5e-8,
                op0=ALU.mult, op1=ALU.add), reads=[bpg], writes=[bgg])
            pb, bpb = prb
            p.op("t", lambda t: t.matmul(pb[:, :], ones_r[0][:], gg[:],
                                         start=True, stop=True),
                 reads=[ones_r[1], bgg], writes=[bpb])
            p.op("s", lambda s, wi=wi: s.copy(out=thr[0][:, wi:wi + 1], in_=pb[:, :]),
                 reads=[bpb], writes=[thr[1]])
            p.op("v", lambda v, wi=wi: v.tensor_scalar_mul(
                out=thr[0][:, 3 + wi:4 + wi], in0=thr[0][:, wi:wi + 1], scalar1=-1.0),
                reads=[thr[1]], writes=[thr[1]])
        p.barrier()

    def quantize(pq, wf, bwf, qb, bqb, scr, bscr, ncol, wi):
        pq.op("s", lambda s: s.activation(
            out=qb[:, 0:ncol], in_=wf[:, 0:ncol], func=AF.Sign,
            bias=thr[0][:, 3 + wi:4 + wi]), reads=[bwf, thr[1]], writes=[bqb])
        pq.op("s", lambda s: s.activation(
            out=scr[:, 0:ncol], in_=wf[:, 0:ncol], func=AF.Sign,
            bias=thr[0][:, wi:wi + 1]), reads=[bwf, thr[1]], writes=[bscr])
        pq.op("v", lambda v: v.tensor_tensor(
            out=qb[:, 0:ncol], in0=qb[:, 0:ncol], in1=scr[:, 0:ncol],
            op=ALU.add), reads=[bscr, bqb], writes=[bqb])

    with ExitStack() as ph:
        SBp = lambda n, sh, dt: sbuf(nc, ph, pre + n, sh, dt, sname="L" + n)
        PSp = lambda n, sh: psum(nc, ph, pre + n, sh, sname="L" + n)
        xnr = SBp("xnr", [128, KD * T], BF16)
        b_xnr = xnr[1]
        p.dma("y", xnr[0][:].rearrange("p (k t) -> p k t", t=T),
              d_xn.rearrange("(k p) t -> p k t", p=128), reads=[b_xn_h], writes=[b_xnr])
        w1s = [SBp("w1s%d" % i, [128, KD * 128], F32) for i in range(2)]
        w2s = [SBp("w2s%d" % i, [128, KD * 128], F32) for i in range(2)]
        q1d = [SBp("q1_%d" % i, [128, KD * 128], BF16) for i in range(2)]
        q2d = [SBp("q2_%d" % i, [128, KD * 128], BF16) for i in range(2)]
        qsd = [SBp("qs_%d" % i, [128, KD * 128], BF16) for i in range(2)]
        ps1 = [PSp("ps1%d" % n, [128, 512]) for n in range(4)]
        ps2 = [PSp("ps2%d" % n, [128, 512]) for n in range(4)]
        sgt = SBp("sgt", [128, T], BF16)
        hidt = SBp("hidt", [128, T], BF16)
        def loadq12(mm):
            wa, bwa = w1s[mm % 2]
            wb, bwb = w2s[mm % 2]
            p.dma("y", wa[:].rearrange("p (k c) -> p k c", k=KD),
                  d_w1T[:, mm * 128:(mm + 1) * 128].rearrange("(k p) c -> p k c", p=128),
                  reads=[b_w1T], writes=[bwa])
            p.dma("s", wb[:].rearrange("p (k c) -> p k c", k=KD),
                  d_w2T[:, mm * 128:(mm + 1) * 128].rearrange("(k p) c -> p k c", p=128),
                  reads=[b_w2T], writes=[bwb])
            q1 = q1d[mm % 2]
            q2 = q2d[mm % 2]
            qs = qsd[mm % 2]
            quantize(p, wa, bwa, q1[0], q1[1], qs[0], qs[1], KD * 128, 0)
            quantize(p, wb, bwb, q2[0], q2[1], qs[0], qs[1], KD * 128, 1)
        loadq12(0)
        for m in range(MH):
            if m + 1 < MH:
                loadq12(m + 1)
            q1 = q1d[m % 2]
            q2 = q2d[m % 2]
            sg, bsg = sgt
            hd, bhd = hidt
            for k in range(KD):
                for n in range(4):
                    pa, bpa = ps1[n]
                    p.op("t", lambda t, pa=pa, k=k, n=n, q1=q1: t.matmul(
                        pa[:, :], q1[0][:, k * 128:(k + 1) * 128],
                        xnr[0][:, k * T + n * 512:k * T + (n + 1) * 512],
                        start=(k == 0), stop=(k == KD - 1)),
                        reads=[q1[1], b_xnr], writes=[bpa])
                for n in range(4):
                    pb, bpb = ps2[n]
                    p.op("t", lambda t, pb=pb, k=k, n=n, q2=q2: t.matmul(
                        pb[:, :], q2[0][:, k * 128:(k + 1) * 128],
                        xnr[0][:, k * T + n * 512:k * T + (n + 1) * 512],
                        start=(k == 0), stop=(k == KD - 1)),
                        reads=[q2[1], b_xnr], writes=[bpb])
            for n in range(4):
                pa, bpa = ps1[n]
                pb, bpb = ps2[n]
                p.op("s", lambda s, pa=pa, n=n: s.activation(
                    out=sg[:, n * 512:(n + 1) * 512], in_=pa[:, :], func=AF.Sigmoid,
                    scale=chalf[0][:, 0:1]), reads=[bpa, chalf[1]], writes=[bsg])
                p.op("v", lambda v, pb=pb, n=n: v.tensor_tensor(
                    out=hd[:, n * 512:(n + 1) * 512], in0=sg[:, n * 512:(n + 1) * 512],
                    in1=pb[:, :], op=ALU.mult), reads=[bsg, bpb], writes=[bhd])
            p.dma("g", hid_h[m * 128:(m + 1) * 128, :], hd[:], reads=[bhd],
                  writes=[b_hid_h])
        p.barrier()

    with ExitStack() as ph:
        SBp = lambda n, sh, dt: sbuf(nc, ph, pre + n, sh, dt, sname="L" + n)
        PSp = lambda n, sh: psum(nc, ph, pre + n, sh, sname="L" + n)
        hfr = SBp("hfr", [128, MH * 1024], BF16)
        w3s = [SBp("w3s%d" % i, [128, MH * 128], F32) for i in range(2)]
        q3d = [SBp("q3_%d" % i, [128, MH * 128], BF16) for i in range(2)]
        qs3d = [SBp("qs3_%d" % i, [128, MH * 128], BF16) for i in range(2)]
        ps3 = [PSp("ps3%d" % n, [128, 512]) for n in range(4)]
        outt = SBp("outt", [128, 1024], F32)
        NM = D // 128

        def loadq3(it):
            mm = it % NM
            w3_, bw3_ = w3s[it % 2]
            p.dma("s", w3_[:].rearrange("p (k c) -> p k c", k=MH),
                  d_w3T[:, mm * 128:(mm + 1) * 128].rearrange("(k p) c -> p k c", p=128),
                  reads=[b_w3T], writes=[bw3_])
            q3 = q3d[it % 2]
            qs3 = qs3d[it % 2]
            quantize(p, w3_, bw3_, q3[0], q3[1], qs3[0], qs3[1], MH * 128, 2)
        loadq3(0)
        for it in range(2 * NM):
            th, m = it // NM, it % NM
            toff = th * 1024
            bh = hfr[1]
            if m == 0:
                p.dma("y", hfr[0][:].rearrange("p (k t) -> p k t", t=1024),
                      hid_h[:, toff:toff + 1024].rearrange("(k p) t -> p k t", p=128),
                      reads=[b_hid_h], writes=[bh])
            if it + 1 < 2 * NM:
                loadq3(it + 1)
            q3 = q3d[it % 2]
            ot, bot = outt
            for k in range(MH):
                for n in range(2):
                    ps, bps = ps3[(th * 2 + n) % 4]
                    p.op("t", lambda t, ps=ps, k=k, n=n, q3=q3: t.matmul(
                        ps[:, :], q3[0][:, k * 128:(k + 1) * 128],
                        hfr[0][:, k * 1024 + n * 512:k * 1024 + (n + 1) * 512],
                        start=(k == 0), stop=(k == MH - 1)),
                        reads=[q3[1], bh], writes=[bps])
            for n in range(2):
                ps, bps = ps3[(th * 2 + n) % 4]
                p.op("s", lambda s, ps=ps, n=n: s.activation(
                    out=ot[:, n * 512:(n + 1) * 512], in_=ps[:, :],
                    func=AF.Identity, scale=chalf[0][:, 1:2]),
                    reads=[bps, chalf[1]], writes=[bot])
            p.dma("y", o_f0[m * 128:(m + 1) * 128, toff:toff + 1024],
                  ot[:, 0:1024], reads=[bot], writes=[b_f0])
        p.barrier()
    p.barrier()




def build_ffn0():
    nc = bass.Bass("TRN2", target_bir_lowering=False)
    es = ExitStack()
    dram = lambda n, sh, dt, k="ExternalInput": (nc.dram_tensor(n, sh, dt, kind=k).ap(),
                                                 BufT(sname="S" + n, waw=False))
    d_xn, b_xn_h = dram("xn", [D, T], BF16)
    d_w1T, b_w1T = dram("w1T", [D, HF], F32)
    d_w2T, b_w2T = dram("w2T", [D, HF], F32)
    d_w3T, b_w3T = dram("w3T", [HF, D], F32)
    o_f0, b_f0 = dram("o_ffn0", [D, T], F32, "ExternalOutput")
    hid_h, b_hid_h = dram("hid_hbm", [HF, T], BF16, "Internal")
    Dio = {
        "xn": (d_xn, b_xn_h),
        "w1T": (d_w1T, b_w1T),
        "w2T": (d_w2T, b_w2T),
        "w3T": (d_w3T, b_w3T),
        "o_ffn0": (o_f0, b_f0),
        "hid_hbm": (hid_h, b_hid_h),
    }
    p = Prog(nc, es)
    emit_ffn0(nc, es, p, Dio)
    es.close()
    return nc


def emit_rg1(nc, es, p, Dio, pre="d1_"):
    d_x, b_x = Dio["x_f"]
    d_a0, b_a0 = Dio["attn0"]
    d_f0, b_f0 = Dio["ffn0"]
    d_win1, b_win1 = Dio["w_in1"]
    d_bin, b_bin = Dio["b_in"]
    d_wg, b_wg = Dio["wg"]
    d_bg, b_bg = Dio["bg"]
    d_arp, b_arp = Dio["arp"]
    d_aip, b_aip = Dio["aip"]
    d_mask, b_mask = Dio["maskrow"]
    o_x1, b_x1 = Dio["o_x1"]
    o_xn1, b_oxn1 = Dio["o_xn1"]
    o_ec, b_oec = Dio["o_ec"]
    o_es, b_oes = Dio["o_es"]
    o_ur, b_our = Dio["o_ur"]
    o_ui, b_oui = Dio["o_ui"]
    o_sum, b_osum = Dio["o_sumry"]
    SB = lambda n, sh, dt: sbuf(nc, es, pre + n, sh, dt, sname="L" + n)

    cpool, b_cp = SB("cpool", [128, 12], F32)
    p.op("v", lambda v: v.memset(cpool[:, 0:1], -1.0), writes=[b_cp])
    p.op("v", lambda v: v.memset(cpool[:, 1:2], 1.0), writes=[b_cp])
    p.op("v", lambda v: v.memset(cpool[:, 2:3], 1e-5), writes=[b_cp])
    p.op("v", lambda v: v.memset(cpool[:, 3:4], -0.5), writes=[b_cp])
    p.op("v", lambda v: v.memset(cpool[:, 4:5], 0.5), writes=[b_cp])
    p.op("v", lambda v: v.memset(cpool[:, 5:6], 2.0), writes=[b_cp])
    p.op("v", lambda v: v.memset(cpool[:, 6:7], float(np.pi)), writes=[b_cp])
    p.op("v", lambda v: v.memset(cpool[:, 7:8], float(-np.pi)), writes=[b_cp])
    p.op("v", lambda v: v.memset(cpool[:, 8:9], float(np.pi / 2)), writes=[b_cp])
    p.op("v", lambda v: v.memset(cpool[:, 9:10], float(2 * np.pi)), writes=[b_cp])
    p.op("v", lambda v: v.memset(cpool[:, 10:11], float(-2 * np.pi)), writes=[b_cp])
    c_m1 = cpool[:, 0:1]; c_p1 = cpool[:, 1:2]; c_mh = cpool[:, 3:4]
    c_half = cpool[:, 4:5]; c_two = cpool[:, 5:6]; c_pi = cpool[:, 6:7]
    c_mpi = cpool[:, 7:8]; c_pi2 = cpool[:, 8:9]; c_2pi = cpool[:, 9:10]

    spc, b_spc = SB("spc", [128, 8], F32)
    aic, b_aic = SB("aic", [128, 8], F32)
    with ExitStack() as ph:
        t1, bt1 = sbuf(nc, ph, pre + "spt1", [128, 8], F32)
        t2, bt2 = sbuf(nc, ph, pre + "spt2", [128, 8], F32)
        p.dma("y", aic[:], d_aip, reads=[b_aip], writes=[b_aic])
        p.dma("y", t1[:], d_arp, reads=[b_arp], writes=[bt1])
        p.op("s", lambda s: s.activation(out=t2[:], in_=t1[:], func=AF.Abs),
             reads=[bt1], writes=[bt2])
        p.op("s", lambda s: s.activation(out=t2[:], in_=t2[:], func=AF.Exp,
                                         scale=c_m1), reads=[bt2, b_cp], writes=[bt2])
        p.op("s", lambda s: s.activation(out=t2[:], in_=t2[:], func=AF.Ln,
                                         bias=c_p1), reads=[bt2, b_cp], writes=[bt2])
        p.op("s", lambda s: s.activation(out=t1[:], in_=t1[:], func=AF.Relu),
             reads=[bt1], writes=[bt1])
        p.op("v", lambda v: v.tensor_tensor(out=spc[:], in0=t1[:], in1=t2[:],
                                            op=ALU.add), reads=[bt1, bt2], writes=[b_spc])
        p.op("v", lambda v: v.tensor_scalar_mul(out=spc[:], in0=spc[:], scalar1=-8.0),
             reads=[b_spc], writes=[b_spc])
        p.barrier()

    xh, b_xh = SB("xh", [128, KD * T], BF16)

    ph_xn = ExitStack()
    xn1, b_xn1 = sbuf(nc, ph_xn, pre + "xn1", [128, KD * T], BF16)
    with ExitStack() as ph:
        SBp = lambda n, sh, dt: sbuf(nc, ph, pre + n, sh, dt, sname="L" + n)
        PSp = lambda n, sh: psum(nc, ph, pre + n, sh, sname="L" + n)
        e_a, b_ea = SBp("e_a", [128, 33], BF16)
        e_b, b_eb = SBp("e_b", [128, 33], BF16)
        onesb, b_on = SBp("onesb", [1, 128], BF16)
        p.op("v", lambda v: v.memset(e_a[:, :], 0.0), writes=[b_ea])
        p.op("v", lambda v: v.memset(e_a[:, 0:1], 1.0), writes=[b_ea])
        p.op("v", lambda v: v.memset(e_b[:, :], 0.0), writes=[b_eb])
        p.op("v", lambda v: v.memset(e_b[:, 32:33], 1.0), writes=[b_eb])
        p.op("v", lambda v: v.memset(onesb[:], 1.0), writes=[b_on])
        x1t = [SBp("x1t0", [128, T], F32)] * 2
        att = SBp("att", [128, T], F32)
        sqb = [SBp("sqb0", [128, T], BF16)] * 2
        pstat = [PSp("pstat%d" % n, [33, 512]) for n in range(4)]
        m_r = SBp("m_r", [1, T], F32)
        e_r = SBp("e_r", [1, T], F32)
        q_r = SBp("q_r", [1, T], F32)
        r_r = SBp("r_r", [1, T], F32)
        mrowb = SBp("mrowb", [1, T], BF16)
        rrowb = SBp("rrowb", [1, T], BF16)
        pbc = [PSp("pbc%d" % i, [128, 512]) for i in range(2)]
        mbc = SBp("mbc", [128, T], BF16)
        rbc = SBp("rbc", [128, T], BF16)
        tscr = SBp("tscr", [128, T], BF16)
        for k in range(KD):
            bb = k % 2
            xb, bxb = x1t[bb]
            at, bat = att
            sq, bsq = sqb[bb]
            p.dma("y", xb[:], d_x[k * 128:(k + 1) * 128, 0:T], reads=[b_x], writes=[bxb])
            p.dma("s", at[:], d_a0[k * 128:(k + 1) * 128, :], reads=[b_a0], writes=[bat])
            p.op("v", lambda v, xb=xb: v.tensor_tensor(out=xb[:], in0=xb[:], in1=at[:],
                                                       op=ALU.add),
                 reads=[bat, bxb], writes=[bxb])
            p.dma("s", at[:], d_f0[k * 128:(k + 1) * 128, :], reads=[b_f0], writes=[bat])
            p.op("v", lambda v, xb=xb: v.tensor_tensor(out=xb[:], in0=xb[:], in1=at[:],
                                                       op=ALU.add),
                 reads=[bat, bxb], writes=[bxb])
            p.dma("y", o_x1[k * 128:(k + 1) * 128, :], xb[:], reads=[bxb], writes=[b_x1])
            p.op("s", lambda s, k=k, xb=xb: s.copy(
                out=xn1[:, k * T:(k + 1) * T], in_=xb[:]), reads=[bxb], writes=[b_xn1])
            p.op("s", lambda s, xb=xb, sq=sq: s.activation(
                out=sq[:], in_=xb[:], func=AF.Square), reads=[bxb], writes=[bsq])
            for n in range(4):
                ps, bps = pstat[n]
                p.op("t", lambda t, k=k, ps=ps, n=n: t.matmul(
                    ps[0:33, :], e_a[:, 0:33], xn1[:, k * T + n * 512:k * T + (n + 1) * 512],
                    start=(k == 0), stop=False), reads=[b_xn1, b_ea], writes=[bps])
                p.op("t", lambda t, k=k, ps=ps, sq=sq, n=n: t.matmul(
                    ps[0:33, :], e_b[:, 0:33], sq[:, n * 512:(n + 1) * 512],
                    start=False, stop=(k == KD - 1)), reads=[bsq, b_eb], writes=[bps])
        mr, bmr = m_r
        er, ber = e_r
        qr, bqr = q_r
        rr, brr = r_r
        for n in range(4):
            ps, bps = pstat[n]
            p.op("s", lambda s, ps=ps, n=n: s.copy(
                out=mr[0:1, n * 512:(n + 1) * 512], in_=ps[0:1, :]),
                reads=[bps], writes=[bmr])
            p.op("s", lambda s, ps=ps, n=n: s.copy(
                out=er[0:1, n * 512:(n + 1) * 512], in_=ps[32:33, :]),
                reads=[bps], writes=[ber])
        p.op("v", lambda v: v.tensor_scalar_mul(out=mr[:], in0=mr[:], scalar1=1.0 / 2048.0),
             reads=[bmr], writes=[bmr])
        p.op("v", lambda v: v.tensor_scalar_mul(out=er[:], in0=er[:], scalar1=1.0 / 2048.0),
             reads=[ber], writes=[ber])
        p.op("v", lambda v: v.tensor_tensor(out=qr[:], in0=mr[:], in1=mr[:], op=ALU.mult),
             reads=[bmr], writes=[bqr])
        p.op("v", lambda v: v.tensor_tensor(out=er[:], in0=er[:], in1=qr[:],
                                            op=ALU.subtract), reads=[bqr, ber], writes=[ber])
        p.op("s", lambda s: s.activation(out=qr[:], in_=er[:], func=AF.Ln,
                                         bias=cpool[0:1, 2:3]), reads=[ber, b_cp],
             writes=[bqr])
        p.op("s", lambda s: s.activation(out=rr[:], in_=qr[:], func=AF.Exp,
                                         scale=cpool[0:1, 3:4]), reads=[bqr, b_cp],
             writes=[brr])
        mb, bmb = mrowb
        rb2, brb2 = rrowb
        p.op("s", lambda s: s.copy(out=mb[:], in_=mr[:]), reads=[bmr], writes=[bmb])
        p.op("s", lambda s: s.copy(out=rb2[:], in_=rr[:]), reads=[brr], writes=[brb2])
        mc, bmc = mbc
        rc, brc = rbc
        for n in range(4):
            ps, bps = pbc[n % 2]
            p.op("t", lambda t, ps=ps, n=n: t.matmul(
                ps[:, :], onesb[:], mb[0:1, n * 512:(n + 1) * 512], start=True, stop=True),
                reads=[b_on, bmb], writes=[bps])
            p.op("s", lambda s, ps=ps, n=n: s.copy(
                out=mc[:, n * 512:(n + 1) * 512], in_=ps[:, :]), reads=[bps], writes=[bmc])
            p.op("t", lambda t, ps=ps, n=n: t.matmul(
                ps[:, :], onesb[:], rb2[0:1, n * 512:(n + 1) * 512], start=True, stop=True),
                reads=[b_on, brb2], writes=[bps])
            p.op("s", lambda s, ps=ps, n=n: s.copy(
                out=rc[:, n * 512:(n + 1) * 512], in_=ps[:, :]), reads=[bps], writes=[brc])
        ts_, bts = tscr
        for k in range(KD):
            p.op("v", lambda v, k=k: v.tensor_tensor(
                out=ts_[:], in0=xn1[:, k * T:(k + 1) * T], in1=mc[:], op=ALU.subtract),
                reads=[b_xn1, bmc], writes=[bts])
            p.op("v", lambda v, k=k: v.tensor_tensor(
                out=xn1[:, k * T:(k + 1) * T], in0=ts_[:], in1=rc[:], op=ALU.mult),
                reads=[bts, brc], writes=[b_xn1])
            p.dma("y", o_xn1[k * 128:(k + 1) * 128, :], xn1[:, k * T:(k + 1) * T],
                  reads=[b_xn1], writes=[b_oxn1])
        p.barrier()

    with ExitStack() as ph:
        SBp = lambda n, sh, dt: sbuf(nc, ph, pre + n, sh, dt, sname="L" + n)
        PSp = lambda n, sh: psum(nc, ph, pre + n, sh, sname="L" + n)
        wv = [SBp("wv%d" % i, [128, KD * 128], BF16) for i in range(2)]
        pxz = [PSp("pxz%d" % n, [128, 512]) for n in range(4)]
        binc, b_binc = SBp("binc", [128, KD], F32)
        p.dma("y", binc[:], d_bin, reads=[b_bin], writes=[b_binc])
        for m in range(KD):
            wb, bwb = wv[m % 2]
            p.dma("g", wb[:].rearrange("p (k c) -> p k c", k=KD),
                  d_win1[:, m * 128:(m + 1) * 128].rearrange("(k p) c -> p k c", p=128),
                  reads=[b_win1], writes=[bwb])
            for k in range(KD):
                for n in range(4):
                    ps, bps = pxz[n]
                    p.op("t", lambda t, ps=ps, k=k, n=n: t.matmul(
                        ps[:, :], wb[:, k * 128:(k + 1) * 128],
                        xn1[:, k * T + n * 512:k * T + (n + 1) * 512],
                        start=(k == 0), stop=(k == KD - 1)),
                        reads=[bwb, b_xn1], writes=[bps])
            for n in range(4):
                ps, bps = pxz[n]
                p.op("s", lambda s, ps=ps, m=m, n=n: s.activation(
                    out=xh[:, m * T + n * 512:m * T + (n + 1) * 512], in_=ps[:, :],
                    func=AF.Identity, bias=binc[:, m:m + 1]),
                    reads=[bps, b_binc], writes=[b_xh])
        p.barrier()
    ph_xn.close()

    with ExitStack() as ph:
        SBp = lambda n, sh, dt: sbuf(nc, ph, pre + n, sh, dt, sname="L" + n)
        PSp = lambda n, sh: psum(nc, ph, pre + n, sh, sname="L" + n)
        wgt = SBp("wgt", [128, 4 * 128], BF16)
        pgx = [PSp("pgx%d" % n, [128, 512]) for n in range(4)]
        pga = [PSp("pga%d" % n, [128, 512]) for n in range(4)]
        bgc, b_bgc = SBp("bgc", [128, 16], F32)
        p.dma("y", bgc[:], d_bg, reads=[b_bg], writes=[b_bgc])
        maskb, b_maskb = SBp("maskb", [128, T], F32)
        p.dma("y", maskb[:], d_mask.broadcast_to([128, T]), reads=[b_mask],
              writes=[b_maskb])
        W = [SBp("w%d" % i, [128, T], F32) for i in range(10)]
        summ, b_summ = SBp("summ", [128, 32], F32)
        carr, b_carr = SBp("carr", [128, (NBLK + 1) * 8], F32)
        cari, b_cari = SBp("cari", [128, (NBLK + 1) * 8], F32)
        rrp, b_rrp = SBp("rrp", [128, (NBLK + 1) * 8], F32)
        rip, b_rip = SBp("rip", [128, (NBLK + 1) * 8], F32)
        ecE, b_ecE = SBp("ecE", [128, NBLK * 8], F32)
        esE, b_esE = SBp("esE", [128, NBLK * 8], F32)
        srE, b_srE = SBp("srE", [128, NBLK * 8], F32)
        siE, b_siE = SBp("siE", [128, NBLK * 8], F32)
        ctmp, b_ctmp = SBp("ctmp", [128, 48], F32)

        for ct in range(8):
            wg, bwg = wgt
            p.dma("g", wg[:, 0:256].rearrange("p (k c) -> p k c", k=2),
                  d_wg.rearrange("(k p) c -> p k c", p=128)[:, 2 * ct:2 * ct + 2, :],
                  reads=[b_wg], writes=[bwg])
            p.dma("g", wg[:, 256:512].rearrange("p (k c) -> p k c", k=2),
                  d_wg.rearrange("(k p) c -> p k c", p=128)[:, 16 + 2 * ct:18 + 2 * ct, :],
                  reads=[b_wg], writes=[bwg])
            gx, bgx = W[0]
            ga, bga = W[1]
            for n in range(4):
                px, bpx = pgx[n]
                pa, bpa = pga[n]
                for kk in range(2):
                    ksrc = (2 * ct + kk) * T + n * 512
                    p.op("t", lambda t, px=px, kk=kk, ksrc=ksrc: t.matmul(
                        px[:, :], wg[:, kk * 128:(kk + 1) * 128],
                        xh[:, ksrc:ksrc + 512],
                        start=(kk == 0), stop=(kk == 1)), reads=[bwg, b_xh], writes=[bpx])
                    p.op("t", lambda t, pa=pa, kk=kk, ksrc=ksrc: t.matmul(
                        pa[:, :], wg[:, (2 + kk) * 128:(3 + kk) * 128],
                        xh[:, ksrc:ksrc + 512],
                        start=(kk == 0), stop=(kk == 1)), reads=[bwg, b_xh], writes=[bpa])
                p.op("s", lambda s, px=px, n=n, ct=ct: s.activation(
                    out=gx[:, n * 512:(n + 1) * 512], in_=px[:, :], func=AF.Sigmoid,
                    bias=bgc[:, ct:ct + 1]), reads=[bpx, b_bgc], writes=[bgx])
                p.op("s", lambda s, pa=pa, n=n, ct=ct: s.activation(
                    out=ga[:, n * 512:(n + 1) * 512], in_=pa[:, :], func=AF.Sigmoid,
                    bias=bgc[:, 8 + ct:9 + ct]), reads=[bpa, b_bgc], writes=[bga])
            la, bla = W[2]
            th, bth = W[3]
            p.op("v", lambda v, ct=ct: v.tensor_scalar(
                out=la[:], in0=ga[:], scalar1=spc[:, ct:ct + 1], scalar2=None,
                op0=ALU.mult), reads=[bga, b_spc], writes=[bla])
            p.op("v", lambda v, ct=ct: v.tensor_scalar(
                out=th[:], in0=ga[:], scalar1=aic[:, ct:ct + 1], scalar2=None,
                op0=ALU.mult), reads=[bga, b_aic], writes=[bth])
            nt, bnt = W[4]
            p.op("s", lambda s: s.activation(out=nt[:], in_=la[:], func=AF.Exp,
                                             scale=c_two), reads=[bla, b_cp], writes=[bnt])
            p.op("v", lambda v: v.tensor_scalar(
                out=nt[:], in0=nt[:], scalar1=-1.0, scalar2=1.0, op0=ALU.mult,
                op1=ALU.add), reads=[bnt], writes=[bnt])
            p.op("s", lambda s: s.activation(out=nt[:], in_=nt[:], func=AF.Ln),
                 reads=[bnt], writes=[bnt])
            p.op("s", lambda s: s.activation(out=nt[:], in_=nt[:], func=AF.Exp,
                                             scale=c_half), reads=[bnt, b_cp], writes=[bnt])
            lb, blb = W[5]
            tb, btb = W[6]
            p.op("v", lambda v: v.tensor_tensor_scan(
                out=lb[:], data0=maskb[:], data1=la[:], initial=0.0,
                op0=ALU.mult, op1=ALU.add), reads=[b_maskb, bla], writes=[blb])
            p.op("v", lambda v: v.tensor_tensor_scan(
                out=tb[:], data0=maskb[:], data1=th[:], initial=0.0,
                op0=ALU.mult, op1=ALU.add), reads=[b_maskb, bth], writes=[btb])
            xr, bxr = W[2]
            xi, bxi = W[3]
            p.op("v", lambda v, ct=ct: v.tensor_tensor(
                out=xr[:], in0=gx[:], in1=xh[:, ct * T:(ct + 1) * T], op=ALU.mult),
                reads=[bgx, b_xh], writes=[bxr])
            p.op("v", lambda v: v.tensor_tensor(out=xr[:], in0=xr[:], in1=nt[:],
                                                op=ALU.mult), reads=[bnt, bxr], writes=[bxr])
            p.op("v", lambda v, ct=ct: v.tensor_tensor(
                out=xi[:], in0=gx[:], in1=xh[:, (8 + ct) * T:(9 + ct) * T], op=ALU.mult),
                reads=[bgx, b_xh], writes=[bxi])
            p.op("v", lambda v: v.tensor_tensor(out=xi[:], in0=xi[:], in1=nt[:],
                                                op=ALU.mult), reads=[bnt, bxi], writes=[bxi])
            mm_, bmm = W[4]
            sc_, bsc_ = W[7]
            p.op("v", lambda v: v.tensor_scalar(
                out=mm_[:], in0=tb[:], scalar1=float(np.pi), scalar2=None,
                op0=ALU.add), reads=[btb, b_cp], writes=[bmm])
            for _ in range(5):
                p.op("v", lambda v: v.tensor_scalar(
                    out=sc_[:], in0=mm_[:], scalar1=float(2 * np.pi), scalar2=None,
                    op0=ALU.is_ge), reads=[bmm], writes=[bsc_])
                p.op("v", lambda v: v.scalar_tensor_tensor(
                    out=mm_[:], in0=sc_[:], scalar=cpool[:, 10:11], in1=mm_[:],
                    op0=ALU.mult, op1=ALU.add), reads=[bsc_, b_cp, bmm], writes=[bmm])
            sinb, bsin = W[6]
            p.op("s", lambda s: s.activation(out=sinb[:], in_=mm_[:], func=AF.Sin,
                                             bias=c_mpi), reads=[bmm, b_cp], writes=[bsin])
            cosb, bcos = W[7]
            p.op("s", lambda s: s.activation(out=cosb[:], in_=mm_[:], func=AF.Abs,
                                             bias=c_mpi), reads=[bmm, b_cp], writes=[bcos])
            p.op("s", lambda s: s.activation(out=cosb[:], in_=cosb[:], func=AF.Sin,
                                             scale=c_m1, bias=c_pi2),
                 reads=[bcos, b_cp], writes=[bcos])
            eb, beb = W[4]
            p.op("s", lambda s: s.activation(out=eb[:], in_=lb[:], func=AF.Exp),
                 reads=[blb], writes=[beb])
            ei, bei = W[8]
            p.op("s", lambda s: s.activation(out=ei[:], in_=lb[:], func=AF.Exp,
                                             scale=c_m1), reads=[blb, b_cp], writes=[bei])
            ur, bur = W[5]
            t9, bt9 = W[9]
            p.op("v", lambda v: v.tensor_tensor(out=ur[:], in0=xr[:], in1=cosb[:],
                                                op=ALU.mult), reads=[bxr, bcos], writes=[bur])
            p.op("v", lambda v: v.tensor_tensor(out=t9[:], in0=xi[:], in1=sinb[:],
                                                op=ALU.mult), reads=[bxi, bsin], writes=[bt9])
            p.op("v", lambda v: v.tensor_tensor(out=ur[:], in0=ur[:], in1=t9[:],
                                                op=ALU.add), reads=[bt9, bur], writes=[bur])
            p.op("v", lambda v: v.tensor_tensor(out=ur[:], in0=ur[:], in1=ei[:],
                                                op=ALU.mult), reads=[bei, bur], writes=[bur])
            p.op("v", lambda v: v.tensor_tensor(out=t9[:], in0=xi[:], in1=cosb[:],
                                                op=ALU.mult), reads=[bxi, bcos], writes=[bt9])
            ui2, bui2 = W[1]
            p.op("v", lambda v: v.tensor_tensor(out=ui2[:], in0=xr[:], in1=sinb[:],
                                                op=ALU.mult), reads=[bxr, bsin], writes=[bui2])
            p.op("v", lambda v: v.tensor_tensor(out=t9[:], in0=t9[:], in1=ui2[:],
                                                op=ALU.subtract), reads=[bui2, bt9], writes=[bt9])
            p.op("v", lambda v: v.tensor_tensor(out=t9[:], in0=t9[:], in1=ei[:],
                                                op=ALU.mult), reads=[bei, bt9], writes=[bt9])
            ui, bui = t9, bt9
            ec, bec = W[2]
            p.op("v", lambda v: v.tensor_tensor(out=ec[:], in0=eb[:], in1=cosb[:],
                                                op=ALU.mult), reads=[beb, bcos], writes=[bec])
            es_, bes = W[3]
            p.op("v", lambda v: v.tensor_tensor(out=es_[:], in0=eb[:], in1=sinb[:],
                                                op=ALU.mult), reads=[beb, bsin], writes=[bes])
            p.dma("y", o_ec[ct * 128:(ct + 1) * 128, :], ec[:], reads=[bec], writes=[b_oec])
            p.dma("s", o_es[ct * 128:(ct + 1) * 128, :], es_[:], reads=[bes], writes=[b_oes])
            p.dma("y", o_ur[ct * 128:(ct + 1) * 128, :], ur[:], reads=[bur], writes=[b_our])
            p.dma("s", o_ui[ct * 128:(ct + 1) * 128, :], ui[:], reads=[bui], writes=[b_oui])
            sr0, bsr0 = W[4]
            si0, bsi0 = W[6]
            p.op("v", lambda v: v.tensor_tensor_scan(
                out=sr0[:], data0=maskb[:], data1=ur[:], initial=0.0,
                op0=ALU.mult, op1=ALU.add), reads=[b_maskb, bur], writes=[bsr0])
            p.op("v", lambda v: v.tensor_tensor_scan(
                out=si0[:], data0=maskb[:], data1=ui[:], initial=0.0,
                op0=ALU.mult, op1=ALU.add), reads=[b_maskb, bui], writes=[bsi0])
            for srcv, bsrc, dst, bdst in [
                    (ec, bec, ecE, b_ecE), (es_, bes, esE, b_esE),
                    (sr0, bsr0, srE, b_srE), (si0, bsi0, siE, b_siE)]:
                p.op("v", lambda v, srcv=srcv, dst=dst, ct=ct: v.tensor_copy(
                    out=dst[:].rearrange("p (b c) -> p b c", c=8)[:, :, ct],
                    in_=srcv[:].rearrange("p (b s) -> p b s", s=TB)[:, :, TB - 1]),
                    reads=[bsrc], writes=[bdst])
        p.op("v", lambda v: v.memset(carr[:, 0:8], 0.0), writes=[b_carr])
        p.op("v", lambda v: v.memset(cari[:, 0:8], 0.0), writes=[b_cari])
        p.op("v", lambda v: v.memset(rrp[:, 0:8], 1.0), writes=[b_rrp])
        p.op("v", lambda v: v.memset(rip[:, 0:8], 0.0), writes=[b_rip])
        for bk in range(NBLK):
            s0, s1 = slice(bk * 8, bk * 8 + 8), slice((bk + 1) * 8, (bk + 1) * 8 + 8)
            p.op("v", lambda v, s0=s0: v.tensor_tensor(
                out=ctmp[:, 0:8], in0=carr[:, s0], in1=srE[:, s0],
                op=ALU.add), reads=[b_carr, b_srE], writes=[b_ctmp])
            p.op("v", lambda v, s0=s0: v.tensor_tensor(
                out=ctmp[:, 8:16], in0=cari[:, s0], in1=siE[:, s0],
                op=ALU.add), reads=[b_cari, b_siE], writes=[b_ctmp])
            p.op("v", lambda v, s0=s0: v.tensor_tensor(
                out=ctmp[:, 16:24], in0=ecE[:, s0], in1=ctmp[:, 0:8],
                op=ALU.mult), reads=[b_ecE, b_ctmp], writes=[b_ctmp])
            p.op("v", lambda v, s0=s0: v.tensor_tensor(
                out=ctmp[:, 24:32], in0=esE[:, s0], in1=ctmp[:, 8:16],
                op=ALU.mult), reads=[b_esE, b_ctmp], writes=[b_ctmp])
            p.op("v", lambda v, s1=s1: v.tensor_tensor(
                out=carr[:, s1], in0=ctmp[:, 16:24], in1=ctmp[:, 24:32],
                op=ALU.subtract), reads=[b_ctmp], writes=[b_carr])
            p.op("v", lambda v, s0=s0: v.tensor_tensor(
                out=ctmp[:, 16:24], in0=esE[:, s0], in1=ctmp[:, 0:8],
                op=ALU.mult), reads=[b_esE, b_ctmp], writes=[b_ctmp])
            p.op("v", lambda v, s0=s0: v.tensor_tensor(
                out=ctmp[:, 24:32], in0=ecE[:, s0], in1=ctmp[:, 8:16],
                op=ALU.mult), reads=[b_ecE, b_ctmp], writes=[b_ctmp])
            p.op("v", lambda v, s1=s1: v.tensor_tensor(
                out=cari[:, s1], in0=ctmp[:, 16:24], in1=ctmp[:, 24:32],
                op=ALU.add), reads=[b_ctmp], writes=[b_cari])
            p.op("g", lambda g, s0=s0: g.tensor_tensor(
                out=ctmp[:, 32:40], in0=ecE[:, s0], in1=rrp[:, s0],
                op=ALU.mult), reads=[b_ecE, b_rrp], writes=[b_ctmp])
            p.op("g", lambda g, s0=s0: g.tensor_tensor(
                out=ctmp[:, 40:48], in0=esE[:, s0], in1=rip[:, s0],
                op=ALU.mult), reads=[b_esE, b_rip], writes=[b_ctmp])
            p.op("g", lambda g, s1=s1: g.tensor_tensor(
                out=rrp[:, s1], in0=ctmp[:, 32:40], in1=ctmp[:, 40:48],
                op=ALU.subtract), reads=[b_ctmp], writes=[b_rrp])
            p.op("g", lambda g, s0=s0: g.tensor_tensor(
                out=ctmp[:, 32:40], in0=esE[:, s0], in1=rrp[:, s0],
                op=ALU.mult), reads=[b_esE, b_rrp], writes=[b_ctmp])
            p.op("g", lambda g, s0=s0: g.tensor_tensor(
                out=ctmp[:, 40:48], in0=ecE[:, s0], in1=rip[:, s0],
                op=ALU.mult), reads=[b_ecE, b_rip], writes=[b_ctmp])
            p.op("g", lambda g, s1=s1: g.tensor_tensor(
                out=rip[:, s1], in0=ctmp[:, 32:40], in1=ctmp[:, 40:48],
                op=ALU.add), reads=[b_ctmp], writes=[b_rip])
        NB8 = NBLK * 8
        p.op("v", lambda v: v.tensor_copy(out=summ[:, 0:8], in_=rrp[:, NB8:NB8 + 8]),
             reads=[b_rrp], writes=[b_summ])
        p.op("v", lambda v: v.tensor_copy(out=summ[:, 8:16], in_=rip[:, NB8:NB8 + 8]),
             reads=[b_rip], writes=[b_summ])
        p.op("v", lambda v: v.tensor_copy(out=summ[:, 16:24], in_=carr[:, NB8:NB8 + 8]),
             reads=[b_carr], writes=[b_summ])
        p.op("v", lambda v: v.tensor_copy(out=summ[:, 24:32], in_=cari[:, NB8:NB8 + 8]),
             reads=[b_cari], writes=[b_summ])
        p.dma("y", o_sum[:, :], summ[:], reads=[b_summ], writes=[b_osum])
        p.barrier()
    p.barrier()




def build_rg1():
    nc = bass.Bass("TRN2", target_bir_lowering=False)
    es = ExitStack()
    dram = lambda n, sh, dt, k="ExternalInput": (nc.dram_tensor(n, sh, dt, kind=k).ap(),
                                                 BufT(sname="S" + n, waw=False))
    d_x, b_x = dram("x_f", [D, TH], F32)
    d_a0, b_a0 = dram("attn0", [D, T], F32)
    d_f0, b_f0 = dram("ffn0", [D, T], F32)
    d_win1, b_win1 = dram("w_in1", [D, D], F32)
    d_bin, b_bin = dram("b_in", [128, KD], F32)
    d_wg, b_wg = dram("wg", [2 * 8 * 256, 128], F32)
    d_bg, b_bg = dram("bg", [128, 16], F32)
    d_arp, b_arp = dram("arp", [128, 8], F32)
    d_aip, b_aip = dram("aip", [128, 8], F32)
    d_mask, b_mask = dram("maskrow", [1, T], F32)
    o_x1, b_x1 = dram("o_x1", [D, T], F32, "ExternalOutput")
    o_xn1, b_oxn1 = dram("o_xn1", [D, T], BF16, "ExternalOutput")
    o_ec, b_oec = dram("o_ec", [DH, T], F32, "ExternalOutput")
    o_es, b_oes = dram("o_es", [DH, T], F32, "ExternalOutput")
    o_ur, b_our = dram("o_ur", [DH, T], F32, "ExternalOutput")
    o_ui, b_oui = dram("o_ui", [DH, T], F32, "ExternalOutput")
    o_sum, b_osum = dram("o_sumry", [128, 32], F32, "ExternalOutput")
    Dio = {
        "x_f": (d_x, b_x),
        "attn0": (d_a0, b_a0),
        "ffn0": (d_f0, b_f0),
        "w_in1": (d_win1, b_win1),
        "b_in": (d_bin, b_bin),
        "wg": (d_wg, b_wg),
        "bg": (d_bg, b_bg),
        "arp": (d_arp, b_arp),
        "aip": (d_aip, b_aip),
        "maskrow": (d_mask, b_mask),
        "o_x1": (o_x1, b_x1),
        "o_xn1": (o_xn1, b_oxn1),
        "o_ec": (o_ec, b_oec),
        "o_es": (o_es, b_oes),
        "o_ur": (o_ur, b_our),
        "o_ui": (o_ui, b_oui),
        "o_sumry": (o_sum, b_osum),
    }
    p = Prog(nc, es)
    emit_rg1(nc, es, p, Dio)
    es.close()
    return nc


def emit_rg2(nc, es, p, Dio, pre="d2_"):
    d_ec, b_ec = Dio["ec"]
    d_es, b_es = Dio["es"]
    d_ur, b_ur = Dio["ur"]
    d_ui, b_ui = Dio["ui"]
    d_mask, b_mask = Dio["maskrow"]
    d_wo1, b_wo1 = Dio["w_out1"]
    d_bout, b_bout = Dio["b_out"]
    o_at1, b_at1 = Dio["o_attn1"]
    fused = "cx2g" in Dio
    if not fused:
        d_pf1r, b_pf1r = Dio["pf1r"]
        d_pf1i, b_pf1i = Dio["pf1i"]
        d_pf2r, b_pf2r = Dio["pf2r"]
        d_pf2i, b_pf2i = Dio["pf2i"]
        d_phr, b_phr = Dio["predHr"]
        d_phi, b_phi = Dio["predHi"]
    else:
        d_cx2g, b_cx2g = Dio["cx2g"]
        d_sel1, b_sel1 = Dio["sel1f"]
        d_sel2, b_sel2 = Dio["sel2f"]
        d_selh, b_selh = Dio["selhf"]
        d_idf1, b_idf1 = Dio["idf1"]
        d_idf2, b_idf2 = Dio["idf2"]
    SB = lambda n, sh, dt: sbuf(nc, es, pre + n, sh, dt, sname="L" + n)

    htot, b_htot = SB("htot", [128, KD * T], BF16)
    with ExitStack() as ph:
        SBp = lambda n, sh, dt: sbuf(nc, ph, pre + n, sh, dt, sname="L" + n)
        maskb, b_maskb = SBp("maskb", [128, T], F32)
        p.dma("y", maskb[:], d_mask.broadcast_to([128, T]), reads=[b_mask],
              writes=[b_maskb])
        pfs = {}
        if not fused:
            for nm, dd, bb in [("f1r", d_pf1r, b_pf1r), ("f1i", d_pf1i, b_pf1i),
                               ("f2r", d_pf2r, b_pf2r), ("f2i", d_pf2i, b_pf2i),
                               ("hr", d_phr, b_phr), ("hi", d_phi, b_phi)]:
                tile, bt = SBp("pf_" + nm, [128, 64], F32)
                p.dma("y", tile[:], dd, reads=[bb], writes=[bt])
                pfs[nm] = (tile, bt)
        else:
            smA, b_smA = SBp("smA", [128, 8 * 32], F32)
            p.dma("y", smA[:].rearrange("p (j c) -> p j c", c=32),
                  d_cx2g.rearrange("(j p) c -> p j c", p=128),
                  reads=[b_cx2g], writes=[b_smA])
            selbc = {}
            for nm, dd, bb in [("s1", d_sel1, b_sel1), ("s2", d_sel2, b_sel2),
                               ("sh", d_selh, b_selh), ("i1", d_idf1, b_idf1),
                               ("i2", d_idf2, b_idf2)]:
                tile, bt = SBp("selb_" + nm, [128, 64], F32)
                p.dma("y", tile[:], dd.broadcast_to([128, 64]), reads=[bb], writes=[bt])
                selbc[nm] = (tile, bt)
            scr3, b_scr3 = SBp("scr3", [128, 64], F32)
            for nm, sel, coff, fill in [("f1r", "s1", 0, "i1"), ("f1i", "s1", 8, None),
                                        ("f2r", "s2", 0, "i2"), ("f2i", "s2", 8, None),
                                        ("hr", "sh", 16, None), ("hi", "sh", 24, None)]:
                tile, bt = SBp("pf_" + nm, [128, 64], F32)
                if fill is None:
                    p.op("v", lambda v, tile=tile: v.memset(tile[:], 0.0), writes=[bt])
                else:
                    ft, fbt = selbc[fill]
                    p.op("v", lambda v, tile=tile, ft=ft: v.tensor_copy(
                        out=tile[:], in_=ft[:]), reads=[fbt], writes=[bt])
                st_, sbt = selbc[sel]
                for k in range(8):
                    src_v = smA[:, k * 32 + coff:k * 32 + coff + 8].rearrange(
                        "p (c one) -> p c one", one=1).broadcast_to([128, 8, 8])
                    sel_v = st_[:, k * 8:(k + 1) * 8].rearrange(
                        "p (one j) -> p one j", one=1).broadcast_to([128, 8, 8])
                    p.op("v", lambda v, src_v=src_v, sel_v=sel_v: v.tensor_tensor(
                        out=scr3[:].rearrange("p (c j) -> p c j", j=8),
                        in0=src_v, in1=sel_v, op=ALU.mult),
                        reads=[b_smA, sbt], writes=[b_scr3])
                    p.op("v", lambda v, tile=tile: v.tensor_tensor(
                        out=tile[:], in0=tile[:], in1=scr3[:], op=ALU.add),
                        reads=[b_scr3, bt], writes=[bt])
                pfs[nm] = (tile, bt)
        W = [SBp("wd%d" % i, [128, T], F32) for i in range(7)]
        sm = [SBp("sm%d" % i, [128, 8], F32) for i in range(4)]
        carr, b_carr = SBp("carr", [128, NBLK * 8], F32)
        cari, b_cari = SBp("cari", [128, NBLK * 8], F32)
        ecE, b_ecE = SBp("ecE", [128, NBLK * 8], F32)
        esE, b_esE = SBp("esE", [128, NBLK * 8], F32)
        srE, b_srE = SBp("srE", [128, NBLK * 8], F32)
        siE, b_siE = SBp("siE", [128, NBLK * 8], F32)
        ctmp, b_ctmp = SBp("ctmp", [128, 32], F32)
        for ct in range(8):
            ec, bec = W[0]
            es_, bes = W[1]
            ur, bur = W[2]
            ui, bui = W[3]
            p.dma("y", ec[:], d_ec[ct * 128:(ct + 1) * 128, :], reads=[b_ec], writes=[bec])
            p.dma("s", es_[:], d_es[ct * 128:(ct + 1) * 128, :], reads=[b_es], writes=[bes])
            p.dma("y", ur[:], d_ur[ct * 128:(ct + 1) * 128, :], reads=[b_ur], writes=[bur])
            p.dma("s", ui[:], d_ui[ct * 128:(ct + 1) * 128, :], reads=[b_ui], writes=[bui])
            cs = slice(ct * 8, (ct + 1) * 8)
            w0, b_w0 = sm[0]
            w1, b_w1 = sm[1]
            t0, b_t0 = sm[2]
            t1, b_t1 = sm[3]
            f1r, bf1r = pfs["f1r"]; f1i, bf1i = pfs["f1i"]
            f2r, bf2r = pfs["f2r"]; f2i, bf2i = pfs["f2i"]
            phr, bphr = pfs["hr"]; phi, bphi = pfs["hi"]
            p.op("v", lambda v, cs=cs: v.tensor_tensor(out=t0[:], in0=f1r[:, cs], in1=f2r[:, cs],
                                                op=ALU.mult), reads=[bf1r, bf2r], writes=[b_t0])
            p.op("v", lambda v, cs=cs: v.tensor_tensor(out=t1[:], in0=f1i[:, cs], in1=f2i[:, cs],
                                                op=ALU.mult), reads=[bf1i, bf2i], writes=[b_t1])
            p.op("v", lambda v: v.tensor_tensor(out=w0[:], in0=t0[:], in1=t1[:],
                                                op=ALU.subtract), reads=[b_t0, b_t1], writes=[b_w0])
            p.op("v", lambda v, cs=cs: v.tensor_tensor(out=t0[:], in0=f1r[:, cs], in1=f2i[:, cs],
                                                op=ALU.mult), reads=[bf1r, bf2i], writes=[b_t0])
            p.op("v", lambda v, cs=cs: v.tensor_tensor(out=t1[:], in0=f1i[:, cs], in1=f2r[:, cs],
                                                op=ALU.mult), reads=[bf1i, bf2r], writes=[b_t1])
            p.op("v", lambda v: v.tensor_tensor(out=w1[:], in0=t0[:], in1=t1[:],
                                                op=ALU.add), reads=[b_t0, b_t1], writes=[b_w1])
            p.op("v", lambda v, cs=cs: v.tensor_tensor(out=t0[:], in0=w0[:], in1=phr[:, cs],
                                                op=ALU.mult), reads=[b_w0, bphr], writes=[b_t0])
            p.op("v", lambda v, cs=cs: v.tensor_tensor(out=t1[:], in0=w1[:], in1=phi[:, cs],
                                                op=ALU.mult), reads=[b_w1, bphi], writes=[b_t1])
            p.op("v", lambda v: v.tensor_tensor(out=t0[:], in0=t0[:], in1=t1[:],
                                                op=ALU.subtract), reads=[b_t1, b_t0], writes=[b_t0])
            p.op("v", lambda v, ct=ct: v.tensor_reduce(
                out=carr[:, ct:ct + 1], in_=t0[:], axis=mybir.AxisListType.X, op=ALU.add),
                reads=[b_t0], writes=[b_carr])
            p.op("v", lambda v, cs=cs: v.tensor_tensor(out=t0[:], in0=w0[:], in1=phi[:, cs],
                                                op=ALU.mult), reads=[b_w0, bphi], writes=[b_t0])
            p.op("v", lambda v, cs=cs: v.tensor_tensor(out=t1[:], in0=w1[:], in1=phr[:, cs],
                                                op=ALU.mult), reads=[b_w1, bphr], writes=[b_t1])
            p.op("v", lambda v: v.tensor_tensor(out=t0[:], in0=t0[:], in1=t1[:],
                                                op=ALU.add), reads=[b_t1, b_t0], writes=[b_t0])
            p.op("v", lambda v, ct=ct: v.tensor_reduce(
                out=cari[:, ct:ct + 1], in_=t0[:], axis=mybir.AxisListType.X, op=ALU.add),
                reads=[b_t0], writes=[b_cari])
            sr0, bsr0 = W[4]
            si0, bsi0 = W[5]
            p.op("v", lambda v, ur=ur, sr0=sr0: v.tensor_tensor_scan(
                out=sr0[:], data0=maskb[:], data1=ur[:], initial=0.0,
                op0=ALU.mult, op1=ALU.add), reads=[b_maskb, bur], writes=[bsr0])
            p.op("v", lambda v, ui=ui, si0=si0: v.tensor_tensor_scan(
                out=si0[:], data0=maskb[:], data1=ui[:], initial=0.0,
                op0=ALU.mult, op1=ALU.add), reads=[b_maskb, bui], writes=[bsi0])
            for srcv, bsrc, dst, bdst in [
                    (ec, bec, ecE, b_ecE), (es_, bes, esE, b_esE),
                    (sr0, bsr0, srE, b_srE), (si0, bsi0, siE, b_siE)]:
                p.op("v", lambda v, srcv=srcv, dst=dst, ct=ct: v.tensor_copy(
                    out=dst[:].rearrange("p (b c) -> p b c", c=8)[:, :, ct],
                    in_=srcv[:].rearrange("p (b s) -> p b s", s=TB)[:, :, TB - 1]),
                    reads=[bsrc], writes=[bdst])
        for bk in range(NBLK - 1):
            s0, s1 = slice(bk * 8, bk * 8 + 8), slice((bk + 1) * 8, (bk + 1) * 8 + 8)
            p.op("v", lambda v, s0=s0: v.tensor_tensor(
                out=ctmp[:, 0:8], in0=carr[:, s0], in1=srE[:, s0],
                op=ALU.add), reads=[b_carr, b_srE], writes=[b_ctmp])
            p.op("v", lambda v, s0=s0: v.tensor_tensor(
                out=ctmp[:, 8:16], in0=cari[:, s0], in1=siE[:, s0],
                op=ALU.add), reads=[b_cari, b_siE], writes=[b_ctmp])
            p.op("v", lambda v, s0=s0: v.tensor_tensor(
                out=ctmp[:, 16:24], in0=ecE[:, s0], in1=ctmp[:, 0:8],
                op=ALU.mult), reads=[b_ecE, b_ctmp], writes=[b_ctmp])
            p.op("v", lambda v, s0=s0: v.tensor_tensor(
                out=ctmp[:, 24:32], in0=esE[:, s0], in1=ctmp[:, 8:16],
                op=ALU.mult), reads=[b_esE, b_ctmp], writes=[b_ctmp])
            p.op("v", lambda v, s1=s1: v.tensor_tensor(
                out=carr[:, s1], in0=ctmp[:, 16:24], in1=ctmp[:, 24:32],
                op=ALU.subtract), reads=[b_ctmp], writes=[b_carr])
            p.op("v", lambda v, s0=s0: v.tensor_tensor(
                out=ctmp[:, 16:24], in0=esE[:, s0], in1=ctmp[:, 0:8],
                op=ALU.mult), reads=[b_esE, b_ctmp], writes=[b_ctmp])
            p.op("v", lambda v, s0=s0: v.tensor_tensor(
                out=ctmp[:, 24:32], in0=ecE[:, s0], in1=ctmp[:, 8:16],
                op=ALU.mult), reads=[b_ecE, b_ctmp], writes=[b_ctmp])
            p.op("v", lambda v, s1=s1: v.tensor_tensor(
                out=cari[:, s1], in0=ctmp[:, 16:24], in1=ctmp[:, 24:32],
                op=ALU.add), reads=[b_ctmp], writes=[b_cari])
        for ct in range(8):
            ec, bec = W[0]
            es_, bes = W[1]
            ur, bur = W[2]
            ui, bui = W[3]
            p.dma("y", ec[:], d_ec[ct * 128:(ct + 1) * 128, :], reads=[b_ec], writes=[bec])
            p.dma("s", es_[:], d_es[ct * 128:(ct + 1) * 128, :], reads=[b_es], writes=[bes])
            p.dma("y", ur[:], d_ur[ct * 128:(ct + 1) * 128, :], reads=[b_ur], writes=[bur])
            p.dma("s", ui[:], d_ui[ct * 128:(ct + 1) * 128, :], reads=[b_ui], writes=[bui])
            urv = ur[:].rearrange("p (b s) -> p b s", s=TB)
            uiv = ui[:].rearrange("p (b s) -> p b s", s=TB)
            cview = carr[:].rearrange("p (b c) -> p b c", c=8)[:, :, ct]
            ciview = cari[:].rearrange("p (b c) -> p b c", c=8)[:, :, ct]
            p.op("v", lambda v, urv=urv, cview=cview: v.tensor_tensor(
                out=urv[:, :, 0:1].rearrange("p b s -> p (b s)"),
                in0=urv[:, :, 0:1].rearrange("p b s -> p (b s)"), in1=cview,
                op=ALU.add), reads=[b_carr, bur], writes=[bur])
            p.op("v", lambda v, uiv=uiv, ciview=ciview: v.tensor_tensor(
                out=uiv[:, :, 0:1].rearrange("p b s -> p (b s)"),
                in0=uiv[:, :, 0:1].rearrange("p b s -> p (b s)"), in1=ciview,
                op=ALU.add), reads=[b_cari, bui], writes=[bui])
            sr, bsr = W[4]
            si, bsi = W[5]
            p.op("v", lambda v, ur=ur, sr=sr: v.tensor_tensor_scan(
                out=sr[:], data0=maskb[:], data1=ur[:], initial=0.0,
                op0=ALU.mult, op1=ALU.add), reads=[b_maskb, bur], writes=[bsr])
            p.op("v", lambda v, ui=ui, si=si: v.tensor_tensor_scan(
                out=si[:], data0=maskb[:], data1=ui[:], initial=0.0,
                op0=ALU.mult, op1=ALU.add), reads=[b_maskb, bui], writes=[bsi])
            t5, bt5 = W[6]
            t9, bt9 = W[2]
            p.op("v", lambda v: v.tensor_tensor(out=t5[:], in0=ec[:], in1=sr[:],
                                                op=ALU.mult), reads=[bec, bsr], writes=[bt5])
            p.op("v", lambda v: v.tensor_tensor(out=t9[:], in0=es_[:], in1=si[:],
                                                op=ALU.mult), reads=[bes, bsi], writes=[bt9])
            p.op("v", lambda v, ct=ct: v.tensor_tensor(
                out=htot[:, ct * T:(ct + 1) * T], in0=t5[:], in1=t9[:],
                op=ALU.subtract), reads=[bt5, bt9], writes=[b_htot])
            p.op("v", lambda v: v.tensor_tensor(out=t5[:], in0=es_[:], in1=sr[:],
                                                op=ALU.mult), reads=[bes, bsr], writes=[bt5])
            p.op("v", lambda v: v.tensor_tensor(out=t9[:], in0=ec[:], in1=si[:],
                                                op=ALU.mult), reads=[bec, bsi], writes=[bt9])
            p.op("v", lambda v, ct=ct: v.tensor_tensor(
                out=htot[:, (8 + ct) * T:(9 + ct) * T], in0=t5[:], in1=t9[:],
                op=ALU.add), reads=[bt5, bt9], writes=[b_htot])
        p.barrier()

    with ExitStack() as ph:
        SBp = lambda n, sh, dt: sbuf(nc, ph, pre + n, sh, dt, sname="L" + n)
        PSp = lambda n, sh: psum(nc, ph, pre + n, sh, sname="L" + n)
        wv = [SBp("wv%d" % i, [128, KD * 128], BF16) for i in range(2)]
        pso = [PSp("pso%d" % n, [128, 512]) for n in range(4)]
        oat = SBp("oat", [128, T], F32)
        boc, b_boc = SBp("boc", [128, KD], F32)
        p.dma("y", boc[:], d_bout, reads=[b_bout], writes=[b_boc])
        for m in range(KD):
            wb, bwb = wv[m % 2]
            p.dma("g", wb[:].rearrange("p (k c) -> p k c", k=KD),
                  d_wo1[:, m * 128:(m + 1) * 128].rearrange("(k p) c -> p k c", p=128),
                  reads=[b_wo1], writes=[bwb])
            oa, boa = oat
            for k in range(KD):
                for n in range(4):
                    ps, bps = pso[n]
                    p.op("t", lambda t, ps=ps, k=k, n=n: t.matmul(
                        ps[:, :], wb[:, k * 128:(k + 1) * 128],
                        htot[:, k * T + n * 512:k * T + (n + 1) * 512],
                        start=(k == 0), stop=(k == KD - 1)),
                        reads=[bwb, b_htot], writes=[bps])
            for n in range(4):
                ps, bps = pso[n]
                p.op("s", lambda s, ps=ps, n=n, m=m: s.activation(
                    out=oa[:, n * 512:(n + 1) * 512], in_=ps[:, :], func=AF.Identity,
                    bias=boc[:, m:m + 1]), reads=[bps, b_boc], writes=[boa])
            p.dma("y", o_at1[m * 128:(m + 1) * 128, :], oa[:], reads=[boa],
                  writes=[b_at1])
        p.barrier()
    p.barrier()




def build_rg2():
    nc = bass.Bass("TRN2", target_bir_lowering=False)
    es = ExitStack()
    dram = lambda n, sh, dt, k="ExternalInput": (nc.dram_tensor(n, sh, dt, kind=k).ap(),
                                                 BufT(sname="S" + n, waw=False))
    d_ec, b_ec = dram("ec", [DH, T], F32)
    d_es, b_es = dram("es", [DH, T], F32)
    d_ur, b_ur = dram("ur", [DH, T], F32)
    d_ui, b_ui = dram("ui", [DH, T], F32)
    d_pf1r, b_pf1r = dram("pf1r", [128, 64], F32)
    d_pf1i, b_pf1i = dram("pf1i", [128, 64], F32)
    d_pf2r, b_pf2r = dram("pf2r", [128, 64], F32)
    d_pf2i, b_pf2i = dram("pf2i", [128, 64], F32)
    d_phr, b_phr = dram("predHr", [128, 64], F32)
    d_phi, b_phi = dram("predHi", [128, 64], F32)
    d_mask, b_mask = dram("maskrow", [1, T], F32)
    d_wo1, b_wo1 = dram("w_out1", [D, D], F32)
    d_bout, b_bout = dram("b_out", [128, KD], F32)
    o_at1, b_at1 = dram("o_attn1", [D, T], F32, "ExternalOutput")
    Dio = {
        "ec": (d_ec, b_ec),
        "es": (d_es, b_es),
        "ur": (d_ur, b_ur),
        "ui": (d_ui, b_ui),
        "pf1r": (d_pf1r, b_pf1r),
        "pf1i": (d_pf1i, b_pf1i),
        "pf2r": (d_pf2r, b_pf2r),
        "pf2i": (d_pf2i, b_pf2i),
        "predHr": (d_phr, b_phr),
        "predHi": (d_phi, b_phi),
        "maskrow": (d_mask, b_mask),
        "w_out1": (d_wo1, b_wo1),
        "b_out": (d_bout, b_bout),
        "o_attn1": (o_at1, b_at1),
    }
    p = Prog(nc, es)
    emit_rg2(nc, es, p, Dio)
    es.close()
    return nc


def emit_ffn1(nc, es, p, Dio, pre="e_"):
    d_xn1, b_xn1h = Dio["xn1"]
    d_gwT, b_gwT = Dio["gwT"]
    d_uwT, b_uwT = Dio["uwT"]
    d_dwT, b_dwT = Dio["dwT"]
    d_gs, b_gs = Dio["gsc"]
    d_us, b_us = Dio["usc"]
    d_ds, b_ds = Dio["dsc"]
    d_x1, b_x1 = Dio["x1"]
    d_at1, b_at1 = Dio["attn1"]
    o_out, b_out = Dio["o_out"]
    hid_h, b_hid_h = Dio["hid1_hbm"]
    SB = lambda n, sh, dt: sbuf(nc, es, pre + n, sh, dt, sname="L" + n)
    MH = HF // 128

    with ExitStack() as ph:
        SBp = lambda n, sh, dt: sbuf(nc, ph, pre + n, sh, dt, sname="L" + n)
        PSp = lambda n, sh: psum(nc, ph, pre + n, sh, sname="L" + n)
        xnr = SBp("xnr", [128, KD * T], BF16)
        b_xnr = xnr[1]
        p.dma("y", xnr[0][:].rearrange("p (k t) -> p k t", t=T),
              d_xn1.rearrange("(k p) t -> p k t", p=128), reads=[b_xn1h], writes=[b_xnr])
        gq = [SBp("gq%d" % i, [128, KD * 128], BF16) for i in range(2)]
        uq = [SBp("uq%d" % i, [128, KD * 128], BF16) for i in range(2)]
        gsc, b_gsc = SBp("gsc_t", [128, MH], F32)
        usc, b_usc = SBp("usc_t", [128, MH], F32)
        p.dma("y", gsc[:], d_gs, reads=[b_gs], writes=[b_gsc])
        p.dma("y", usc[:], d_us, reads=[b_us], writes=[b_usc])
        ps1 = [PSp("ps1%d" % n, [128, 512]) for n in range(4)]
        ps2 = [PSp("ps2%d" % n, [128, 512]) for n in range(4)]
        sgt = SBp("sgt", [128, T], BF16)
        upt = SBp("upt", [128, T], BF16)
        hidt = SBp("hidt", [128, T], BF16)
        for m in range(MH):
            ga, bga = gq[m % 2]
            ua, bua = uq[m % 2]
            p.dma("g", ga[:].rearrange("p (k c) -> p k c", k=KD),
                  d_gwT[:, m * 128:(m + 1) * 128].rearrange("(k p) c -> p k c", p=128),
                  reads=[b_gwT], writes=[bga])
            p.dma("g", ua[:].rearrange("p (k c) -> p k c", k=KD),
                  d_uwT[:, m * 128:(m + 1) * 128].rearrange("(k p) c -> p k c", p=128),
                  reads=[b_uwT], writes=[bua])
            sg, bsg = sgt
            up, bup = upt
            hd, bhd = hidt
            for k in range(KD):
                for n in range(4):
                    pa, bpa = ps1[n]
                    p.op("t", lambda t, pa=pa, ga=ga, k=k, n=n: t.matmul(
                        pa[:, :], ga[:, k * 128:(k + 1) * 128],
                        xnr[0][:, k * T + n * 512:k * T + (n + 1) * 512],
                        start=(k == 0), stop=(k == KD - 1)),
                        reads=[bga, b_xnr], writes=[bpa])
                for n in range(4):
                    pb, bpb = ps2[n]
                    p.op("t", lambda t, pb=pb, ua=ua, k=k, n=n: t.matmul(
                        pb[:, :], ua[:, k * 128:(k + 1) * 128],
                        xnr[0][:, k * T + n * 512:k * T + (n + 1) * 512],
                        start=(k == 0), stop=(k == KD - 1)),
                        reads=[bua, b_xnr], writes=[bpb])
            for n in range(4):
                pa, bpa = ps1[n]
                pb, bpb = ps2[n]
                p.op("s", lambda s, pa=pa, n=n, m=m: s.activation(
                    out=sg[:, n * 512:(n + 1) * 512], in_=pa[:, :], func=AF.Silu,
                    scale=gsc[:, m:m + 1]), reads=[bpa, b_gsc], writes=[bsg])
                p.op("s", lambda s, pb=pb, n=n, m=m: s.activation(
                    out=up[:, n * 512:(n + 1) * 512], in_=pb[:, :], func=AF.Identity,
                    scale=usc[:, m:m + 1]), reads=[bpb, b_usc], writes=[bup])
            p.op("v", lambda v: v.tensor_tensor(out=hd[:], in0=sg[:], in1=up[:],
                                                op=ALU.mult), reads=[bsg, bup],
                 writes=[bhd])
            p.dma("y", hid_h[m * 128:(m + 1) * 128, :], hd[:], reads=[bhd],
                  writes=[b_hid_h])
        p.barrier()

    with ExitStack() as ph:
        SBp = lambda n, sh, dt: sbuf(nc, ph, pre + n, sh, dt, sname="L" + n)
        PSp = lambda n, sh: psum(nc, ph, pre + n, sh, sname="L" + n)
        hfr = SBp("hfr", [128, MH * 1024], BF16)
        dq = [SBp("dq%d" % i, [128, MH * 128], BF16) for i in range(2)]
        dsc, b_dsc = SBp("dsc_t", [128, KD], F32)
        p.dma("y", dsc[:], d_ds, reads=[b_ds], writes=[b_dsc])
        ps3 = [PSp("ps3%d" % n, [128, 512]) for n in range(4)]
        outt = SBp("outt", [128, 1024], F32)
        rest = SBp("rest", [128, 1024], F32)
        for th in range(2):
            toff = th * 1024
            bh = hfr[1]
            p.dma("y", hfr[0][:].rearrange("p (k t) -> p k t", t=1024),
                  hid_h[:, toff:toff + 1024].rearrange("(k p) t -> p k t", p=128),
                  reads=[b_hid_h], writes=[bh])
            for m in range(KD):
                da, bda = dq[m % 2]
                p.dma("g", da[:].rearrange("p (k c) -> p k c", k=MH),
                      d_dwT[:, m * 128:(m + 1) * 128].rearrange("(k p) c -> p k c", p=128),
                      reads=[b_dwT], writes=[bda])
                ot, bot = outt
                rs, brs = rest
                for k in range(MH):
                    for n in range(2):
                        ps, bps = ps3[(th * 2 + n) % 4]
                        p.op("t", lambda t, ps=ps, da=da, k=k, n=n: t.matmul(
                            ps[:, :], da[:, k * 128:(k + 1) * 128],
                            hfr[0][:, k * 1024 + n * 512:k * 1024 + (n + 1) * 512],
                            start=(k == 0), stop=(k == MH - 1)),
                            reads=[bda, bh], writes=[bps])
                for n in range(2):
                    ps, bps = ps3[(th * 2 + n) % 4]
                    p.op("s", lambda s, ps=ps, n=n, m=m: s.activation(
                        out=ot[:, n * 512:(n + 1) * 512], in_=ps[:, :],
                        func=AF.Identity, scale=dsc[:, m:m + 1]),
                        reads=[bps, b_dsc], writes=[bot])
                p.dma("y", rs[:], d_x1[m * 128:(m + 1) * 128, toff:toff + 1024],
                      reads=[b_x1], writes=[brs])
                p.op("v", lambda v: v.tensor_tensor(out=ot[:, 0:1024], in0=ot[:, 0:1024],
                                                    in1=rs[:], op=ALU.add),
                     reads=[brs, bot], writes=[bot])
                p.dma("y", rs[:], d_at1[m * 128:(m + 1) * 128, toff:toff + 1024],
                      reads=[b_at1], writes=[brs])
                p.op("v", lambda v: v.tensor_tensor(out=ot[:, 0:1024], in0=ot[:, 0:1024],
                                                    in1=rs[:], op=ALU.add),
                     reads=[brs, bot], writes=[bot])
                p.dma("y", o_out[m * 128:(m + 1) * 128, toff:toff + 1024],
                      ot[:, 0:1024], reads=[bot], writes=[b_out])
        p.barrier()
    p.barrier()


def build_fused():
    nc = bass.Bass("TRN2", target_bir_lowering=False)
    es = ExitStack()
    dram = lambda n, sh, dt, k="ExternalInput": (nc.dram_tensor(n, sh, dt, kind=k).ap(),
                                                 BufT(sname="S" + n, waw=False))
    I = {}
    for n, sh, dt in [
        ("x_f", [D, TH], F32), ("hmaskc", [128, 1], F32), ("w_inT", [D, 2 * DI], F32),
        ("w_xT", [DI, 160], F32), ("w_dtT", [DTR, DI], F32), ("conv_w", [DI, DCONV], F32),
        ("conv_b", [DI, 1], F32), ("b_dt", [DI, 1], F32), ("alog", [DI, NST], F32),
        ("dskip", [128, MI], F32), ("w_outT", [DI, D], F32),
        ("w1T", [D, HF], F32), ("w2T", [D, HF], F32), ("w3T", [HF, D], F32),
        ("w_in1", [D, D], F32), ("b_in", [128, KD], F32),
        ("wg", [2 * 8 * 256, 128], F32), ("bg", [128, 16], F32),
        ("arp", [128, 8], F32), ("aip", [128, 8], F32), ("maskrow", [1, T], F32),
        ("m2T", [8, 8], F32), ("mv", [1, 8], F32),
        ("w_out1", [D, D], F32), ("b_out", [128, KD], F32),
        ("sel1f", [1, 64], F32), ("sel2f", [1, 64], F32), ("selhf", [1, 64], F32),
        ("idf1", [1, 64], F32), ("idf2", [1, 64], F32),
        ("gwT", [D, HF], F32), ("uwT", [D, HF], F32), ("dwT", [HF, D], F32),
        ("gsc", [128, HF // 128], F32), ("usc", [128, HF // 128], F32),
        ("dsc", [128, KD], F32), ("identb", [128, 128], BF16),
    ]:
        I[n] = dram(n, sh, dt)
    for n, sh, dt in [
        ("xn_h", [D, T], BF16), ("delta", [DI, T], F32), ("du", [DI, T], BF16),
        ("u", [DI, T], F32), ("sz", [DI, T], F32), ("bc", [32, T], BF16),
        ("ub_hbm", [DI, T], BF16), ("y1", [DI, T], F32),
        ("cx1_in", [128, 544], F32), ("cx1_out", [8 * 128, 544], F32),
        ("attn0", [D, T], F32), ("ffn0", [D, T], F32), ("hid_hbm", [HF, T], BF16),
        ("ygb_hbm", [DI, T], BF16), ("x1", [D, T], F32), ("xn1", [D, T], BF16),
        ("ec_h", [DH, T], F32), ("es_h", [DH, T], F32), ("ur_h", [DH, T], F32),
        ("ui_h", [DH, T], F32), ("cx2_in", [128, 32], F32),
        ("cx2_out", [8 * 128, 32], F32), ("attn1", [D, T], F32),
        ("hid1_hbm", [HF, T], BF16),
    ]:
        I[n] = dram(n, sh, dt, "Internal")
    I["o_out"] = dram("o_out", [D, T], F32, "ExternalOutput")

    p = Prog(nc, es)
    sl = lambda name, s0, s1: (I[name][0][:, s0:s1], I[name][1])

    with ExitStack() as est, nc.named_scope("s_a"):
        emit_l0pre(nc, est, p, dict(
            x_f=I["x_f"], hmaskc=I["hmaskc"], w_inT=I["w_inT"], w_xT=I["w_xT"],
            w_dtT=I["w_dtT"], conv_w=I["conv_w"], conv_b=I["conv_b"], b_dt=I["b_dt"],
            o_xn=I["xn_h"], o_delta=I["delta"], o_du=I["du"], o_u=I["u"],
            o_sz=I["sz"], o_bc=I["bc"], o_dsum=sl("cx1_in", 512, 544),
            ub_hbm=I["ub_hbm"]))
    with ExitStack() as est, nc.named_scope("s_b1"):
        emit_l0scan1(nc, est, p, dict(
            delta=I["delta"], du=I["du"], bc=I["bc"], alog=I["alog"],
            identb=I["identb"], o_y1=I["y1"], o_sc=sl("cx1_in", 0, 512)))
    p.op("g", lambda g: g.collective_compute(
        "AllGather", ALU.bypass, replica_groups=[list(range(NCORES))],
        ins=[I["cx1_in"][0].opt()], outs=[I["cx1_out"][0].opt()]),
        reads=[I["cx1_in"][1]], writes=[I["cx1_out"][1]], damt=None, sem=p.dsem("cc1x"))
    p.barrier()
    with ExitStack() as est, nc.named_scope("s_b2"):
        emit_l0scan2(nc, est, p, dict(
            delta=I["delta"], bc=I["bc"], alog=I["alog"], y1=I["y1"], u=I["u"],
            sz=I["sz"], dskip=I["dskip"], m2T=I["m2T"], w_outT=I["w_outT"],
            identb=I["identb"],
            o_attn0=I["attn0"], ygb_hbm=I["ygb_hbm"], cx1g=I["cx1_out"],
            mv=I["mv"]))
    with ExitStack() as est, nc.named_scope("s_c"):
        emit_ffn0(nc, est, p, dict(
            xn=I["xn_h"], w1T=I["w1T"], w2T=I["w2T"], w3T=I["w3T"],
            o_ffn0=I["ffn0"], hid_hbm=I["hid_hbm"]))
    with ExitStack() as est, nc.named_scope("s_d1"):
        emit_rg1(nc, est, p, dict(
            x_f=I["x_f"], attn0=I["attn0"], ffn0=I["ffn0"], w_in1=I["w_in1"],
            b_in=I["b_in"], wg=I["wg"], bg=I["bg"], arp=I["arp"], aip=I["aip"],
            maskrow=I["maskrow"], o_x1=I["x1"], o_xn1=I["xn1"], o_ec=I["ec_h"],
            o_es=I["es_h"], o_ur=I["ur_h"], o_ui=I["ui_h"], o_sumry=I["cx2_in"]))
    p.op("g", lambda g: g.collective_compute(
        "AllGather", ALU.bypass, replica_groups=[list(range(NCORES))],
        ins=[I["cx2_in"][0].opt()], outs=[I["cx2_out"][0].opt()]),
        reads=[I["cx2_in"][1]], writes=[I["cx2_out"][1]], damt=None, sem=p.dsem("cc2x"))
    p.barrier()
    with ExitStack() as est, nc.named_scope("s_d2"):
        emit_rg2(nc, est, p, dict(
            ec=I["ec_h"], es=I["es_h"], ur=I["ur_h"], ui=I["ui_h"],
            maskrow=I["maskrow"], w_out1=I["w_out1"], b_out=I["b_out"],
            o_attn1=I["attn1"], cx2g=I["cx2_out"], sel1f=I["sel1f"],
            sel2f=I["sel2f"], selhf=I["selhf"], idf1=I["idf1"], idf2=I["idf2"]))
    with ExitStack() as est, nc.named_scope("s_e"):
        emit_ffn1(nc, est, p, dict(
            xn1=I["xn1"], gwT=I["gwT"], uwT=I["uwT"], dwT=I["dwT"], gsc=I["gsc"],
            usc=I["usc"], dsc=I["dsc"], x1=I["x1"], attn1=I["attn1"],
            o_out=I["o_out"], hid1_hbm=I["hid1_hbm"]))
    p.barrier()
    es.close()
    return nc




def build_ffn1():
    nc = bass.Bass("TRN2", target_bir_lowering=False)
    es = ExitStack()
    dram = lambda n, sh, dt, k="ExternalInput": (nc.dram_tensor(n, sh, dt, kind=k).ap(),
                                                 BufT(sname="S" + n, waw=False))
    d_xn1, b_xn1h = dram("xn1", [D, T], BF16)
    d_gwT, b_gwT = dram("gwT", [D, HF], F32)
    d_uwT, b_uwT = dram("uwT", [D, HF], F32)
    d_dwT, b_dwT = dram("dwT", [HF, D], F32)
    d_gs, b_gs = dram("gsc", [128, HF // 128], F32)
    d_us, b_us = dram("usc", [128, HF // 128], F32)
    d_ds, b_ds = dram("dsc", [128, KD], F32)
    d_x1, b_x1 = dram("x1", [D, T], F32)
    d_at1, b_at1 = dram("attn1", [D, T], F32)
    o_out, b_out = dram("o_out", [D, T], F32, "ExternalOutput")
    hid_h, b_hid_h = dram("hid1_hbm", [HF, T], BF16, "Internal")
    Dio = {
        "xn1": (d_xn1, b_xn1h),
        "gwT": (d_gwT, b_gwT),
        "uwT": (d_uwT, b_uwT),
        "dwT": (d_dwT, b_dwT),
        "gsc": (d_gs, b_gs),
        "usc": (d_us, b_us),
        "dsc": (d_ds, b_ds),
        "x1": (d_x1, b_x1),
        "attn1": (d_at1, b_at1),
        "o_out": (o_out, b_out),
        "hid1_hbm": (hid_h, b_hid_h),
    }
    p = Prog(nc, es)
    emit_ffn1(nc, es, p, Dio)
    es.close()
    return nc


def _shard_tokens(arr_bt_f, core):
    b, q = core // 4, core % 4
    sl = arr_bt_f[b, q * T:(q + 1) * T, :]
    main = np.ascontiguousarray(sl.T)
    if q == 0:
        halo = np.zeros((arr_bt_f.shape[2], 4), np.float32)
    else:
        halo = np.ascontiguousarray(arr_bt_f[b, q * T - 4:q * T, :].T)
    return np.ascontiguousarray(np.concatenate([main, halo], axis=1))


def _launch(nc, in_maps):
    res = run_bass_kernel_spmd(nc, in_maps, core_ids=list(range(NCORES)))
    return res.results


def kernel(**inp):
    f32 = np.float32
    x = np.asarray(inp["L0_attn_hidden_states"], f32)
    cT = lambda a: np.ascontiguousarray(np.asarray(a, f32).T)
    cC = lambda a: np.ascontiguousarray(np.asarray(a, f32))
    pcol = lambda a, m: np.ascontiguousarray(np.asarray(a, f32).reshape(m, 128).T)

    w_inT = cT(inp["L0_attn_W_in"])
    w_xT = cT(inp["L0_attn_W_x"])
    w_dtT = cT(inp["L0_attn_W_dt"])
    conv_w = cC(inp["L0_attn_conv_w"])
    conv_b = cC(np.asarray(inp["L0_attn_conv_b"], f32).reshape(DI, 1))
    b_dt = cC(np.asarray(inp["L0_attn_b_dt"], f32).reshape(DI, 1))
    alog = cC(inp["L0_attn_A_log"])
    dskip = pcol(inp["L0_attn_D_skip"], MI)
    w_outT = cT(inp["L0_attn_W_out"])
    w1T = cT(inp["L0_ffn_w1"])
    w2T = cT(inp["L0_ffn_w2"])
    w3T = cT(inp["L0_ffn_w3"])
    w_in1 = cC(inp["L1_attn_W_in"])
    b_in = pcol(inp["L1_attn_b_in"], KD)
    wg = np.ascontiguousarray(np.concatenate(
        [np.asarray(inp["L1_attn_Wg_x"], f32).reshape(8 * 256, 128),
         np.asarray(inp["L1_attn_Wg_a"], f32).reshape(8 * 256, 128)], axis=0))
    bg = np.ascontiguousarray(np.concatenate(
        [pcol(inp["L1_attn_bg_x"], 8), pcol(inp["L1_attn_bg_a"], 8)], axis=1))
    arp = pcol(inp["L1_attn_a_real_param"], 8)
    aip = pcol(inp["L1_attn_a_imag_param"], 8)
    w_out1 = cC(inp["L1_attn_W_out"])
    b_out = pcol(inp["L1_attn_b_out"], KD)
    gwT = cT(inp["L1_ffn_gate_w"])
    uwT = cT(inp["L1_ffn_up_w"])
    dwT = cT(inp["L1_ffn_down_w"])
    gsc = pcol(inp["L1_ffn_gate_s"], HF // 128)
    usc = pcol(inp["L1_ffn_up_s"], HF // 128)
    dsc = pcol(inp["L1_ffn_down_s"], KD)
    maskrow = np.ones((1, T), f32)
    maskrow[0, ::TB] = 0.0

    if FUSED:
        nc = build_fused()
        maps = []
        for c in range(NCORES):
            b, q = c // 4, c % 4
            preds = [b * 4 + j for j in range(q)]
            m2 = np.zeros((8, 8), f32)
            for j in preds:
                for k in range(8):
                    if k // 4 == b and j % 4 < k % 4 < q:
                        m2[j, k] = 1.0
            mv = np.zeros((1, 8), f32)
            mv[0, preds] = 1.0
            sel1 = np.zeros((8, 8), f32)
            sel2 = np.zeros((8, 8), f32)
            selh = np.zeros((8, 8), f32)
            idf1 = np.ones((1, 64), f32)
            idf2 = np.ones((1, 64), f32)
            for j in preds:
                jq = j % 4
                between = [b * 4 + k for k in range(jq + 1, q)]
                selh[j, j] = 1.0
                if len(between) >= 1:
                    sel1[between[0], j] = 1.0
                    idf1[0, j::8] = 0.0
                if len(between) >= 2:
                    sel2[between[1], j] = 1.0
                    idf2[0, j::8] = 0.0
            maps.append(dict(
                x_f=_shard_tokens(x, c),
                hmaskc=np.full((128, 1), 0.0 if q == 0 else 1.0, f32),
                w_inT=w_inT, w_xT=w_xT, w_dtT=w_dtT, conv_w=conv_w, conv_b=conv_b,
                b_dt=b_dt, alog=alog, dskip=dskip, w_outT=w_outT,
                w1T=w1T, w2T=w2T, w3T=w3T, w_in1=w_in1, b_in=b_in, wg=wg, bg=bg,
                arp=arp, aip=aip, maskrow=maskrow,
                m2T=np.ascontiguousarray(m2.T), mv=mv,
                w_out1=w_out1, b_out=b_out,
                sel1f=np.ascontiguousarray(sel1.reshape(1, 64)),
                sel2f=np.ascontiguousarray(sel2.reshape(1, 64)),
                selhf=np.ascontiguousarray(selh.reshape(1, 64)),
                idf1=idf1, idf2=idf2,
                gwT=gwT, uwT=uwT, dwT=dwT, gsc=gsc, usc=usc, dsc=dsc,
                identb=np.eye(128, dtype=bf16)))
        res = _launch(nc, maps)
        out = np.empty((2, 8192, D), f32)
        for c in range(NCORES):
            b, q = c // 4, c % 4
            out[b, q * T:(q + 1) * T, :] = np.asarray(res[c]["o_out"], f32).T
        return out

    nc = build_l0pre()
    maps = []
    for c in range(NCORES):
        q = c % 4
        maps.append(dict(
            x_f=_shard_tokens(x, c),
            hmaskc=np.full((128, 1), 0.0 if q == 0 else 1.0, f32),
            w_inT=w_inT, w_xT=w_xT, w_dtT=w_dtT, conv_w=conv_w, conv_b=conv_b,
            b_dt=b_dt))
    ra = _launch(nc, maps)

    nc = build_l0scan1()
    maps = [dict(delta=ra[c]["o_delta"], du=ra[c]["o_du"], bc=ra[c]["o_bc"],
                 alog=alog, identb=np.eye(128, dtype=bf16)) for c in range(NCORES)]
    rb1 = _launch(nc, maps)

    def chain_masks(c):
        b, q = c // 4, c % 4
        m2 = np.zeros((8, 8), f32)
        preds = [b * 4 + j for j in range(q)]
        for j in preds:
            for k in range(8):
                if k // 4 == b and j % 4 < k % 4 < q:
                    m2[j, k] = 1.0
        return preds, np.ascontiguousarray(m2.T)

    nc = build_l0scan2()
    maps = []
    for c in range(NCORES):
        preds, m2T = chain_masks(c)
        predS = np.zeros((8 * 128, MI * NST), f32)
        predDT = np.zeros((8, MI * 128), f32)
        for j in range(NCORES):
            predDT[j] = np.asarray(ra[j]["o_dsum"], f32).T.reshape(-1)
            if j in preds:
                predS[j * 128:(j + 1) * 128] = np.asarray(rb1[j]["o_sc"], f32)
        maps.append(dict(delta=ra[c]["o_delta"], bc=ra[c]["o_bc"], alog=alog,
                         y1=rb1[c]["o_y1"], u=ra[c]["o_u"], sz=ra[c]["o_sz"],
                         dskip=dskip, predS=predS, predDT=predDT, m2T=m2T,
                         w_outT=w_outT, identb=np.eye(128, dtype=bf16)))
    rb2 = _launch(nc, maps)

    nc = build_ffn0()
    maps = [dict(xn=ra[c]["o_xn"], w1T=w1T, w2T=w2T, w3T=w3T) for c in range(NCORES)]
    rc = _launch(nc, maps)

    nc = build_rg1()
    maps = [dict(x_f=_shard_tokens(x, c), attn0=rb2[c]["o_attn0"],
                 ffn0=rc[c]["o_ffn0"], w_in1=w_in1, b_in=b_in, wg=wg, bg=bg,
                 arp=arp, aip=aip, maskrow=maskrow) for c in range(NCORES)]
    rd1 = _launch(nc, maps)

    nc = build_rg2()
    maps = []
    for c in range(NCORES):
        b, q = c // 4, c % 4
        preds = [b * 4 + j for j in range(q)]
        pf1r = np.ones((128, 64), f32); pf1i = np.zeros((128, 64), f32)
        pf2r = np.ones((128, 64), f32); pf2i = np.zeros((128, 64), f32)
        predHr = np.zeros((128, 64), f32)
        predHi = np.zeros((128, 64), f32)
        sms = [np.asarray(rd1[j]["o_sumry"], f32) for j in range(NCORES)]
        for j in preds:
            jq = j % 4
            between = [b * 4 + k for k in range(jq + 1, q)]
            for ct in range(8):
                predHr[:, ct * 8 + j] = sms[j][:, 16 + ct]
                predHi[:, ct * 8 + j] = sms[j][:, 24 + ct]
                if len(between) >= 1:
                    pf1r[:, ct * 8 + j] = sms[between[0]][:, ct]
                    pf1i[:, ct * 8 + j] = sms[between[0]][:, 8 + ct]
                if len(between) >= 2:
                    pf2r[:, ct * 8 + j] = sms[between[1]][:, ct]
                    pf2i[:, ct * 8 + j] = sms[between[1]][:, 8 + ct]
        maps.append(dict(ec=rd1[c]["o_ec"], es=rd1[c]["o_es"], ur=rd1[c]["o_ur"],
                         ui=rd1[c]["o_ui"], pf1r=pf1r, pf1i=pf1i, pf2r=pf2r,
                         pf2i=pf2i, predHr=predHr, predHi=predHi, maskrow=maskrow,
                         w_out1=w_out1, b_out=b_out))
    rd2 = _launch(nc, maps)

    nc = build_ffn1()
    maps = [dict(xn1=rd1[c]["o_xn1"], gwT=gwT, uwT=uwT, dwT=dwT, gsc=gsc, usc=usc,
                 dsc=dsc, x1=rd1[c]["o_x1"], attn1=rd2[c]["o_attn1"])
            for c in range(NCORES)]
    re_ = _launch(nc, maps)

    out = np.empty((2, 8192, D), f32)
    for c in range(NCORES):
        b, q = c // 4, c % 4
        out[b, q * T:(q + 1) * T, :] = np.asarray(re_[c]["o_out"], f32).T
    return out
